# version 01
# baseline (speedup 1.0000x reference)
import numpy as np
import ml_dtypes
import concourse.bacc as bacc
import concourse.mybir as mybir
import concourse.dve_ops as dve_ops
from concourse.dve_op import DveOp
from concourse.dve_spec import (
    Spec, Src0, Src1, C0, C1, C2, Zero, One, maxx, minn, lower, Bin, AluOp,
)
from concourse.alu_op_type import AluOpType
from concourse.bass_utils import run_bass_kernel_spmd

F32 = mybir.dt.float32
BF16 = mybir.dt.bfloat16
AF = mybir.ActivationFunctionType
AX = mybir.AxisListType.X
MAGIC = float(2 ** 23)

def _mk_op(name, spec, reference):
    sha = lower(spec, ver="v3", perf_en=False, name=name).sha("v3")
    return DveOp(name, spec, subdim=False, uops_sha={"v3": sha}, reference=reference)

_t = Bin(AluOp.ABSOLUTE_VALUE, Src0, Src0)
_two = One + One
_w = minn(minn(_t + _t, _t + _two), _t * C0 + C1)
FP4A_SPEC = Spec(body=_w + C2)

def _ref_fp4a(in0, in1, s0, s1, imm2):
    t = np.abs(in0)
    w = np.minimum(np.minimum(t + t, t + 2.0), t * 0.5 + 4.0)
    return (np.ceil(w - 0.5) + imm2).astype(in0.dtype)

_g = Src0 >= C0
SIGNSCALE_SPEC = Spec(body=((_g + _g) - C1) * Src1)

def _ref_signscale(in0, in1, s0, s1, imm2):
    return (np.where(in0 >= 0, 1.0, -1.0) * in1).astype(in0.dtype)

_p1 = Src0 + Src0
_p2 = _p1 + _p1
FP4B_SPEC = Spec(body=maxx(maxx(Src0 - C2, _p1 - C0), _p2 - C1) * Src1)

def _ref_fp4b(in0, in1, s0, s1, imm2):
    i = in0 - imm2
    q2 = np.maximum(np.maximum(i, i + i - 4.0), 4.0 * i - 16.0)
    return (q2 * in1).astype(in0.dtype)

FP4A = _mk_op("ANT_FP4A", FP4A_SPEC, _ref_fp4a)
SIGNSCALE = _mk_op("ANT_SIGNSCALE", SIGNSCALE_SPEC, _ref_signscale)
FP4B = _mk_op("ANT_FP4B", FP4B_SPEC, _ref_fp4b)

def _register_ops():
    names = [o.name for o in dve_ops.OPS]
    if "ANT_FP4A" in names:
        return
    i0 = names.index("DVEHASH32L_SEED")
    for j, op in enumerate([FP4A, SIGNSCALE, FP4B]):
        old = dve_ops.OPS[i0 + j]
        dve_ops.OPS[i0 + j] = op
        dve_ops._SUB_OPCODE_FOR_NAME[op.name] = dve_ops._SUB_OPCODE_FOR_NAME[old.name]
        dve_ops.CUSTOM_DVE_REFS[op.name] = op.reference
        dve_ops.CUSTOM_DVE_SPECS[op.name] = op.spec

_register_ops()

B, T, D = 2, 8192, 1024
II = 2816
H, DH = 16, 64
NCH = 4
CT = 512
NT = CT // 128
NCORE = 8
EPS = 1e-5
KD = D // 128
KI = II // 128
OC = [(0, 512), (512, 512), (1024, 512), (1536, 512), (2048, 512), (2560, 256)]

_DEBUG = False
_LIMIT = None
_FSTAGE = None
_ALIMIT = None


def build_nc():
    nc = bacc.Bacc("TRN2", target_bir_lowering=False, debug=False)

    x_d = nc.dram_tensor("x", [NCH * CT, D], F32, kind="ExternalInput").ap()
    wqkv_d = nc.dram_tensor("wqkv", [3 * D, D], F32, kind="ExternalInput").ap()
    wo_d = nc.dram_tensor("wo", [D, D], F32, kind="ExternalInput").ap()
    wg_d = nc.dram_tensor("wg", [II, D], F32, kind="ExternalInput").ap()
    wu_d = nc.dram_tensor("wu", [II, D], F32, kind="ExternalInput").ap()
    wd_d = nc.dram_tensor("wd", [D, II], F32, kind="ExternalInput").ap()
    rpos_d = nc.dram_tensor("rposd", [128, NCH * CT], F32, kind="ExternalInput").ap()
    triu_d = nc.dram_tensor("triu4", [128, 512], F32, kind="ExternalInput").ap()
    mska_d = nc.dram_tensor("maska", [128, 4], F32, kind="ExternalInput").ap()
    y_d = nc.dram_tensor("y", [NCH * CT, D], F32, kind="ExternalOutput").ap()

    wgT_d = nc.dram_tensor("wgT", [128, KD * II], BF16, kind="Internal").ap()
    wuT_d = nc.dram_tensor("wuT", [128, KD * II], BF16, kind="Internal").ap()
    wdT_d = nc.dram_tensor("wdT", [128, KI * D], BF16, kind="Internal").ap()
    wqkvT_d = nc.dram_tensor("wqkvT", [128, KD * 3 * D], BF16, kind="Internal").ap()
    woT_d = nc.dram_tensor("woT", [128, KD * D], BF16, kind="Internal").ap()
    cc_in = nc.dram_tensor("cc_in", [128, 512], F32, kind="Internal").ap()
    cc_out = nc.dram_tensor("cc_out", [4 * 128, 512], F32, kind="Internal").ap()

    dbg_aps = {}
    if _LIMIT in ("ffn1", "track", "attn", "main") or _FSTAGE or _ALIMIT:
        dbg_aps["dbg"] = nc.dram_tensor("dbg", [CT, D], F32, kind="ExternalOutput").ap()
        dbg_aps["dbg2"] = nc.dram_tensor("dbg2", [128, 4 * 512], F32, kind="ExternalOutput").ap()

    C = {}
    SEM = {}

    import contextlib
    st = contextlib.ExitStack()

    def sem(name):
        SEM[name] = st.enter_context(nc.semaphore(name))
        C[name] = 0

    for s in ["LD", "GP", "TP", "SP", "VE", "AC", "PE", "CC", "OUT"]:
        sem(s)

    def inc(instr, s, k=None):
        k = k if k is not None else (16 if s in ("LD", "GP", "TP", "SP", "OUT") else 1)
        instr.then_inc(SEM[s], k)
        C[s] += k

    def wt(eng, s, v=None):
        eng.wait_ge(SEM[s], C[s] if v is None else v)

    with st:
        xsb = st.enter_context(nc.sbuf_tensor("xsb", [128, NT * D], F32))
        xout = st.enter_context(nc.sbuf_tensor("xout", [128, D], F32))
        rpos = st.enter_context(nc.sbuf_tensor("rposc", [128, CT], F32))
        triu4 = st.enter_context(nc.sbuf_tensor("triu4s", [128, 512], F32))
        mska = st.enter_context(nc.sbuf_tensor("mskas", [128, 4], F32))
        sbase = st.enter_context(nc.sbuf_tensor("sbase", [128, 512], F32))
        sinc = st.enter_context(nc.sbuf_tensor("sinc", [128, 512], F32))
        ccs = st.enter_context(nc.sbuf_tensor("ccs", [128, 512], F32))
        stats = st.enter_context(nc.sbuf_tensor("stats", [128, 16], F32))
        scrub = st.enter_context(nc.sbuf_tensor("scrub", [128, 4], F32))
        block = st.enter_context(nc.Block())
        if True:
            def sec(engname, fn):
                getattr(block, engname)(fn)

            import contextlib as _ctx
            with _ctx.ExitStack() as es0:
                ec0 = es0.enter_context
                wf32 = ec0(nc.sbuf_tensor("wf32", [128, II], F32))
                wz = ec0(nc.sbuf_tensor("wz", [128, II], F32))
                wsbf = ec0(nc.sbuf_tensor("wsbf", [128, II], F32))
                wq = ec0(nc.sbuf_tensor("wq", [128, II], BF16))
                wtst = ec0(nc.sbuf_tensor("wtst", [128, II], BF16))
                wmb = ec0(nc.sbuf_tensor("wmb", [128, II // 16], F32))
                wrs = ec0(nc.sbuf_tensor("wrs", [128, II // 16], F32))
                wsb2 = ec0(nc.sbuf_tensor("wsb2", [128, II // 16], F32))
                def _(sy):
                    wt(sy, "LD")
                    wt(sy, "SP")
                    wt(sy, "OUT")
                    wt(sy, "GP")
                    wt(sy, "TP")
                    wt(sy, "LD")
                    inc(sy.dma_start(out=triu4[:, :], in_=triu_d), "LD")
                    wt(sy, "LD")
                    inc(sy.dma_start(out=mska[:, :], in_=mska_d), "LD")
                    wt(sy, "LD")
                    inc(sy.dma_start(
                        out=xsb[:, :].rearrange("p (t j) -> p t j", t=NT),
                        in_=x_d[0:CT, :].rearrange("(t p) j -> p t j", p=128)), "LD")
                sec("sync", _)

                def _(v):
                    v.memset(sbase[:, :], 0.0)
                    inc(v.memset(scrub[:, 1:2], EPS), "VE")
                sec("vector", _)

                def quant_weight(src_d, dst_d, nrows, K):
                    npc = nrows // 128
                    kc = K // 128
                    NB = K // 16
                    for pc in range(npc):
                        def _(sy, pc=pc):
                            wt(sy, "VE")
                            wt(sy, "LD")
                            inc(sy.dma_start(out=wf32[:, 0:K],
                                             in_=src_d[pc * 128:(pc + 1) * 128, :]), "LD")
                        sec("sync", _)

                        def _(v, pc=pc, K=K, NB=NB):
                            wt(v, "LD")
                            wt(v, "TP")
                            rsb = wrs[:, 0:NB].unsqueeze(2).broadcast_to((128, NB, 16))
                            sbb = wsb2[:, 0:NB].unsqueeze(2).broadcast_to((128, NB, 16))
                            v.tensor_reduce(wmb[:, 0:NB],
                                            wf32[:, 0:K].rearrange("p (a b) -> p a b", b=16),
                                            AX, AluOpType.max, apply_absolute_value=True)
                            v.drain()
                            v.tensor_scalar(wmb[:, 0:NB], wmb[:, 0:NB], 6e-12, None, AluOpType.max)
                            v.drain()
                            v.reciprocal(wrs[:, 0:NB], wmb[:, 0:NB])
                            v.tensor_scalar(wsb2[:, 0:NB], wmb[:, 0:NB], 1.0 / 12.0, None, AluOpType.mult)
                            v.drain()
                            v.tensor_scalar(wrs[:, 0:NB], wrs[:, 0:NB], 6.0, None, AluOpType.mult)
                            v.tensor_copy(wsbf[:, 0:K].rearrange("p (a b) -> p a b", b=16), sbb)
                            v.drain()
                            v.tensor_tensor(wz[:, 0:K].rearrange("p (a b) -> p a b", b=16),
                                            wf32[:, 0:K].rearrange("p (a b) -> p a b", b=16),
                                            rsb, AluOpType.mult)
                            v._custom_dve(SIGNSCALE, out=wsbf[:, 0:K], in0=wf32[:, 0:K],
                                          in1=wsbf[:, 0:K], s0=0.0, s1=1.0)
                            v.drain()
                            v._custom_dve(FP4A, out=wz[:, 0:K], in0=wz[:, 0:K],
                                          s0=0.5, s1=4.0, imm2=MAGIC)
                            v.drain()
                            inc(v._custom_dve(FP4B, out=wq[:, 0:K], in0=wz[:, 0:K],
                                              in1=wsbf[:, 0:K], s0=2 * MAGIC + 4.0,
                                              s1=4 * MAGIC + 16.0, imm2=MAGIC), "VE")
                        sec("vector", _)

                        def _(sy, pc=pc, K=K, kc=kc, nrows=nrows):
                            wt(sy, "VE")
                            wt(sy, "LD")
                            wt(sy, "SP")
                            wt(sy, "OUT")
                            wt(sy, "GP")
                            wt(sy, "TP")
                            inc(sy.dma_start(
                                out=wtst[:, 0:K].rearrange("p (c j) -> p c j", c=kc),
                                in_=wq[:, 0:K], transpose=True), "TP")
                            wt(sy, "TP")
                            dst = dst_d.rearrange("p (c r) -> p c r", c=kc)[:, :, pc * 128:(pc + 1) * 128]
                            wt(sy, "SP")
                            inc(sy.dma_start(out=dst,
                                             in_=wtst[:, 0:K].rearrange("p (c j) -> p c j", c=kc)), "SP")
                        sec("sync", _)

                quant_weight(wg_d, wgT_d, II, D)
                quant_weight(wu_d, wuT_d, II, D)
                quant_weight(wd_d, wdT_d, D, II)

                def cast_weight(src_d, dst_d, nrows):
                    npc = nrows // 128
                    for pc in range(npc):
                        def _(g, pc=pc):
                            wt(g, "TP")
                            wt(g, "GP")
                            inc(g.dma_start(out=wq[:, 0:D],
                                            in_=src_d[pc * 128:(pc + 1) * 128, :]), "GP")
                        sec("gpsimd", _)

                        def _(sy, pc=pc, nrows=nrows):
                            wt(sy, "GP")
                            wt(sy, "LD")
                            wt(sy, "SP")
                            wt(sy, "OUT")
                            wt(sy, "TP")
                            inc(sy.dma_start(
                                out=wtst[:, 0:D].rearrange("p (c j) -> p c j", c=KD),
                                in_=wq[:, 0:D], transpose=True), "TP")
                            wt(sy, "TP")
                            dst = dst_d.rearrange("p (c r) -> p c r", c=KD)[:, :, pc * 128:(pc + 1) * 128]
                            wt(sy, "SP")
                            inc(sy.dma_start(out=dst,
                                             in_=wtst[:, 0:D].rearrange("p (c j) -> p c j", c=KD)), "SP")
                        sec("sync", _)

                cast_weight(wqkv_d, wqkvT_d, 3 * D)
                cast_weight(wo_d, woT_d, D)

            nch = 0 if _LIMIT == "setup" else (1 if _LIMIT in ("ffn1", "track", "attn", "chunk1", "main") else NCH)
            for ch in range(nch):
                chunk(nc, block, sec, C, SEM, inc, wt, ch,
                      x_d, y_d, wgT_d, wuT_d, wdT_d, wqkvT_d, woT_d,
                      cc_in, cc_out, rpos_d,
                      xsb, xout, rpos, triu4, mska, sbase, sinc, ccs,
                      stats, scrub, dbg_aps)

            def _(sy):
                wt(sy, "OUT")
            sec("sync", _)

    nc.finalize()
    return nc


def chunk(nc, block, sec, C, SEM, inc, wt, ch,
          x_d, y_d, wgT_d, wuT_d, wdT_d, wqkvT_d, woT_d, cc_in, cc_out, rpos_d,
          xsb, xout, rpos, triu4, mska, sbase, sinc, ccs, stats, scrub,
          dbg_aps):
    F32 = mybir.dt.float32
    BF16 = mybir.dt.bfloat16

    def ffn_phase(tag, resid_scale_out):
        import contextlib
        with contextlib.ExitStack() as es:
            ec = es.enter_context
            WgT = ec(nc.sbuf_tensor(f"WgT{tag}", [128, KD * II], BF16))
            WuT = ec(nc.sbuf_tensor(f"WuT{tag}", [128, KD * II], BF16))
            WdT = ec(nc.sbuf_tensor(f"WdT{tag}", [128, KI * D], BF16))
            cbuf = ec(nc.sbuf_tensor(f"cbuf{tag}", [128, D], F32))
            zbuf = ec(nc.sbuf_tensor(f"zbuf{tag}", [128, D], F32))
            sbf = ec(nc.sbuf_tensor(f"sbf{tag}", [128, D], F32))
            xq = ec(nc.sbuf_tensor(f"xq{tag}", [128, D], BF16))
            xqT = ec(nc.sbuf_tensor(f"xqT{tag}", [128, KD * 128], BF16))
            sgb = ec(nc.sbuf_tensor(f"sg{tag}", [128, 512], F32))
            hb = ec(nc.sbuf_tensor(f"hb{tag}", [128, 512], F32))
            zh = ec(nc.sbuf_tensor(f"zh{tag}", [128, 512], F32))
            hq = ec(nc.sbuf_tensor(f"hq{tag}", [128, II], BF16))
            hqT = ec(nc.sbuf_tensor(f"hqT{tag}", [128, KI * 128], BF16))
            mb = ec(nc.sbuf_tensor(f"mb{tag}", [128, 64], F32))
            rs = ec(nc.sbuf_tensor(f"rs{tag}", [128, 64], F32))
            sb2 = ec(nc.sbuf_tensor(f"sb2{tag}", [128, 64], F32))
            psg0 = ec(nc.psum_tensor(f"psg{tag}", [128, 512], F32))
            psg1 = ec(nc.psum_tensor(f"psg2{tag}", [128, 512], F32))
            psu0 = ec(nc.psum_tensor(f"psu{tag}", [128, 512], F32))
            psu1 = ec(nc.psum_tensor(f"psu2{tag}", [128, 512], F32))
            psd = ec(nc.psum_tensor(f"psd{tag}", [128, 1024], F32))
            psg = [psg0, psg1]
            psu = [psu0, psu1]

            def _(sy):
                wt(sy, "VE")
                wt(sy, "AC")
                wt(sy, "PE")
                wt(sy, "SP")
                wt(sy, "LD")
                inc(sy.dma_start(out=WgT[:, :], in_=wgT_d), "LD")
                wt(sy, "LD")
                inc(sy.dma_start(out=WuT[:, :], in_=wuT_d), "LD")
                wt(sy, "LD")
                inc(sy.dma_start(out=WdT[:, :], in_=wdT_d), "LD")
            sec("sync", _)
            w_ld = C["LD"]

            fstage = _FSTAGE if tag.startswith("f1c0") else None
            ntt = 1 if fstage else NT
            for t in range(ntt):
                xt = xsb[:, t * D:(t + 1) * D]

                def _(a, t=t):
                    wt(a, "VE")
                    if t == 0:
                        wt(a, "LD")
                    a.activation(zbuf[:, :], xt, AF.Square, accum_out=stats[:, 1:2])
                    a.activation(cbuf[:, :], xt, AF.Identity, accum_out=stats[:, 0:1])
                    a.drain()
                    a.activation(stats[:, 2:3], stats[:, 0:1], AF.Identity, scale=1.0 / D)
                    a.activation(stats[:, 3:4], stats[:, 1:2], AF.Identity,
                                 scale=1.0 / D, bias=scrub[:, 1:2])
                    a.drain()
                    a.activation(stats[:, 8:9], stats[:, 2:3], AF.Square)
                    a.drain()
                    a.activation(stats[:, 4:5], stats[:, 8:9], AF.Identity,
                                 scale=-1.0, bias=stats[:, 3:4])
                    a.drain()
                    a.activation(stats[:, 5:6], stats[:, 4:5], AF.Ln)
                    a.drain()
                    inc(a.activation(stats[:, 6:7], stats[:, 5:6], AF.Exp, scale=-0.5), "AC")
                sec("scalar", _)
                if fstage == "ln":
                    def _(sy):
                        wt(sy, "AC")
                        wt(sy, "OUT")
                        inc(sy.dma_start(out=dbg_aps["dbg2"][:, 0:16], in_=stats[:, :]), "OUT")
                    sec("sync", _)
                    return

                def _(v, t=t):
                    wt(v, "AC")
                    rsb = rs[:, :].unsqueeze(2).broadcast_to((128, 64, 16))
                    sbb = sb2[:, :].unsqueeze(2).broadcast_to((128, 64, 16))
                    v.tensor_scalar(cbuf[:, :], xt, stats[:, 2:3], None, AluOpType.subtract)
                    v.drain()
                    v.tensor_reduce(mb[:, :], cbuf[:, :].rearrange("p (a b) -> p a b", b=16),
                                    AX, AluOpType.max, apply_absolute_value=True)
                    v.drain()
                    v.tensor_scalar(mb[:, :], mb[:, :], 6e-12, None, AluOpType.max)
                    v.drain()
                    v.reciprocal(rs[:, :], mb[:, :])
                    v.tensor_scalar(sb2[:, :], mb[:, :], stats[:, 6:7], 1.0 / 12.0,
                                    AluOpType.mult, AluOpType.mult)
                    v.drain()
                    v.tensor_scalar(rs[:, :], rs[:, :], 6.0, None, AluOpType.mult)
                    v.tensor_copy(sbf[:, :].rearrange("p (a b) -> p a b", b=16), sbb)
                    v.drain()
                    v.tensor_tensor(zbuf[:, :].rearrange("p (a b) -> p a b", b=16),
                                    cbuf[:, :].rearrange("p (a b) -> p a b", b=16),
                                    rsb, AluOpType.mult)
                    v._custom_dve(SIGNSCALE, out=cbuf[:, :], in0=cbuf[:, :], in1=sbf[:, :],
                                  s0=0.0, s1=1.0)
                    v.drain()
                    v._custom_dve(FP4A, out=zbuf[:, :], in0=zbuf[:, :], s0=0.5, s1=4.0, imm2=MAGIC)
                    v.drain()
                    inc(v._custom_dve(FP4B, out=xq[:, :], in0=zbuf[:, :], in1=cbuf[:, :],
                                      s0=2 * MAGIC + 4.0, s1=4 * MAGIC + 16.0, imm2=MAGIC), "VE")
                sec("vector", _)

                def _(sy, t=t):
                    wt(sy, "VE")
                    wt(sy, "LD")
                    wt(sy, "SP")
                    wt(sy, "OUT")
                    wt(sy, "GP")
                    wt(sy, "TP")
                    inc(sy.dma_start(out=xqT[:, :].rearrange("p (c j) -> p c j", c=KD),
                                     in_=xq[:, :], transpose=True), "TP")
                    wt(sy, "TP")
                sec("sync", _)
                tp_xq = C["TP"]
                if fstage == "quant":
                    def _(v):
                        wt(v, "TP")
                        inc(v.tensor_copy(sbf[:, :], xq[:, :]), "VE")
                    sec("vector", _)
                    def _(sy):
                        wt(sy, "VE")
                        wt(sy, "OUT")
                        inc(sy.dma_start(out=dbg_aps["dbg"][0:128, :], in_=sbf[:, :]), "OUT")
                        wt(sy, "OUT")
                        inc(sy.dma_start(out=dbg_aps["dbg2"][:, 0:64], in_=mb[:, :]), "OUT")
                    sec("sync", _)
                    return

                if fstage == "wload":
                    def _(a, t=t):
                        wt(a, "LD", w_ld)
                        wt(a, "TP", tp_xq)
                        inc(a.activation(sgb[:, :], WgT[:, 0:512], AF.Identity), "AC")
                    sec("scalar", _)
                    def _(v, t=t):
                        wt(v, "TP", tp_xq)
                        inc(v.tensor_copy(hb[:, 0:128], xqT[:, 0:128]), "VE")
                    sec("vector", _)
                    def _(sy):
                        wt(sy, "AC")
                        wt(sy, "VE")
                        wt(sy, "OUT")
                        inc(sy.dma_start(out=dbg_aps["dbg"][0:128, 0:512], in_=sgb[:, :]), "OUT")
                        wt(sy, "OUT")
                        inc(sy.dma_start(out=dbg_aps["dbg"][128:256, 0:128], in_=hb[:, 0:128]), "OUT")
                    sec("sync", _)
                    return
                if fstage in ("mm", "mm1", "mmc"):
                    if fstage == "mmc":
                        def _(v, t=t):
                            wt(v, "LD", w_ld)
                            inc(v.tensor_copy(hq[:, 0:512], WgT[:, 0:512]), "VE")
                        sec("vector", _)
                    def _(te, t=t):
                        wt(te, "TP", tp_xq)
                        wt(te, "LD", w_ld)
                        wt(te, "VE")
                        nk = 1 if fstage == "mm1" else KD
                        for k in range(nk):
                            rhs = hq[:, 0:512] if fstage == "mmc" else WgT[:, k * II: k * II + 512]
                            ins_ = te.matmul(psg[0][:, :], xqT[:, k * 128:(k + 1) * 128], rhs,
                                             start=(k == 0), stop=(k == nk - 1))
                        inc(ins_, "PE")
                    sec("tensor", _)
                    def _(a, t=t):
                        wt(a, "PE")
                        inc(a.activation(sgb[:, :], psg[0][:, :], AF.Identity), "AC")
                    sec("scalar", _)
                    def _(sy):
                        wt(sy, "AC")
                        wt(sy, "OUT")
                        inc(sy.dma_start(out=dbg_aps["dbg"][0:128, 0:512], in_=sgb[:, :]), "OUT")
                    sec("sync", _)
                    return

                for oi, (o0, ow) in enumerate(OC):
                    bk = oi % 2
                    nb = ow // 16
                    def _(te, t=t, oi=oi, o0=o0, ow=ow, bk=bk):
                        wt(te, "TP", tp_xq)
                        wt(te, "LD", w_ld)
                        wt(te, "VE")
                        wt(te, "AC")
                        for k in range(KD):
                            ins_ = te.matmul(psg[bk][:, 0:ow], xqT[:, k * 128:(k + 1) * 128],
                                             WgT[:, k * II + o0:k * II + o0 + ow],
                                             start=(k == 0), stop=(k == KD - 1))
                        inc(ins_, "PE")
                        for k in range(KD):
                            ins_ = te.matmul(psu[bk][:, 0:ow], xqT[:, k * 128:(k + 1) * 128],
                                             WuT[:, k * II + o0:k * II + o0 + ow],
                                             start=(k == 0), stop=(k == KD - 1))
                        inc(ins_, "PE")
                    sec("tensor", _)

                    def _(a, t=t, oi=oi, o0=o0, ow=ow, bk=bk):
                        wt(a, "PE")
                        inc(a.activation(sgb[:, 0:ow], psg[bk][:, 0:ow], AF.Silu), "AC")
                    sec("scalar", _)

                    def _(v, t=t, oi=oi, o0=o0, ow=ow, bk=bk, nb=nb):
                        wt(v, "AC")
                        rsb = rs[:, 0:nb].unsqueeze(2).broadcast_to((128, nb, 16))
                        sbb = sb2[:, 0:nb].unsqueeze(2).broadcast_to((128, nb, 16))
                        v.tensor_tensor(hb[:, 0:ow], psu[bk][:, 0:ow], sgb[:, 0:ow], AluOpType.mult)
                        v.drain()
                        if fstage == "gu0" and oi == 0:
                            inc(v.tensor_copy(zh[:, 0:ow], hb[:, 0:ow]), "VE")
                            return
                        v.tensor_reduce(mb[:, 0:nb], hb[:, 0:ow].rearrange("p (a b) -> p a b", b=16),
                                        AX, AluOpType.max, apply_absolute_value=True)
                        v.drain()
                        v.tensor_scalar(mb[:, 0:64], mb[:, 0:64], 6e-12, None, AluOpType.max)
                        v.drain()
                        v.reciprocal(rs[:, 0:64], mb[:, 0:64])
                        v.tensor_scalar(sb2[:, 0:64], mb[:, 0:64], 1.0 / 12.0, None, AluOpType.mult)
                        v.drain()
                        v.tensor_scalar(rs[:, 0:64], rs[:, 0:64], 6.0, None, AluOpType.mult)
                        v.tensor_copy(sbf[:, 0:ow].rearrange("p (a b) -> p a b", b=16), sbb)
                        v.drain()
                        v.tensor_tensor(zh[:, 0:ow].rearrange("p (a b) -> p a b", b=16),
                                        hb[:, 0:ow].rearrange("p (a b) -> p a b", b=16),
                                        rsb, AluOpType.mult)
                        v._custom_dve(SIGNSCALE, out=hb[:, 0:ow], in0=hb[:, 0:ow],
                                      in1=sbf[:, 0:ow], s0=0.0, s1=1.0)
                        v.drain()
                        v._custom_dve(FP4A, out=zh[:, 0:ow], in0=zh[:, 0:ow],
                                      s0=0.5, s1=4.0, imm2=MAGIC)
                        v.drain()
                        inc(v._custom_dve(FP4B, out=hq[:, o0:o0 + ow], in0=zh[:, 0:ow],
                                          in1=hb[:, 0:ow], s0=2 * MAGIC + 4.0,
                                          s1=4 * MAGIC + 16.0, imm2=MAGIC), "VE")
                    sec("vector", _)
                    if fstage == "gu0" and oi == 0:
                        def _(sy):
                            wt(sy, "VE")
                            wt(sy, "OUT")
                            inc(sy.dma_start(out=dbg_aps["dbg"][0:128, 0:512], in_=zh[:, 0:512]), "OUT")
                            wt(sy, "OUT")
                            inc(sy.dma_start(out=dbg_aps["dbg"][128:256, 0:512], in_=sgb[:, 0:512]), "OUT")
                        sec("sync", _)
                        return
                    if fstage == "gu" and oi == 0:
                        def _(v):
                            inc(v.tensor_copy(zh[:, :], hq[:, 0:512]), "VE")
                        sec("vector", _)
                        def _(sy):
                            wt(sy, "VE")
                            wt(sy, "OUT")
                            inc(sy.dma_start(out=dbg_aps["dbg"][0:128, 0:512], in_=zh[:, :]), "OUT")
                            wt(sy, "OUT")
                            inc(sy.dma_start(out=dbg_aps["dbg"][128:256, 0:512], in_=sgb[:, :]), "OUT")
                        sec("sync", _)
                        return

                def _(sy, t=t):
                    wt(sy, "VE")
                    wt(sy, "LD")
                    wt(sy, "SP")
                    wt(sy, "OUT")
                    wt(sy, "GP")
                    wt(sy, "TP")
                    inc(sy.dma_start(out=hqT[:, :].rearrange("p (c j) -> p c j", c=KI),
                                     in_=hq[:, :], transpose=True), "TP")
                    wt(sy, "TP")
                sec("sync", _)
                tp_hq = C["TP"]

                def _(te, t=t):
                    wt(te, "TP", tp_hq)
                    wt(te, "VE")
                    for half in range(2):
                        for k in range(KI):
                            ins_ = te.matmul(psd[:, half * 512:(half + 1) * 512],
                                             hqT[:, k * 128:(k + 1) * 128],
                                             WdT[:, k * D + half * 512:k * D + half * 512 + 512],
                                             start=(k == 0), stop=(k == KI - 1))
                    inc(ins_, "PE")
                sec("tensor", _)

                if resid_scale_out is None:
                    def _(v, t=t):
                        wt(v, "PE")
                        inc(v.scalar_tensor_tensor(xt, psd[:, 0:D], 0.5, xt,
                                                   AluOpType.mult, AluOpType.add), "VE")
                    sec("vector", _)
                    if fstage == "down":
                        def _(sy):
                            wt(sy, "VE")
                            wt(sy, "OUT")
                            inc(sy.dma_start(out=dbg_aps["dbg"][0:128, :], in_=xt), "OUT")
                        sec("sync", _)
                        return
                else:
                    def _(v, t=t):
                        wt(v, "PE")
                        wt(v, "OUT")
                        inc(v.scalar_tensor_tensor(xout[:, :], psd[:, 0:D], 0.5, xt,
                                                   AluOpType.mult, AluOpType.add), "VE")
                    sec("vector", _)

                    def _(sy, t=t):
                        wt(sy, "VE")
                        rows = y_d[ch * CT + t * 128: ch * CT + (t + 1) * 128, :]
                        wt(sy, "LD")
                        wt(sy, "SP")
                        wt(sy, "OUT")
                        wt(sy, "GP")
                        wt(sy, "TP")
                        wt(sy, "OUT")
                        inc(sy.dma_start(out=rows, in_=xout[:, :]), "OUT")
                    sec("sync", _)

    ffn_phase(f"f1c{ch}", None)
    if _FSTAGE:
        return
    if _LIMIT == "ffn1":
        def _(sy):
            wt(sy, "VE")
            wt(sy, "OUT")
            inc(sy.dma_start(out=dbg_aps["dbg"].rearrange("(t p) j -> p t j", p=128),
                             in_=xsb[:, :].rearrange("p (t j) -> p t j", t=NT)), "OUT")
        sec("sync", _)
        return

    import contextlib
    with contextlib.ExitStack() as es:
        ec = es.enter_context
        gath = ec(nc.sbuf_tensor(f"gath{ch}", [128, 4 * 512], F32))
        u2T = ec(nc.sbuf_tensor(f"u2T{ch}", [128, KD * CT], BF16))
        u2b = ec(nc.sbuf_tensor(f"u2b{ch}", [128, D], BF16))
        qdm = ec(nc.sbuf_tensor(f"qdm{ch}", [128, 8 * CT], BF16))
        kdm = ec(nc.sbuf_tensor(f"kdm{ch}", [128, 8 * CT], BF16))
        ktok = ec(nc.sbuf_tensor(f"ktok{ch}", [128, NT * D], BF16))
        vtok = ec(nc.sbuf_tensor(f"vtok{ch}", [128, NT * D], BF16))
        wfc = ec(nc.sbuf_tensor(f"wfc{ch}", [128, 2 * KD * 128], BF16))
        wv = ec(nc.sbuf_tensor(f"wv{ch}", [128, KD * D], BF16))
        wow = ec(nc.sbuf_tensor(f"wow{ch}", [128, KD * D], BF16))
        scb = ec(nc.sbuf_tensor(f"scb{ch}", [128, 512], BF16))
        stot = ec(nc.sbuf_tensor(f"stot{ch}", [128, NT * 512], BF16))
        sprt = ec(nc.sbuf_tensor(f"sprt{ch}", [128, NT * 512], F32))
        attb = ec(nc.sbuf_tensor(f"attb{ch}", [128, NT * KD * 128], BF16))
        cbufa = ec(nc.sbuf_tensor(f"cbufa{ch}", [128, D], F32))
        def _(sy):
            wt(sy, "VE")
            wt(sy, "AC")
            wt(sy, "PE")
            wt(sy, "LD")
            inc(sy.dma_start(out=rpos[:, :], in_=rpos_d[:, ch * CT:(ch + 1) * CT]), "LD")
        sec("sync", _)

        for t in range(NT):
            xt = xsb[:, t * D:(t + 1) * D]

            def _(a, t=t):
                wt(a, "VE")
                wt(a, "TP")
                a.activation(cbufa[:, :], xt, AF.Square, accum_out=stats[:, 1:2])
                a.activation(u2b[:, :], xt, AF.Identity, accum_out=stats[:, 0:1])
                a.drain()
                a.activation(stats[:, 2:3], stats[:, 0:1], AF.Identity, scale=1.0 / D)
                a.activation(stats[:, 9:10], stats[:, 0:1], AF.Identity, scale=-1.0 / D)
                a.activation(stats[:, 3:4], stats[:, 1:2], AF.Identity,
                             scale=1.0 / D, bias=scrub[:, 1:2])
                a.drain()
                a.activation(stats[:, 8:9], stats[:, 2:3], AF.Square)
                a.drain()
                a.activation(stats[:, 4:5], stats[:, 8:9], AF.Identity,
                             scale=-1.0, bias=stats[:, 3:4])
                a.drain()
                a.activation(stats[:, 5:6], stats[:, 4:5], AF.Ln)
                a.drain()
                a.activation(stats[:, 6:7], stats[:, 5:6], AF.Exp, scale=-0.5)
                a.drain()
                a.activation(stats[:, 7:8], stats[:, 6:7], AF.Identity,
                             scale=stats[:, 9:10])
                a.drain()
                inc(a.activation(u2b[:, :], xt, AF.Identity,
                                 bias=stats[:, 7:8], scale=stats[:, 6:7]), "AC")
            sec("scalar", _)

            def _(sy, t=t):
                wt(sy, "AC")
                wt(sy, "LD")
                wt(sy, "SP")
                wt(sy, "OUT")
                wt(sy, "GP")
                dst = u2T[:, :].rearrange("p (c j) -> p c j", c=KD)[:, :, t * 128:(t + 1) * 128]
                wt(sy, "TP")
                inc(sy.dma_start(out=dst, in_=u2b[:, :], transpose=True), "TP")
                wt(sy, "TP")
            sec("sync", _)

        def _(sy):
            wt(sy, "AC")
            wt(sy, "LD")
            wt(sy, "SP")
            wt(sy, "OUT")
            wt(sy, "GP")
            wt(sy, "TP")
            wt(sy, "LD")
            inc(sy.dma_start(out=wv[:, :].rearrange("p (c j) -> p c j", c=KD),
                             in_=wqkvT_d.rearrange("p (c j) -> p c j", c=KD)[:, :, 2 * D:3 * D]), "LD")
            wt(sy, "LD")
            inc(sy.dma_start(out=wow[:, :], in_=woT_d), "LD")
        sec("sync", _)
        vw_ld = C["LD"]
        tp_u2 = C["TP"]

        with contextlib.ExitStack() as es2:
            ec2 = es2.enter_context
            psqk0 = ec2(nc.psum_tensor(f"psqk{ch}", [128, 512], F32))
            psqk1 = ec2(nc.psum_tensor(f"psqk2{ch}", [128, 512], F32))
            psv = ec2(nc.psum_tensor(f"psv{ch}", [128, 1024], F32))
            psst = ec2(nc.psum_tensor(f"psst{ch}", [128, 512], F32))
            psqk = [psqk0, psqk1]
            for fc in range(16):
                bk = fc % 2
                def _(sy, fc=fc, bk=bk):
                    if fc >= 2:
                        wt(sy, "PE")
                    src = wqkvT_d.rearrange("p (c j) -> p c j", c=KD)[:, :, fc * 128:(fc + 1) * 128]
                    wt(sy, "LD")
                    inc(sy.dma_start(out=wfc[:, bk * KD * 128:(bk + 1) * KD * 128]
                                     .rearrange("p (c j) -> p c j", c=KD), in_=src), "LD")
                sec("sync", _)

                def _(te, fc=fc, bk=bk):
                    wt(te, "LD")
                    wt(te, "TP", tp_u2)
                    wt(te, "AC")
                    for k in range(KD):
                        ins_ = te.matmul(psqk[bk][:, :],
                                         wfc[:, bk * KD * 128 + k * 128: bk * KD * 128 + (k + 1) * 128],
                                         u2T[:, k * CT:(k + 1) * CT],
                                         start=(k == 0), stop=(k == KD - 1))
                    inc(ins_, "PE")
                sec("tensor", _)

                def _(a, fc=fc, bk=bk):
                    wt(a, "PE")
                    if fc < 8:
                        inc(a.activation(qdm[:, fc * CT:(fc + 1) * CT], psqk[bk][:, :],
                                         AF.Identity, scale=0.125), "AC")
                    else:
                        dst = kdm[:, :].rearrange("p (t j) -> p t j", t=NT)[
                            :, :, (fc - 8) * 128:(fc - 7) * 128]
                        inc(a.activation(dst, psqk[bk][:, :], AF.Sign), "AC")
                sec("scalar", _)
            qk_ac = C["AC"]

            for t in range(NT):
                def _(te, t=t):
                    wt(te, "LD", vw_ld)
                    wt(te, "TP", tp_u2)
                    if t > 0:
                        wt(te, "AC")
                    for half in range(2):
                        for k in range(KD):
                            ins_ = te.matmul(psv[:, half * 512:(half + 1) * 512],
                                             u2T[:, k * CT + t * 128:k * CT + (t + 1) * 128],
                                             wv[:, k * D + half * 512:k * D + half * 512 + 512],
                                             start=(k == 0), stop=(k == KD - 1))
                    inc(ins_, "PE")
                sec("tensor", _)

                def _(a, t=t):
                    wt(a, "PE")
                    inc(a.activation(vtok[:, t * D:(t + 1) * D], psv[:, :], AF.Sign), "AC")
                sec("scalar", _)

            def _(sy):
                wt(sy, "AC", qk_ac)
                wt(sy, "LD")
                wt(sy, "SP")
                wt(sy, "OUT")
                wt(sy, "GP")
                for t in range(NT):
                    wt(sy, "TP")
                    inc(sy.dma_start(out=ktok[:, t * D:(t + 1) * D]
                                     .rearrange("p (c j) -> p c j", c=8),
                                     in_=kdm[:, t * D:(t + 1) * D], transpose=True), "TP")
                    wt(sy, "TP")
            sec("sync", _)
            tp_k = C["TP"]

            with contextlib.ExitStack() as esp:
                spre = esp.enter_context(nc.sbuf_tensor(f"spre{ch}", [128, 512], F32))
                def _(v):
                    inc(v.memset(spre[:, :], 0.0), "VE")
                sec("vector", _)

                for t in range(NT):
                    def _(te, t=t):
                        wt(te, "TP", tp_k)
                        wt(te, "AC")
                        wt(te, "VE")
                        for h in range(H):
                            pb = 64 * (h % 2)
                            fs = (h // 2) * 64
                            ins_ = te.matmul(psst[pb:pb + 64, fs:fs + 64],
                                             ktok[:, t * D + h * 64:t * D + (h + 1) * 64],
                                             vtok[:, t * D + h * 64:t * D + (h + 1) * 64],
                                             start=True, stop=True,
                                             skip_group_check=True)
                        inc(ins_, "PE")
                    sec("tensor", _)

                    def _(v, t=t):
                        wt(v, "PE")
                        v.tensor_copy(sprt[:, t * 512:(t + 1) * 512], spre[:, :])
                        v.drain()
                        inc(v.tensor_tensor(spre[:, :], psst[:, :], spre[:, :],
                                            AluOpType.add), "VE")
                    sec("vector", _)

                def _(v):
                    v.drain()
                    inc(v.tensor_copy(ccs[:, :], spre[:, :]), "VE")
                sec("vector", _)

                def _(sy):
                    wt(sy, "VE")
                    wt(sy, "CC")
                    wt(sy, "SP")
                    inc(sy.dma_start(out=cc_in, in_=ccs[:, :]), "SP")
                sec("sync", _)

                def _(g):
                    wt(g, "SP")
                    wt(g, "LD")
                    wt(g, "CC")
                    inc(g.collective_compute(
                        "AllGather", AluOpType.bypass,
                        replica_groups=[[0, 1, 2, 3], [4, 5, 6, 7]],
                        ins=[cc_in], outs=[cc_out]), "CC")
                sec("gpsimd", _)

                def _(sy):
                    wt(sy, "CC")
                    wt(sy, "LD")
                    inc(sy.dma_start(out=gath[:, :].rearrange("p (j f) -> p j f", j=4),
                                     in_=cc_out.rearrange("(j p) f -> p j f", p=128)), "LD")
                sec("sync", _)

                def _(v):
                    wt(v, "LD")
                    v.tensor_copy(sinc[:, :], sbase[:, :])
                    for j in range(4):
                        v.drain()
                        v.scalar_tensor_tensor(sinc[:, :], gath[:, j * 512:(j + 1) * 512],
                                               mska[:, j:j + 1], sinc[:, :],
                                               AluOpType.mult, AluOpType.add)
                    for j in range(4):
                        v.drain()
                        v.scalar_tensor_tensor(sbase[:, :], gath[:, j * 512:(j + 1) * 512],
                                               1.0, sbase[:, :], AluOpType.mult, AluOpType.add)
                    v.drain()
                    for t in range(NT):
                        ins_ = v.tensor_tensor(stot[:, t * 512:(t + 1) * 512],
                                               sprt[:, t * 512:(t + 1) * 512], sinc[:, :],
                                               AluOpType.add)
                    inc(ins_, "VE")
                sec("vector", _)
                s_ready = C["VE"]
                es2.close()
                if _LIMIT == "track":
                    def _(sy):
                        wt(sy, "VE")
                        wt(sy, "OUT")
                        inc(sy.dma_start(out=dbg_aps["dbg"].rearrange("(t p) j -> p t j", p=128),
                                         in_=xsb[:, :].rearrange("p (t j) -> p t j", t=NT)), "OUT")
                        wt(sy, "OUT")
                        inc(sy.dma_start(out=dbg_aps["dbg2"][:, 0:512], in_=sprt[:, 0:512]), "OUT")
                    sec("sync", _)
                    return

                with contextlib.ExitStack() as es3:
                    ec3 = es3.enter_context
                    pssc0 = ec3(nc.psum_tensor(f"pssc{ch}", [128, 512], F32))
                    pssc1 = ec3(nc.psum_tensor(f"pssc2{ch}", [128, 512], F32))
                    pso4 = [ec3(nc.psum_tensor(f"pso{ch}_{i}", [128, 512], F32))
                            for i in range(4)]
                    pssc = [pssc0, pssc1]
                    for t in range(NT):
                        for p in range(2):
                            for q in range(2):
                                def _(te, t=t, p=p, q=q):
                                    wt(te, "VE")
                                    if p == 0 and q == 0:
                                        wt(te, "AC")
                                    for jj in range(4):
                                        j = 4 * q + jj
                                        h = 2 * j + p
                                        pb = 64 * p
                                        fq = j * CT + t * 128
                                        fk = t * D + j * 128
                                        ins_ = te.matmul(pssc[p][:, jj * 128:(jj + 1) * 128],
                                                         kdm[pb:pb + 64, fk:fk + 128],
                                                         qdm[pb:pb + 64, fq:fq + 128],
                                                         start=True, stop=True,
                                                         skip_group_check=True)
                                    inc(ins_, "PE")
                                sec("tensor", _)

                                def _(v, t=t, p=p, q=q):
                                    wt(v, "PE")
                                    inc(v.tensor_tensor(scb[:, :], pssc[p][:, :], triu4[:, :],
                                                        AluOpType.mult), "VE")
                                sec("vector", _)
                                if _ALIMIT == "sc" and t == 0 and p == 0 and q == 0:
                                    def _(g):
                                        wt(g, "VE")
                                        inc(g.dma_start(out=dbg_aps["dbg"][0:128, 0:512], in_=scb[:, :]), "OUT")
                                    sec("gpsimd", _)
                                    return

                                def _(te, t=t, p=p, q=q):
                                    wt(te, "VE")
                                    for jj in range(4):
                                        j = 4 * q + jj
                                        h = 2 * j + p
                                        pb = 64 * p
                                        fq = j * CT + t * 128
                                        te.matmul(pso4[2 * p + q][pb:pb + 64, jj * 128:(jj + 1) * 128],
                                                  vtok[:, t * D + h * 64:t * D + (h + 1) * 64],
                                                  scb[:, jj * 128:(jj + 1) * 128],
                                                  start=True, stop=False,
                                                  skip_group_check=True)
                                        ins_ = te.matmul(pso4[2 * p + q][pb:pb + 64, jj * 128:(jj + 1) * 128],
                                                  stot[pb:pb + 64, t * 512 + j * 64:
                                                       t * 512 + j * 64 + 64],
                                                  qdm[pb:pb + 64, fq:fq + 128],
                                                  start=False, stop=True,
                                                  skip_group_check=True)
                                    inc(ins_, "PE")
                                sec("tensor", _)
                                if _ALIMIT == "ic" and t == 0 and p == 0 and q == 0:
                                    def _(v):
                                        wt(v, "PE")
                                        inc(v.tensor_copy(sprt[0:64, 0:512], pso4[0][0:64, :]), "VE")
                                    sec("vector", _)
                                    def _(g):
                                        wt(g, "VE")
                                        inc(g.dma_start(out=dbg_aps["dbg"][0:64, 0:512], in_=sprt[0:64, 0:512]), "OUT")
                                    sec("gpsimd", _)
                                    return

                        def _(a, t=t):
                            wt(a, "PE")
                            inc(a.nop(), "AC")
                        sec("scalar", _)

                        def _(v, t=t):
                            wt(v, "AC")
                            rp = rpos[:, t * 128:(t + 1) * 128]
                            for p in range(2):
                                for q in range(2):
                                    pb = 64 * p
                                    rpb = rp[pb:pb + 64, :].unsqueeze(1).broadcast_to((64, 4, 128))
                                    dst = attb[pb:pb + 64,
                                               t * KD * 128 + q * 4 * 128:
                                               t * KD * 128 + (q * 4 + 4) * 128]
                                    ins_ = v.tensor_tensor(
                                        dst.rearrange("p (s i) -> p s i", s=4),
                                        pso4[2 * p + q][pb:pb + 64, :].rearrange("p (s i) -> p s i", s=4),
                                        rpb, AluOpType.mult)
                            inc(ins_, "VE")
                        sec("vector", _)
                        def _(a, t=t):
                            wt(a, "VE")
                            inc(a.nop(), "AC")
                        sec("scalar", _)

            if _LIMIT == "main":
                def _(g):
                    wt(g, "VE")
                    wt(g, "OUT")
                    inc(g.dma_start(out=dbg_aps["dbg"].rearrange("(a p) j -> p a j", p=128),
                                    in_=attb[:, :].rearrange("p (a j) -> p a j", a=NT)), "OUT")
                sec("gpsimd", _)
                return

            with contextlib.ExitStack() as esy:
                psy = esy.enter_context(nc.psum_tensor(f"psy{ch}", [128, 1024], F32))
                for t in range(NT):
                    def _(te, t=t):
                        wt(te, "VE")
                        wt(te, "LD", vw_ld)
                        if t > 0:
                            wt(te, "AC")
                        for half in range(2):
                            for k in range(KD):
                                ins_ = te.matmul(psy[:, half * 512:(half + 1) * 512],
                                                 attb[:, (t * KD + k) * 128:(t * KD + k + 1) * 128],
                                                 wow[:, k * D + half * 512:k * D + half * 512 + 512],
                                                 start=(k == 0), stop=(k == KD - 1))
                        inc(ins_, "PE")
                    sec("tensor", _)

                    def _(v, t=t):
                        wt(v, "PE")
                        xt = xsb[:, t * D:(t + 1) * D]
                        inc(v.tensor_tensor(xt, psy[:, :], xt, AluOpType.add), "VE")
                    sec("vector", _)

                    def _(a, t=t):
                        wt(a, "VE")
                        inc(a.nop(), "AC")
                    sec("scalar", _)

    if _LIMIT == "attn":
        def _(sy):
            wt(sy, "VE")
            wt(sy, "OUT")
            inc(sy.dma_start(out=dbg_aps["dbg"].rearrange("(t p) j -> p t j", p=128),
                             in_=xsb[:, :].rearrange("p (t j) -> p t j", t=NT)), "OUT")
        sec("sync", _)
        return
    ffn_phase(f"f2c{ch}", "out")

    if ch + 1 < NCH:
        def _(sy):
            wt(sy, "VE")
            wt(sy, "LD")
            inc(sy.dma_start(
                out=xsb[:, :].rearrange("p (t j) -> p t j", t=NT),
                in_=x_d[(ch + 1) * CT:(ch + 2) * CT, :].rearrange("(t p) j -> p t j", p=128)), "LD")
        sec("sync", _)

        def _(v):
            wt(v, "LD")
            inc(v.nop(), "VE")
        sec("vector", _)



def _spc(v, scrub, src):
    v.tensor_copy(scrub[:, 2:3], src)

_NC_CACHE = None
_NC_KEY = 'x'


def kernel(L0_attn_x, L0_attn_W_qkv, L0_attn_W_o, L0_ffn_w_gate, L0_ffn_w_up,
           L0_ffn_w_down):
    global _NC_CACHE
    x = np.ascontiguousarray(np.asarray(L0_attn_x, dtype=np.float32)).reshape(B * T, D)
    wqkv = np.ascontiguousarray(np.asarray(L0_attn_W_qkv, dtype=np.float32))
    wo = np.ascontiguousarray(np.asarray(L0_attn_W_o, dtype=np.float32))
    wg = np.ascontiguousarray(np.asarray(L0_ffn_w_gate, dtype=np.float32))
    wu = np.ascontiguousarray(np.asarray(L0_ffn_w_up, dtype=np.float32))
    wd = np.ascontiguousarray(np.asarray(L0_ffn_w_down, dtype=np.float32))

    global _NC_KEY
    if _NC_CACHE is None or _NC_KEY != _LIMIT:
        _NC_CACHE = build_nc()
        _NC_KEY = _LIMIT
    nc = _NC_CACHE

    triu = np.tile(np.triu(np.ones((128, 128), dtype=np.float32)), (1, 4))
    in_maps = []
    spans = []
    for c in range(NCORE):
        seq, q = divmod(c, 4)
        starts = [seq * T + ch * (4 * CT) + q * CT for ch in range(NCH)]
        spans.append(starts)
        xo = np.concatenate([x[s:s + CT] for s in starts], axis=0)
        pos = np.concatenate(
            [np.arange(ch * 4 * CT + q * CT + 1, ch * 4 * CT + q * CT + CT + 1)
             for ch in range(NCH)]).astype(np.float32)
        rpos = np.broadcast_to((1.0 / pos)[None, :], (128, NCH * CT)).copy()
        mask = np.broadcast_to((np.arange(4) < q).astype(np.float32)[None, :], (128, 4)).copy()
        in_maps.append({
            "x": xo, "wqkv": wqkv, "wo": wo, "wg": wg, "wu": wu, "wd": wd,
            "rposd": rpos, "triu4": triu, "maska": mask,
        })

    res = run_bass_kernel_spmd(nc, in_maps, core_ids=list(range(NCORE)))
    global _RES
    _RES = res
    out = np.empty((B * T, D), dtype=np.float32)
    for c in range(NCORE):
        yc = res.results[c]["y"]
        for ch in range(NCH):
            out[spans[c][ch]:spans[c][ch] + CT] = yc[ch * CT:(ch + 1) * CT]
    return out.reshape(B, T, D)


# version 02
# speedup vs baseline: 1.0867x; 1.0867x over previous
import numpy as np
import ml_dtypes
import concourse.bacc as bacc
import concourse.mybir as mybir
import concourse.dve_ops as dve_ops
from concourse.dve_op import DveOp
from concourse.dve_spec import (
    Spec, Src0, Src1, C0, C1, C2, Zero, One, maxx, minn, lower, Bin, AluOp,
)
from concourse.alu_op_type import AluOpType
from concourse.bass_utils import run_bass_kernel_spmd

F32 = mybir.dt.float32
BF16 = mybir.dt.bfloat16
AF = mybir.ActivationFunctionType
AX = mybir.AxisListType.X
MAGIC = float(2 ** 23)

def _mk_op(name, spec, reference):
    sha = lower(spec, ver="v3", perf_en=False, name=name).sha("v3")
    return DveOp(name, spec, subdim=False, uops_sha={"v3": sha}, reference=reference)

_t = Bin(AluOp.ABSOLUTE_VALUE, Src0, Src0)
_two = One + One
_w = minn(minn(_t + _t, _t + _two), _t * C0 + C1)
FP4A_SPEC = Spec(body=_w + C2)

def _ref_fp4a(in0, in1, s0, s1, imm2):
    t = np.abs(in0)
    w = np.minimum(np.minimum(t + t, t + 2.0), t * 0.5 + 4.0)
    return (np.ceil(w - 0.5) + imm2).astype(in0.dtype)

_g = Src0 >= C0
SIGNSCALE_SPEC = Spec(body=((_g + _g) - C1) * Src1)

def _ref_signscale(in0, in1, s0, s1, imm2):
    return (np.where(in0 >= 0, 1.0, -1.0) * in1).astype(in0.dtype)

_p1 = Src0 + Src0
_p2 = _p1 + _p1
FP4B_SPEC = Spec(body=maxx(maxx(Src0 - C2, _p1 - C0), _p2 - C1) * Src1)

def _ref_fp4b(in0, in1, s0, s1, imm2):
    i = in0 - imm2
    q2 = np.maximum(np.maximum(i, i + i - 4.0), 4.0 * i - 16.0)
    return (q2 * in1).astype(in0.dtype)

FP4A = _mk_op("ANT_FP4A", FP4A_SPEC, _ref_fp4a)
SIGNSCALE = _mk_op("ANT_SIGNSCALE", SIGNSCALE_SPEC, _ref_signscale)
FP4B = _mk_op("ANT_FP4B", FP4B_SPEC, _ref_fp4b)

def _register_ops():
    names = [o.name for o in dve_ops.OPS]
    if "ANT_FP4A" in names:
        return
    i0 = names.index("DVEHASH32L_SEED")
    for j, op in enumerate([FP4A, SIGNSCALE, FP4B]):
        old = dve_ops.OPS[i0 + j]
        dve_ops.OPS[i0 + j] = op
        dve_ops._SUB_OPCODE_FOR_NAME[op.name] = dve_ops._SUB_OPCODE_FOR_NAME[old.name]
        dve_ops.CUSTOM_DVE_REFS[op.name] = op.reference
        dve_ops.CUSTOM_DVE_SPECS[op.name] = op.spec

_register_ops()

B, T, D = 2, 8192, 1024
II = 2816
H, DH = 16, 64
NCH = 4
CT = 512
NT = CT // 128
NCORE = 8
EPS = 1e-5
KD = D // 128
KI = II // 128
OC = [(0, 512), (512, 512), (1024, 512), (1536, 512), (2048, 512), (2560, 256)]

_DEBUG = False
_LIMIT = None
_FSTAGE = None
_ALIMIT = None


def build_nc():
    nc = bacc.Bacc("TRN2", target_bir_lowering=False, debug=False)

    x_d = nc.dram_tensor("x", [NCH * CT, D], F32, kind="ExternalInput").ap()
    wqkv_d = nc.dram_tensor("wqkv", [3 * D, D], F32, kind="ExternalInput").ap()
    wo_d = nc.dram_tensor("wo", [D, D], F32, kind="ExternalInput").ap()
    wg_d = nc.dram_tensor("wg", [3 * 128, D], F32, kind="ExternalInput").ap()
    wu_d = nc.dram_tensor("wu", [3 * 128, D], F32, kind="ExternalInput").ap()
    wd_d = nc.dram_tensor("wd", [128, II], F32, kind="ExternalInput").ap()
    rpos_d = nc.dram_tensor("rposd", [128, NCH * CT], F32, kind="ExternalInput").ap()
    triu_d = nc.dram_tensor("triu4", [128, 512], F32, kind="ExternalInput").ap()
    mska_d = nc.dram_tensor("maska", [128, 4], F32, kind="ExternalInput").ap()
    y_d = nc.dram_tensor("y", [NCH * CT, D], F32, kind="ExternalOutput").ap()

    wgT_d = nc.dram_tensor("wgT", [128, KD * II], BF16, kind="Internal").ap()
    wuT_d = nc.dram_tensor("wuT", [128, KD * II], BF16, kind="Internal").ap()
    wdT_d = nc.dram_tensor("wdT", [128, KI * D], BF16, kind="Internal").ap()
    wqkvT_d = nc.dram_tensor("wqkvT", [128, KD * 3 * D], BF16, kind="Internal").ap()
    woT_d = nc.dram_tensor("woT", [128, KD * D], BF16, kind="Internal").ap()
    cc_in = nc.dram_tensor("cc_in", [128, 512], F32, kind="Internal").ap()
    cc_out = nc.dram_tensor("cc_out", [4 * 128, 512], F32, kind="Internal").ap()
    QIN = 3 * KD * 128 * 2 + KI * 128
    wq_in = nc.dram_tensor("wq_in", [128, QIN], BF16, kind="Internal").ap()
    wq_out = nc.dram_tensor("wq_out", [8 * 128, QIN], BF16, kind="Internal").ap()

    dbg_aps = {}
    if _LIMIT in ("ffn1", "track", "attn", "main") or _FSTAGE or _ALIMIT:
        dbg_aps["dbg"] = nc.dram_tensor("dbg", [CT, D], F32, kind="ExternalOutput").ap()
        dbg_aps["dbg2"] = nc.dram_tensor("dbg2", [128, 4 * 512], F32, kind="ExternalOutput").ap()

    C = {}
    SEM = {}

    import contextlib
    st = contextlib.ExitStack()

    def sem(name):
        SEM[name] = st.enter_context(nc.semaphore(name))
        C[name] = 0

    for s in ["LD", "GP", "TP", "SP", "VE", "AC", "PE", "CC", "OUT"]:
        sem(s)

    def inc(instr, s, k=None):
        k = k if k is not None else (16 if s in ("LD", "GP", "TP", "SP", "OUT") else 1)
        instr.then_inc(SEM[s], k)
        C[s] += k

    def wt(eng, s, v=None):
        eng.wait_ge(SEM[s], C[s] if v is None else v)

    with st:
        xsb = st.enter_context(nc.sbuf_tensor("xsb", [128, NT * D], F32))
        xout = st.enter_context(nc.sbuf_tensor("xout", [128, D], F32))
        rpos = st.enter_context(nc.sbuf_tensor("rposc", [128, CT], F32))
        triu4 = st.enter_context(nc.sbuf_tensor("triu4s", [128, 512], F32))
        mska = st.enter_context(nc.sbuf_tensor("mskas", [128, 4], F32))
        sbase = st.enter_context(nc.sbuf_tensor("sbase", [128, 512], F32))
        sinc = st.enter_context(nc.sbuf_tensor("sinc", [128, 512], F32))
        ccs = st.enter_context(nc.sbuf_tensor("ccs", [128, 512], F32))
        stats = st.enter_context(nc.sbuf_tensor("stats", [128, 16], F32))
        scrub = st.enter_context(nc.sbuf_tensor("scrub", [128, 4], F32))
        block = st.enter_context(nc.Block())
        if True:
            def sec(engname, fn):
                getattr(block, engname)(fn)

            import contextlib as _ctx
            with _ctx.ExitStack() as es0:
                ec0 = es0.enter_context
                wf32 = ec0(nc.sbuf_tensor("wf32", [128, II], F32))
                wz = ec0(nc.sbuf_tensor("wz", [128, II], F32))
                wsbf = ec0(nc.sbuf_tensor("wsbf", [128, II], F32))
                wq = ec0(nc.sbuf_tensor("wq", [128, II], BF16))
                wtst = ec0(nc.sbuf_tensor("wtst", [128, II], BF16))
                wmb = ec0(nc.sbuf_tensor("wmb", [128, II // 16], F32))
                wrs = ec0(nc.sbuf_tensor("wrs", [128, II // 16], F32))
                wsb2 = ec0(nc.sbuf_tensor("wsb2", [128, II // 16], F32))
                def _(sy):
                    wt(sy, "LD")
                    wt(sy, "SP")
                    wt(sy, "OUT")
                    wt(sy, "GP")
                    wt(sy, "TP")
                    wt(sy, "LD")
                    inc(sy.dma_start(out=triu4[:, :], in_=triu_d), "LD")
                    wt(sy, "LD")
                    inc(sy.dma_start(out=mska[:, :], in_=mska_d), "LD")
                    wt(sy, "LD")
                    inc(sy.dma_start(
                        out=xsb[:, :].rearrange("p (t j) -> p t j", t=NT),
                        in_=x_d[0:CT, :].rearrange("(t p) j -> p t j", p=128)), "LD")
                sec("sync", _)

                def _(v):
                    v.memset(sbase[:, :], 0.0)
                    inc(v.memset(scrub[:, 1:2], EPS), "VE")
                sec("vector", _)

                def quant_weight(src_d, off, nrows, K):
                    npc = nrows // 128
                    kc = K // 128
                    NB = K // 16
                    for pc in range(npc):
                        def _(sy, pc=pc):
                            wt(sy, "VE")
                            wt(sy, "LD")
                            inc(sy.dma_start(out=wf32[:, 0:K],
                                             in_=src_d[pc * 128:(pc + 1) * 128, :]), "LD")
                        sec("sync", _)

                        def _(v, pc=pc, K=K, NB=NB):
                            wt(v, "LD")
                            wt(v, "TP")
                            rsb = wrs[:, 0:NB].unsqueeze(2).broadcast_to((128, NB, 16))
                            sbb = wsb2[:, 0:NB].unsqueeze(2).broadcast_to((128, NB, 16))
                            v.tensor_reduce(wmb[:, 0:NB],
                                            wf32[:, 0:K].rearrange("p (a b) -> p a b", b=16),
                                            AX, AluOpType.max, apply_absolute_value=True)
                            v.drain()
                            v.tensor_scalar(wmb[:, 0:NB], wmb[:, 0:NB], 6e-12, None, AluOpType.max)
                            v.drain()
                            v.reciprocal(wrs[:, 0:NB], wmb[:, 0:NB])
                            v.tensor_scalar(wsb2[:, 0:NB], wmb[:, 0:NB], 1.0 / 12.0, None, AluOpType.mult)
                            v.drain()
                            v.tensor_scalar(wrs[:, 0:NB], wrs[:, 0:NB], 6.0, None, AluOpType.mult)
                            v.tensor_copy(wsbf[:, 0:K].rearrange("p (a b) -> p a b", b=16), sbb)
                            v.drain()
                            v.tensor_tensor(wz[:, 0:K].rearrange("p (a b) -> p a b", b=16),
                                            wf32[:, 0:K].rearrange("p (a b) -> p a b", b=16),
                                            rsb, AluOpType.mult)
                            v._custom_dve(SIGNSCALE, out=wsbf[:, 0:K], in0=wf32[:, 0:K],
                                          in1=wsbf[:, 0:K], s0=0.0, s1=1.0)
                            v.drain()
                            v._custom_dve(FP4A, out=wz[:, 0:K], in0=wz[:, 0:K],
                                          s0=0.5, s1=4.0, imm2=MAGIC)
                            v.drain()
                            inc(v._custom_dve(FP4B, out=wq[:, 0:K], in0=wz[:, 0:K],
                                              in1=wsbf[:, 0:K], s0=2 * MAGIC + 4.0,
                                              s1=4 * MAGIC + 16.0, imm2=MAGIC), "VE")
                        sec("vector", _)

                        def _(sy, pc=pc, K=K, kc=kc, off=off):
                            wt(sy, "VE")
                            wt(sy, "LD")
                            wt(sy, "SP")
                            wt(sy, "OUT")
                            wt(sy, "GP")
                            wt(sy, "TP")
                            inc(sy.dma_start(
                                out=wtst[:, 0:K].rearrange("p (c j) -> p c j", c=kc),
                                in_=wq[:, 0:K], transpose=True), "TP")
                            wt(sy, "TP")
                            wt(sy, "SP")
                            inc(sy.dma_start(
                                out=wq_in[:, off + pc * K:off + (pc + 1) * K],
                                in_=wtst[:, 0:K]), "SP")
                        sec("sync", _)

                quant_weight(wg_d, 0, 3 * 128, D)
                quant_weight(wu_d, 3 * KD * 128, 3 * 128, D)
                quant_weight(wd_d, 6 * KD * 128, 128, II)

                def cast_weight(src_d, dst_d, nrows):
                    npc = nrows // 128
                    for pc in range(npc):
                        def _(g, pc=pc):
                            wt(g, "TP")
                            wt(g, "GP")
                            inc(g.dma_start(out=wq[:, 0:D],
                                            in_=src_d[pc * 128:(pc + 1) * 128, :]), "GP")
                        sec("gpsimd", _)

                        def _(sy, pc=pc, nrows=nrows):
                            wt(sy, "GP")
                            wt(sy, "LD")
                            wt(sy, "SP")
                            wt(sy, "OUT")
                            wt(sy, "TP")
                            inc(sy.dma_start(
                                out=wtst[:, 0:D].rearrange("p (c j) -> p c j", c=KD),
                                in_=wq[:, 0:D], transpose=True), "TP")
                            wt(sy, "TP")
                            dst = dst_d.rearrange("p (c r) -> p c r", c=KD)[:, :, pc * 128:(pc + 1) * 128]
                            wt(sy, "SP")
                            inc(sy.dma_start(out=dst,
                                             in_=wtst[:, 0:D].rearrange("p (c j) -> p c j", c=KD)), "SP")
                        sec("sync", _)

                cast_weight(wqkv_d, wqkvT_d, 3 * D)
                cast_weight(wo_d, woT_d, D)

                def _(g):
                    wt(g, "SP")
                    wt(g, "GP")
                    wt(g, "CC")
                    inc(g.collective_compute(
                        "AllGather", AluOpType.bypass,
                        replica_groups=[[0, 1, 2, 3, 4, 5, 6, 7]],
                        ins=[wq_in], outs=[wq_out]), "CC")
                sec("gpsimd", _)

                def _(sy):
                    wt(sy, "CC")
                    for dst_d2, off, kc2, nt, spc in (
                            (wgT_d, 0, KD, 22, 3),
                            (wuT_d, 3 * KD * 128, KD, 22, 3),
                            (wdT_d, 6 * KD * 128, KI, 8, 1)):
                        for t in range(nt):
                            j, s = divmod(t, spc)
                            src = wq_out[j * 128:(j + 1) * 128,
                                         off + s * kc2 * 128:
                                         off + (s + 1) * kc2 * 128].rearrange(
                                "p (c r) -> p c r", c=kc2)
                            dst = dst_d2.rearrange(
                                "p (c r) -> p c r", c=kc2)[:, :, t * 128:(t + 1) * 128]
                            wt(sy, "SP")
                            inc(sy.dma_start(out=dst, in_=src), "SP")
                sec("sync", _)

            nch = 0 if _LIMIT == "setup" else (1 if _LIMIT in ("ffn1", "track", "attn", "chunk1", "main") else NCH)
            for ch in range(nch):
                chunk(nc, block, sec, C, SEM, inc, wt, ch,
                      x_d, y_d, wgT_d, wuT_d, wdT_d, wqkvT_d, woT_d,
                      cc_in, cc_out, rpos_d,
                      xsb, xout, rpos, triu4, mska, sbase, sinc, ccs,
                      stats, scrub, dbg_aps)

            def _(sy):
                wt(sy, "OUT")
            sec("sync", _)

    nc.finalize()
    return nc


def chunk(nc, block, sec, C, SEM, inc, wt, ch,
          x_d, y_d, wgT_d, wuT_d, wdT_d, wqkvT_d, woT_d, cc_in, cc_out, rpos_d,
          xsb, xout, rpos, triu4, mska, sbase, sinc, ccs, stats, scrub,
          dbg_aps):
    F32 = mybir.dt.float32
    BF16 = mybir.dt.bfloat16

    def ffn_phase(tag, resid_scale_out):
        import contextlib
        with contextlib.ExitStack() as es:
            ec = es.enter_context
            WgT = ec(nc.sbuf_tensor(f"WgT{tag}", [128, KD * II], BF16))
            WuT = ec(nc.sbuf_tensor(f"WuT{tag}", [128, KD * II], BF16))
            WdT = ec(nc.sbuf_tensor(f"WdT{tag}", [128, KI * D], BF16))
            cbuf = ec(nc.sbuf_tensor(f"cbuf{tag}", [128, D], F32))
            zbuf = ec(nc.sbuf_tensor(f"zbuf{tag}", [128, D], F32))
            sbf = ec(nc.sbuf_tensor(f"sbf{tag}", [128, D], F32))
            xq = ec(nc.sbuf_tensor(f"xq{tag}", [128, D], BF16))
            xqT = ec(nc.sbuf_tensor(f"xqT{tag}", [128, KD * 128], BF16))
            sgb = ec(nc.sbuf_tensor(f"sg{tag}", [128, 512], F32))
            hb = ec(nc.sbuf_tensor(f"hb{tag}", [128, 512], F32))
            zh = ec(nc.sbuf_tensor(f"zh{tag}", [128, 512], F32))
            hq = ec(nc.sbuf_tensor(f"hq{tag}", [128, II], BF16))
            hqT = ec(nc.sbuf_tensor(f"hqT{tag}", [128, KI * 128], BF16))
            mb = ec(nc.sbuf_tensor(f"mb{tag}", [128, 64], F32))
            rs = ec(nc.sbuf_tensor(f"rs{tag}", [128, 64], F32))
            sb2 = ec(nc.sbuf_tensor(f"sb2{tag}", [128, 64], F32))
            psg0 = ec(nc.psum_tensor(f"psg{tag}", [128, 512], F32))
            psg1 = ec(nc.psum_tensor(f"psg2{tag}", [128, 512], F32))
            psu0 = ec(nc.psum_tensor(f"psu{tag}", [128, 512], F32))
            psu1 = ec(nc.psum_tensor(f"psu2{tag}", [128, 512], F32))
            psd = ec(nc.psum_tensor(f"psd{tag}", [128, 1024], F32))
            psg = [psg0, psg1]
            psu = [psu0, psu1]

            def _(sy):
                wt(sy, "VE")
                wt(sy, "AC")
                wt(sy, "PE")
                wt(sy, "SP")
                wt(sy, "LD")
                inc(sy.dma_start(out=WgT[:, :], in_=wgT_d), "LD")
                wt(sy, "LD")
                inc(sy.dma_start(out=WuT[:, :], in_=wuT_d), "LD")
                wt(sy, "LD")
                inc(sy.dma_start(out=WdT[:, :], in_=wdT_d), "LD")
            sec("sync", _)
            w_ld = C["LD"]

            fstage = _FSTAGE if tag.startswith("f1c0") else None
            ntt = 1 if fstage else NT
            for t in range(ntt):
                xt = xsb[:, t * D:(t + 1) * D]

                def _(a, t=t):
                    wt(a, "VE")
                    if t == 0:
                        wt(a, "LD")
                    a.activation(zbuf[:, :], xt, AF.Square, accum_out=stats[:, 1:2])
                    a.activation(cbuf[:, :], xt, AF.Identity, accum_out=stats[:, 0:1])
                    a.drain()
                    a.activation(stats[:, 2:3], stats[:, 0:1], AF.Identity, scale=1.0 / D)
                    a.activation(stats[:, 3:4], stats[:, 1:2], AF.Identity,
                                 scale=1.0 / D, bias=scrub[:, 1:2])
                    a.drain()
                    a.activation(stats[:, 8:9], stats[:, 2:3], AF.Square)
                    a.drain()
                    a.activation(stats[:, 4:5], stats[:, 8:9], AF.Identity,
                                 scale=-1.0, bias=stats[:, 3:4])
                    a.drain()
                    a.activation(stats[:, 5:6], stats[:, 4:5], AF.Ln)
                    a.drain()
                    inc(a.activation(stats[:, 6:7], stats[:, 5:6], AF.Exp, scale=-0.5), "AC")
                sec("scalar", _)
                if fstage == "ln":
                    def _(sy):
                        wt(sy, "AC")
                        wt(sy, "OUT")
                        inc(sy.dma_start(out=dbg_aps["dbg2"][:, 0:16], in_=stats[:, :]), "OUT")
                    sec("sync", _)
                    return

                def _(v, t=t):
                    wt(v, "AC")
                    rsb = rs[:, :].unsqueeze(2).broadcast_to((128, 64, 16))
                    sbb = sb2[:, :].unsqueeze(2).broadcast_to((128, 64, 16))
                    v.tensor_scalar(cbuf[:, :], xt, stats[:, 2:3], None, AluOpType.subtract)
                    v.drain()
                    v.tensor_reduce(mb[:, :], cbuf[:, :].rearrange("p (a b) -> p a b", b=16),
                                    AX, AluOpType.max, apply_absolute_value=True)
                    v.drain()
                    v.tensor_scalar(mb[:, :], mb[:, :], 6e-12, None, AluOpType.max)
                    v.drain()
                    v.reciprocal(rs[:, :], mb[:, :])
                    v.tensor_scalar(sb2[:, :], mb[:, :], stats[:, 6:7], 1.0 / 12.0,
                                    AluOpType.mult, AluOpType.mult)
                    v.drain()
                    v.tensor_scalar(rs[:, :], rs[:, :], 6.0, None, AluOpType.mult)
                    v.tensor_copy(sbf[:, :].rearrange("p (a b) -> p a b", b=16), sbb)
                    v.drain()
                    v.tensor_tensor(zbuf[:, :].rearrange("p (a b) -> p a b", b=16),
                                    cbuf[:, :].rearrange("p (a b) -> p a b", b=16),
                                    rsb, AluOpType.mult)
                    v._custom_dve(SIGNSCALE, out=cbuf[:, :], in0=cbuf[:, :], in1=sbf[:, :],
                                  s0=0.0, s1=1.0)
                    v.drain()
                    v._custom_dve(FP4A, out=zbuf[:, :], in0=zbuf[:, :], s0=0.5, s1=4.0, imm2=MAGIC)
                    v.drain()
                    inc(v._custom_dve(FP4B, out=xq[:, :], in0=zbuf[:, :], in1=cbuf[:, :],
                                      s0=2 * MAGIC + 4.0, s1=4 * MAGIC + 16.0, imm2=MAGIC), "VE")
                sec("vector", _)

                def _(sy, t=t):
                    wt(sy, "VE")
                    wt(sy, "LD")
                    wt(sy, "SP")
                    wt(sy, "OUT")
                    wt(sy, "GP")
                    wt(sy, "TP")
                    inc(sy.dma_start(out=xqT[:, :].rearrange("p (c j) -> p c j", c=KD),
                                     in_=xq[:, :], transpose=True), "TP")
                    wt(sy, "TP")
                sec("sync", _)
                tp_xq = C["TP"]
                if fstage == "quant":
                    def _(v):
                        wt(v, "TP")
                        inc(v.tensor_copy(sbf[:, :], xq[:, :]), "VE")
                    sec("vector", _)
                    def _(sy):
                        wt(sy, "VE")
                        wt(sy, "OUT")
                        inc(sy.dma_start(out=dbg_aps["dbg"][0:128, :], in_=sbf[:, :]), "OUT")
                        wt(sy, "OUT")
                        inc(sy.dma_start(out=dbg_aps["dbg2"][:, 0:64], in_=mb[:, :]), "OUT")
                    sec("sync", _)
                    return

                if fstage == "wload":
                    def _(a, t=t):
                        wt(a, "LD", w_ld)
                        wt(a, "TP", tp_xq)
                        inc(a.activation(sgb[:, :], WgT[:, 0:512], AF.Identity), "AC")
                    sec("scalar", _)
                    def _(v, t=t):
                        wt(v, "TP", tp_xq)
                        inc(v.tensor_copy(hb[:, 0:128], xqT[:, 0:128]), "VE")
                    sec("vector", _)
                    def _(sy):
                        wt(sy, "AC")
                        wt(sy, "VE")
                        wt(sy, "OUT")
                        inc(sy.dma_start(out=dbg_aps["dbg"][0:128, 0:512], in_=sgb[:, :]), "OUT")
                        wt(sy, "OUT")
                        inc(sy.dma_start(out=dbg_aps["dbg"][128:256, 0:128], in_=hb[:, 0:128]), "OUT")
                    sec("sync", _)
                    return
                if fstage in ("mm", "mm1", "mmc"):
                    if fstage == "mmc":
                        def _(v, t=t):
                            wt(v, "LD", w_ld)
                            inc(v.tensor_copy(hq[:, 0:512], WgT[:, 0:512]), "VE")
                        sec("vector", _)
                    def _(te, t=t):
                        wt(te, "TP", tp_xq)
                        wt(te, "LD", w_ld)
                        wt(te, "VE")
                        nk = 1 if fstage == "mm1" else KD
                        for k in range(nk):
                            rhs = hq[:, 0:512] if fstage == "mmc" else WgT[:, k * II: k * II + 512]
                            ins_ = te.matmul(psg[0][:, :], xqT[:, k * 128:(k + 1) * 128], rhs,
                                             start=(k == 0), stop=(k == nk - 1))
                        inc(ins_, "PE")
                    sec("tensor", _)
                    def _(a, t=t):
                        wt(a, "PE")
                        inc(a.activation(sgb[:, :], psg[0][:, :], AF.Identity), "AC")
                    sec("scalar", _)
                    def _(sy):
                        wt(sy, "AC")
                        wt(sy, "OUT")
                        inc(sy.dma_start(out=dbg_aps["dbg"][0:128, 0:512], in_=sgb[:, :]), "OUT")
                    sec("sync", _)
                    return

                for oi, (o0, ow) in enumerate(OC):
                    bk = oi % 2
                    nb = ow // 16
                    def _(te, t=t, oi=oi, o0=o0, ow=ow, bk=bk):
                        wt(te, "TP", tp_xq)
                        wt(te, "LD", w_ld)
                        wt(te, "VE")
                        wt(te, "AC")
                        for k in range(KD):
                            ins_ = te.matmul(psg[bk][:, 0:ow], xqT[:, k * 128:(k + 1) * 128],
                                             WgT[:, k * II + o0:k * II + o0 + ow],
                                             start=(k == 0), stop=(k == KD - 1))
                        inc(ins_, "PE")
                        for k in range(KD):
                            ins_ = te.matmul(psu[bk][:, 0:ow], xqT[:, k * 128:(k + 1) * 128],
                                             WuT[:, k * II + o0:k * II + o0 + ow],
                                             start=(k == 0), stop=(k == KD - 1))
                        inc(ins_, "PE")
                    sec("tensor", _)

                    def _(a, t=t, oi=oi, o0=o0, ow=ow, bk=bk):
                        wt(a, "PE")
                        inc(a.activation(sgb[:, 0:ow], psg[bk][:, 0:ow], AF.Silu), "AC")
                    sec("scalar", _)

                    def _(v, t=t, oi=oi, o0=o0, ow=ow, bk=bk, nb=nb):
                        wt(v, "AC")
                        rsb = rs[:, 0:nb].unsqueeze(2).broadcast_to((128, nb, 16))
                        sbb = sb2[:, 0:nb].unsqueeze(2).broadcast_to((128, nb, 16))
                        v.tensor_tensor(hb[:, 0:ow], psu[bk][:, 0:ow], sgb[:, 0:ow], AluOpType.mult)
                        v.drain()
                        if fstage == "gu0" and oi == 0:
                            inc(v.tensor_copy(zh[:, 0:ow], hb[:, 0:ow]), "VE")
                            return
                        v.tensor_reduce(mb[:, 0:nb], hb[:, 0:ow].rearrange("p (a b) -> p a b", b=16),
                                        AX, AluOpType.max, apply_absolute_value=True)
                        v.drain()
                        v.tensor_scalar(mb[:, 0:64], mb[:, 0:64], 6e-12, None, AluOpType.max)
                        v.drain()
                        v.reciprocal(rs[:, 0:64], mb[:, 0:64])
                        v.tensor_scalar(sb2[:, 0:64], mb[:, 0:64], 1.0 / 12.0, None, AluOpType.mult)
                        v.drain()
                        v.tensor_scalar(rs[:, 0:64], rs[:, 0:64], 6.0, None, AluOpType.mult)
                        v.tensor_copy(sbf[:, 0:ow].rearrange("p (a b) -> p a b", b=16), sbb)
                        v.drain()
                        v.tensor_tensor(zh[:, 0:ow].rearrange("p (a b) -> p a b", b=16),
                                        hb[:, 0:ow].rearrange("p (a b) -> p a b", b=16),
                                        rsb, AluOpType.mult)
                        v._custom_dve(SIGNSCALE, out=hb[:, 0:ow], in0=hb[:, 0:ow],
                                      in1=sbf[:, 0:ow], s0=0.0, s1=1.0)
                        v.drain()
                        v._custom_dve(FP4A, out=zh[:, 0:ow], in0=zh[:, 0:ow],
                                      s0=0.5, s1=4.0, imm2=MAGIC)
                        v.drain()
                        inc(v._custom_dve(FP4B, out=hq[:, o0:o0 + ow], in0=zh[:, 0:ow],
                                          in1=hb[:, 0:ow], s0=2 * MAGIC + 4.0,
                                          s1=4 * MAGIC + 16.0, imm2=MAGIC), "VE")
                    sec("vector", _)
                    if fstage == "gu0" and oi == 0:
                        def _(sy):
                            wt(sy, "VE")
                            wt(sy, "OUT")
                            inc(sy.dma_start(out=dbg_aps["dbg"][0:128, 0:512], in_=zh[:, 0:512]), "OUT")
                            wt(sy, "OUT")
                            inc(sy.dma_start(out=dbg_aps["dbg"][128:256, 0:512], in_=sgb[:, 0:512]), "OUT")
                        sec("sync", _)
                        return
                    if fstage == "gu" and oi == 0:
                        def _(v):
                            inc(v.tensor_copy(zh[:, :], hq[:, 0:512]), "VE")
                        sec("vector", _)
                        def _(sy):
                            wt(sy, "VE")
                            wt(sy, "OUT")
                            inc(sy.dma_start(out=dbg_aps["dbg"][0:128, 0:512], in_=zh[:, :]), "OUT")
                            wt(sy, "OUT")
                            inc(sy.dma_start(out=dbg_aps["dbg"][128:256, 0:512], in_=sgb[:, :]), "OUT")
                        sec("sync", _)
                        return

                def _(sy, t=t):
                    wt(sy, "VE")
                    wt(sy, "LD")
                    wt(sy, "SP")
                    wt(sy, "OUT")
                    wt(sy, "GP")
                    wt(sy, "TP")
                    inc(sy.dma_start(out=hqT[:, :].rearrange("p (c j) -> p c j", c=KI),
                                     in_=hq[:, :], transpose=True), "TP")
                    wt(sy, "TP")
                sec("sync", _)
                tp_hq = C["TP"]

                def _(te, t=t):
                    wt(te, "TP", tp_hq)
                    wt(te, "VE")
                    for half in range(2):
                        for k in range(KI):
                            ins_ = te.matmul(psd[:, half * 512:(half + 1) * 512],
                                             hqT[:, k * 128:(k + 1) * 128],
                                             WdT[:, k * D + half * 512:k * D + half * 512 + 512],
                                             start=(k == 0), stop=(k == KI - 1))
                    inc(ins_, "PE")
                sec("tensor", _)

                if resid_scale_out is None:
                    def _(v, t=t):
                        wt(v, "PE")
                        inc(v.scalar_tensor_tensor(xt, psd[:, 0:D], 0.5, xt,
                                                   AluOpType.mult, AluOpType.add), "VE")
                    sec("vector", _)
                    if fstage == "down":
                        def _(sy):
                            wt(sy, "VE")
                            wt(sy, "OUT")
                            inc(sy.dma_start(out=dbg_aps["dbg"][0:128, :], in_=xt), "OUT")
                        sec("sync", _)
                        return
                else:
                    def _(v, t=t):
                        wt(v, "PE")
                        wt(v, "OUT")
                        inc(v.scalar_tensor_tensor(xout[:, :], psd[:, 0:D], 0.5, xt,
                                                   AluOpType.mult, AluOpType.add), "VE")
                    sec("vector", _)

                    def _(sy, t=t):
                        wt(sy, "VE")
                        rows = y_d[ch * CT + t * 128: ch * CT + (t + 1) * 128, :]
                        wt(sy, "LD")
                        wt(sy, "SP")
                        wt(sy, "OUT")
                        wt(sy, "GP")
                        wt(sy, "TP")
                        wt(sy, "OUT")
                        inc(sy.dma_start(out=rows, in_=xout[:, :]), "OUT")
                    sec("sync", _)

    ffn_phase(f"f1c{ch}", None)
    if _FSTAGE:
        return
    if _LIMIT == "ffn1":
        def _(sy):
            wt(sy, "VE")
            wt(sy, "OUT")
            inc(sy.dma_start(out=dbg_aps["dbg"].rearrange("(t p) j -> p t j", p=128),
                             in_=xsb[:, :].rearrange("p (t j) -> p t j", t=NT)), "OUT")
        sec("sync", _)
        return

    import contextlib
    with contextlib.ExitStack() as es:
        ec = es.enter_context
        gath = ec(nc.sbuf_tensor(f"gath{ch}", [128, 4 * 512], F32))
        u2T = ec(nc.sbuf_tensor(f"u2T{ch}", [128, KD * CT], BF16))
        u2b = ec(nc.sbuf_tensor(f"u2b{ch}", [128, D], BF16))
        qdm = ec(nc.sbuf_tensor(f"qdm{ch}", [128, 8 * CT], BF16))
        kdm = ec(nc.sbuf_tensor(f"kdm{ch}", [128, 8 * CT], BF16))
        ktok = ec(nc.sbuf_tensor(f"ktok{ch}", [128, NT * D], BF16))
        vtok = ec(nc.sbuf_tensor(f"vtok{ch}", [128, NT * D], BF16))
        wfc = ec(nc.sbuf_tensor(f"wfc{ch}", [128, 2 * KD * 128], BF16))
        wv = ec(nc.sbuf_tensor(f"wv{ch}", [128, KD * D], BF16))
        wow = ec(nc.sbuf_tensor(f"wow{ch}", [128, KD * D], BF16))
        scb = ec(nc.sbuf_tensor(f"scb{ch}", [128, 512], BF16))
        stot = ec(nc.sbuf_tensor(f"stot{ch}", [128, NT * 512], BF16))
        sprt = ec(nc.sbuf_tensor(f"sprt{ch}", [128, NT * 512], F32))
        attb = ec(nc.sbuf_tensor(f"attb{ch}", [128, NT * KD * 128], BF16))
        cbufa = ec(nc.sbuf_tensor(f"cbufa{ch}", [128, D], F32))
        def _(sy):
            wt(sy, "VE")
            wt(sy, "AC")
            wt(sy, "PE")
            wt(sy, "LD")
            inc(sy.dma_start(out=rpos[:, :], in_=rpos_d[:, ch * CT:(ch + 1) * CT]), "LD")
        sec("sync", _)

        for t in range(NT):
            xt = xsb[:, t * D:(t + 1) * D]

            def _(a, t=t):
                wt(a, "VE")
                wt(a, "TP")
                a.activation(cbufa[:, :], xt, AF.Square, accum_out=stats[:, 1:2])
                a.activation(u2b[:, :], xt, AF.Identity, accum_out=stats[:, 0:1])
                a.drain()
                a.activation(stats[:, 2:3], stats[:, 0:1], AF.Identity, scale=1.0 / D)
                a.activation(stats[:, 9:10], stats[:, 0:1], AF.Identity, scale=-1.0 / D)
                a.activation(stats[:, 3:4], stats[:, 1:2], AF.Identity,
                             scale=1.0 / D, bias=scrub[:, 1:2])
                a.drain()
                a.activation(stats[:, 8:9], stats[:, 2:3], AF.Square)
                a.drain()
                a.activation(stats[:, 4:5], stats[:, 8:9], AF.Identity,
                             scale=-1.0, bias=stats[:, 3:4])
                a.drain()
                a.activation(stats[:, 5:6], stats[:, 4:5], AF.Ln)
                a.drain()
                a.activation(stats[:, 6:7], stats[:, 5:6], AF.Exp, scale=-0.5)
                a.drain()
                a.activation(stats[:, 7:8], stats[:, 6:7], AF.Identity,
                             scale=stats[:, 9:10])
                a.drain()
                inc(a.activation(u2b[:, :], xt, AF.Identity,
                                 bias=stats[:, 7:8], scale=stats[:, 6:7]), "AC")
            sec("scalar", _)

            def _(sy, t=t):
                wt(sy, "AC")
                wt(sy, "LD")
                wt(sy, "SP")
                wt(sy, "OUT")
                wt(sy, "GP")
                dst = u2T[:, :].rearrange("p (c j) -> p c j", c=KD)[:, :, t * 128:(t + 1) * 128]
                wt(sy, "TP")
                inc(sy.dma_start(out=dst, in_=u2b[:, :], transpose=True), "TP")
                wt(sy, "TP")
            sec("sync", _)

        def _(sy):
            wt(sy, "AC")
            wt(sy, "LD")
            wt(sy, "SP")
            wt(sy, "OUT")
            wt(sy, "GP")
            wt(sy, "TP")
            wt(sy, "LD")
            inc(sy.dma_start(out=wv[:, :].rearrange("p (c j) -> p c j", c=KD),
                             in_=wqkvT_d.rearrange("p (c j) -> p c j", c=KD)[:, :, 2 * D:3 * D]), "LD")
            wt(sy, "LD")
            inc(sy.dma_start(out=wow[:, :], in_=woT_d), "LD")
        sec("sync", _)
        vw_ld = C["LD"]
        tp_u2 = C["TP"]

        with contextlib.ExitStack() as es2:
            ec2 = es2.enter_context
            psqk0 = ec2(nc.psum_tensor(f"psqk{ch}", [128, 512], F32))
            psqk1 = ec2(nc.psum_tensor(f"psqk2{ch}", [128, 512], F32))
            psv = ec2(nc.psum_tensor(f"psv{ch}", [128, 1024], F32))
            psst = ec2(nc.psum_tensor(f"psst{ch}", [128, 512], F32))
            psqk = [psqk0, psqk1]
            for fc in range(16):
                bk = fc % 2
                def _(sy, fc=fc, bk=bk):
                    if fc >= 2:
                        wt(sy, "PE")
                    src = wqkvT_d.rearrange("p (c j) -> p c j", c=KD)[:, :, fc * 128:(fc + 1) * 128]
                    wt(sy, "LD")
                    inc(sy.dma_start(out=wfc[:, bk * KD * 128:(bk + 1) * KD * 128]
                                     .rearrange("p (c j) -> p c j", c=KD), in_=src), "LD")
                sec("sync", _)

                def _(te, fc=fc, bk=bk):
                    wt(te, "LD")
                    wt(te, "TP", tp_u2)
                    wt(te, "AC")
                    for k in range(KD):
                        ins_ = te.matmul(psqk[bk][:, :],
                                         wfc[:, bk * KD * 128 + k * 128: bk * KD * 128 + (k + 1) * 128],
                                         u2T[:, k * CT:(k + 1) * CT],
                                         start=(k == 0), stop=(k == KD - 1))
                    inc(ins_, "PE")
                sec("tensor", _)

                def _(a, fc=fc, bk=bk):
                    wt(a, "PE")
                    if fc < 8:
                        inc(a.activation(qdm[:, fc * CT:(fc + 1) * CT], psqk[bk][:, :],
                                         AF.Identity, scale=0.125), "AC")
                    else:
                        dst = kdm[:, :].rearrange("p (t j) -> p t j", t=NT)[
                            :, :, (fc - 8) * 128:(fc - 7) * 128]
                        inc(a.activation(dst, psqk[bk][:, :], AF.Sign), "AC")
                sec("scalar", _)
            qk_ac = C["AC"]

            for t in range(NT):
                def _(te, t=t):
                    wt(te, "LD", vw_ld)
                    wt(te, "TP", tp_u2)
                    if t > 0:
                        wt(te, "AC")
                    for half in range(2):
                        for k in range(KD):
                            ins_ = te.matmul(psv[:, half * 512:(half + 1) * 512],
                                             u2T[:, k * CT + t * 128:k * CT + (t + 1) * 128],
                                             wv[:, k * D + half * 512:k * D + half * 512 + 512],
                                             start=(k == 0), stop=(k == KD - 1))
                    inc(ins_, "PE")
                sec("tensor", _)

                def _(a, t=t):
                    wt(a, "PE")
                    inc(a.activation(vtok[:, t * D:(t + 1) * D], psv[:, :], AF.Sign), "AC")
                sec("scalar", _)

            def _(sy):
                wt(sy, "AC", qk_ac)
                wt(sy, "LD")
                wt(sy, "SP")
                wt(sy, "OUT")
                wt(sy, "GP")
                for t in range(NT):
                    wt(sy, "TP")
                    inc(sy.dma_start(out=ktok[:, t * D:(t + 1) * D]
                                     .rearrange("p (c j) -> p c j", c=8),
                                     in_=kdm[:, t * D:(t + 1) * D], transpose=True), "TP")
                    wt(sy, "TP")
            sec("sync", _)
            tp_k = C["TP"]

            with contextlib.ExitStack() as esp:
                spre = esp.enter_context(nc.sbuf_tensor(f"spre{ch}", [128, 512], F32))
                def _(v):
                    inc(v.memset(spre[:, :], 0.0), "VE")
                sec("vector", _)

                for t in range(NT):
                    def _(te, t=t):
                        wt(te, "TP", tp_k)
                        wt(te, "AC")
                        wt(te, "VE")
                        for h in range(H):
                            pb = 64 * (h % 2)
                            fs = (h // 2) * 64
                            ins_ = te.matmul(psst[pb:pb + 64, fs:fs + 64],
                                             ktok[:, t * D + h * 64:t * D + (h + 1) * 64],
                                             vtok[:, t * D + h * 64:t * D + (h + 1) * 64],
                                             start=True, stop=True,
                                             skip_group_check=True)
                        inc(ins_, "PE")
                    sec("tensor", _)

                    def _(v, t=t):
                        wt(v, "PE")
                        v.tensor_copy(sprt[:, t * 512:(t + 1) * 512], spre[:, :])
                        v.drain()
                        inc(v.tensor_tensor(spre[:, :], psst[:, :], spre[:, :],
                                            AluOpType.add), "VE")
                    sec("vector", _)

                def _(v):
                    v.drain()
                    inc(v.tensor_copy(ccs[:, :], spre[:, :]), "VE")
                sec("vector", _)

                def _(sy):
                    wt(sy, "VE")
                    wt(sy, "CC")
                    wt(sy, "SP")
                    inc(sy.dma_start(out=cc_in, in_=ccs[:, :]), "SP")
                sec("sync", _)

                def _(g):
                    wt(g, "SP")
                    wt(g, "LD")
                    wt(g, "CC")
                    inc(g.collective_compute(
                        "AllGather", AluOpType.bypass,
                        replica_groups=[[0, 1, 2, 3], [4, 5, 6, 7]],
                        ins=[cc_in], outs=[cc_out]), "CC")
                sec("gpsimd", _)

                def _(sy):
                    wt(sy, "CC")
                    wt(sy, "LD")
                    inc(sy.dma_start(out=gath[:, :].rearrange("p (j f) -> p j f", j=4),
                                     in_=cc_out.rearrange("(j p) f -> p j f", p=128)), "LD")
                sec("sync", _)

                def _(v):
                    wt(v, "LD")
                    v.tensor_copy(sinc[:, :], sbase[:, :])
                    for j in range(4):
                        v.drain()
                        v.scalar_tensor_tensor(sinc[:, :], gath[:, j * 512:(j + 1) * 512],
                                               mska[:, j:j + 1], sinc[:, :],
                                               AluOpType.mult, AluOpType.add)
                    for j in range(4):
                        v.drain()
                        v.scalar_tensor_tensor(sbase[:, :], gath[:, j * 512:(j + 1) * 512],
                                               1.0, sbase[:, :], AluOpType.mult, AluOpType.add)
                    v.drain()
                    for t in range(NT):
                        ins_ = v.tensor_tensor(stot[:, t * 512:(t + 1) * 512],
                                               sprt[:, t * 512:(t + 1) * 512], sinc[:, :],
                                               AluOpType.add)
                    inc(ins_, "VE")
                sec("vector", _)
                s_ready = C["VE"]
                es2.close()
                if _LIMIT == "track":
                    def _(sy):
                        wt(sy, "VE")
                        wt(sy, "OUT")
                        inc(sy.dma_start(out=dbg_aps["dbg"].rearrange("(t p) j -> p t j", p=128),
                                         in_=xsb[:, :].rearrange("p (t j) -> p t j", t=NT)), "OUT")
                        wt(sy, "OUT")
                        inc(sy.dma_start(out=dbg_aps["dbg2"][:, 0:512], in_=sprt[:, 0:512]), "OUT")
                    sec("sync", _)
                    return

                with contextlib.ExitStack() as es3:
                    ec3 = es3.enter_context
                    pssc0 = ec3(nc.psum_tensor(f"pssc{ch}", [128, 512], F32))
                    pssc1 = ec3(nc.psum_tensor(f"pssc2{ch}", [128, 512], F32))
                    pso4 = [ec3(nc.psum_tensor(f"pso{ch}_{i}", [128, 512], F32))
                            for i in range(4)]
                    pssc = [pssc0, pssc1]
                    for t in range(NT):
                        for p in range(2):
                            for q in range(2):
                                def _(te, t=t, p=p, q=q):
                                    wt(te, "VE")
                                    if p == 0 and q == 0:
                                        wt(te, "AC")
                                    for jj in range(4):
                                        j = 4 * q + jj
                                        h = 2 * j + p
                                        pb = 64 * p
                                        fq = j * CT + t * 128
                                        fk = t * D + j * 128
                                        ins_ = te.matmul(pssc[p][:, jj * 128:(jj + 1) * 128],
                                                         kdm[pb:pb + 64, fk:fk + 128],
                                                         qdm[pb:pb + 64, fq:fq + 128],
                                                         start=True, stop=True,
                                                         skip_group_check=True)
                                    inc(ins_, "PE")
                                sec("tensor", _)

                                def _(v, t=t, p=p, q=q):
                                    wt(v, "PE")
                                    inc(v.tensor_tensor(scb[:, :], pssc[p][:, :], triu4[:, :],
                                                        AluOpType.mult), "VE")
                                sec("vector", _)
                                if _ALIMIT == "sc" and t == 0 and p == 0 and q == 0:
                                    def _(g):
                                        wt(g, "VE")
                                        inc(g.dma_start(out=dbg_aps["dbg"][0:128, 0:512], in_=scb[:, :]), "OUT")
                                    sec("gpsimd", _)
                                    return

                                def _(te, t=t, p=p, q=q):
                                    wt(te, "VE")
                                    for jj in range(4):
                                        j = 4 * q + jj
                                        h = 2 * j + p
                                        pb = 64 * p
                                        fq = j * CT + t * 128
                                        te.matmul(pso4[2 * p + q][pb:pb + 64, jj * 128:(jj + 1) * 128],
                                                  vtok[:, t * D + h * 64:t * D + (h + 1) * 64],
                                                  scb[:, jj * 128:(jj + 1) * 128],
                                                  start=True, stop=False,
                                                  skip_group_check=True)
                                        ins_ = te.matmul(pso4[2 * p + q][pb:pb + 64, jj * 128:(jj + 1) * 128],
                                                  stot[pb:pb + 64, t * 512 + j * 64:
                                                       t * 512 + j * 64 + 64],
                                                  qdm[pb:pb + 64, fq:fq + 128],
                                                  start=False, stop=True,
                                                  skip_group_check=True)
                                    inc(ins_, "PE")
                                sec("tensor", _)
                                if _ALIMIT == "ic" and t == 0 and p == 0 and q == 0:
                                    def _(v):
                                        wt(v, "PE")
                                        inc(v.tensor_copy(sprt[0:64, 0:512], pso4[0][0:64, :]), "VE")
                                    sec("vector", _)
                                    def _(g):
                                        wt(g, "VE")
                                        inc(g.dma_start(out=dbg_aps["dbg"][0:64, 0:512], in_=sprt[0:64, 0:512]), "OUT")
                                    sec("gpsimd", _)
                                    return

                        def _(a, t=t):
                            wt(a, "PE")
                            inc(a.nop(), "AC")
                        sec("scalar", _)

                        def _(v, t=t):
                            wt(v, "AC")
                            rp = rpos[:, t * 128:(t + 1) * 128]
                            for p in range(2):
                                for q in range(2):
                                    pb = 64 * p
                                    rpb = rp[pb:pb + 64, :].unsqueeze(1).broadcast_to((64, 4, 128))
                                    dst = attb[pb:pb + 64,
                                               t * KD * 128 + q * 4 * 128:
                                               t * KD * 128 + (q * 4 + 4) * 128]
                                    ins_ = v.tensor_tensor(
                                        dst.rearrange("p (s i) -> p s i", s=4),
                                        pso4[2 * p + q][pb:pb + 64, :].rearrange("p (s i) -> p s i", s=4),
                                        rpb, AluOpType.mult)
                            inc(ins_, "VE")
                        sec("vector", _)
                        def _(a, t=t):
                            wt(a, "VE")
                            inc(a.nop(), "AC")
                        sec("scalar", _)

            if _LIMIT == "main":
                def _(g):
                    wt(g, "VE")
                    wt(g, "OUT")
                    inc(g.dma_start(out=dbg_aps["dbg"].rearrange("(a p) j -> p a j", p=128),
                                    in_=attb[:, :].rearrange("p (a j) -> p a j", a=NT)), "OUT")
                sec("gpsimd", _)
                return

            with contextlib.ExitStack() as esy:
                psy = esy.enter_context(nc.psum_tensor(f"psy{ch}", [128, 1024], F32))
                for t in range(NT):
                    def _(te, t=t):
                        wt(te, "VE")
                        wt(te, "LD", vw_ld)
                        if t > 0:
                            wt(te, "AC")
                        for half in range(2):
                            for k in range(KD):
                                ins_ = te.matmul(psy[:, half * 512:(half + 1) * 512],
                                                 attb[:, (t * KD + k) * 128:(t * KD + k + 1) * 128],
                                                 wow[:, k * D + half * 512:k * D + half * 512 + 512],
                                                 start=(k == 0), stop=(k == KD - 1))
                        inc(ins_, "PE")
                    sec("tensor", _)

                    def _(v, t=t):
                        wt(v, "PE")
                        xt = xsb[:, t * D:(t + 1) * D]
                        inc(v.tensor_tensor(xt, psy[:, :], xt, AluOpType.add), "VE")
                    sec("vector", _)

                    def _(a, t=t):
                        wt(a, "VE")
                        inc(a.nop(), "AC")
                    sec("scalar", _)

    if _LIMIT == "attn":
        def _(sy):
            wt(sy, "VE")
            wt(sy, "OUT")
            inc(sy.dma_start(out=dbg_aps["dbg"].rearrange("(t p) j -> p t j", p=128),
                             in_=xsb[:, :].rearrange("p (t j) -> p t j", t=NT)), "OUT")
        sec("sync", _)
        return
    ffn_phase(f"f2c{ch}", "out")

    if ch + 1 < NCH:
        def _(sy):
            wt(sy, "VE")
            wt(sy, "LD")
            inc(sy.dma_start(
                out=xsb[:, :].rearrange("p (t j) -> p t j", t=NT),
                in_=x_d[(ch + 1) * CT:(ch + 2) * CT, :].rearrange("(t p) j -> p t j", p=128)), "LD")
        sec("sync", _)

        def _(v):
            wt(v, "LD")
            inc(v.nop(), "VE")
        sec("vector", _)



def _spc(v, scrub, src):
    v.tensor_copy(scrub[:, 2:3], src)

_NC_CACHE = None
_NC_KEY = 'x'


def kernel(L0_attn_x, L0_attn_W_qkv, L0_attn_W_o, L0_ffn_w_gate, L0_ffn_w_up,
           L0_ffn_w_down):
    global _NC_CACHE
    x = np.ascontiguousarray(np.asarray(L0_attn_x, dtype=np.float32)).reshape(B * T, D)
    wqkv = np.ascontiguousarray(np.asarray(L0_attn_W_qkv, dtype=np.float32))
    wo = np.ascontiguousarray(np.asarray(L0_attn_W_o, dtype=np.float32))
    wg = np.ascontiguousarray(np.asarray(L0_ffn_w_gate, dtype=np.float32))
    wu = np.ascontiguousarray(np.asarray(L0_ffn_w_up, dtype=np.float32))
    wd = np.ascontiguousarray(np.asarray(L0_ffn_w_down, dtype=np.float32))

    global _NC_KEY
    if _NC_CACHE is None or _NC_KEY != _LIMIT:
        _NC_CACHE = build_nc()
        _NC_KEY = _LIMIT
    nc = _NC_CACHE

    triu = np.tile(np.triu(np.ones((128, 128), dtype=np.float32)), (1, 4))
    wg_pad = np.zeros((3 * 128 * NCORE, D), dtype=np.float32)
    wg_pad[:II] = wg
    wu_pad = np.zeros((3 * 128 * NCORE, D), dtype=np.float32)
    wu_pad[:II] = wu
    in_maps = []
    spans = []
    for c in range(NCORE):
        seq, q = divmod(c, 4)
        starts = [seq * T + ch * (4 * CT) + q * CT for ch in range(NCH)]
        spans.append(starts)
        xo = np.concatenate([x[s:s + CT] for s in starts], axis=0)
        pos = np.concatenate(
            [np.arange(ch * 4 * CT + q * CT + 1, ch * 4 * CT + q * CT + CT + 1)
             for ch in range(NCH)]).astype(np.float32)
        rpos = np.broadcast_to((1.0 / pos)[None, :], (128, NCH * CT)).copy()
        mask = np.broadcast_to((np.arange(4) < q).astype(np.float32)[None, :], (128, 4)).copy()
        in_maps.append({
            "x": xo, "wqkv": wqkv, "wo": wo,
            "wg": wg_pad[c * 384:(c + 1) * 384],
            "wu": wu_pad[c * 384:(c + 1) * 384],
            "wd": wd[c * 128:(c + 1) * 128],
            "rposd": rpos, "triu4": triu, "maska": mask,
        })

    res = run_bass_kernel_spmd(nc, in_maps, core_ids=list(range(NCORE)))
    global _RES
    _RES = res
    out = np.empty((B * T, D), dtype=np.float32)
    for c in range(NCORE):
        yc = res.results[c]["y"]
        for ch in range(NCH):
            out[spans[c][ch]:spans[c][ch] + CT] = yc[ch * CT:(ch + 1) * CT]
    return out.reshape(B, T, D)


# version 03
# speedup vs baseline: 1.1012x; 1.0134x over previous
import numpy as np
import ml_dtypes
import concourse.bacc as bacc
import concourse.mybir as mybir
import concourse.dve_ops as dve_ops
from concourse.dve_op import DveOp
from concourse.dve_spec import (
    Spec, Src0, Src1, C0, C1, C2, Zero, One, maxx, minn, lower, Bin, AluOp,
)
from concourse.alu_op_type import AluOpType
from concourse.bass_utils import run_bass_kernel_spmd

F32 = mybir.dt.float32
BF16 = mybir.dt.bfloat16
AF = mybir.ActivationFunctionType
AX = mybir.AxisListType.X
MAGIC = float(2 ** 23)

def _mk_op(name, spec, reference):
    sha = lower(spec, ver="v3", perf_en=False, name=name).sha("v3")
    return DveOp(name, spec, subdim=False, uops_sha={"v3": sha}, reference=reference)

_t = Bin(AluOp.ABSOLUTE_VALUE, Src0, Src0)
_two = One + One
_w = minn(minn(_t + _t, _t + _two), _t * C0 + C1)
FP4A_SPEC = Spec(body=_w + C2)

def _ref_fp4a(in0, in1, s0, s1, imm2):
    t = np.abs(in0)
    w = np.minimum(np.minimum(t + t, t + 2.0), t * 0.5 + 4.0)
    return (np.ceil(w - 0.5) + imm2).astype(in0.dtype)

_g = Src0 >= C0
SIGNSCALE_SPEC = Spec(body=((_g + _g) - C1) * Src1)

def _ref_signscale(in0, in1, s0, s1, imm2):
    return (np.where(in0 >= 0, 1.0, -1.0) * in1).astype(in0.dtype)

_p1 = Src0 + Src0
_p2 = _p1 + _p1
FP4B_SPEC = Spec(body=maxx(maxx(Src0 - C2, _p1 - C0), _p2 - C1) * Src1)

def _ref_fp4b(in0, in1, s0, s1, imm2):
    i = in0 - imm2
    q2 = np.maximum(np.maximum(i, i + i - 4.0), 4.0 * i - 16.0)
    return (q2 * in1).astype(in0.dtype)

FP4A = _mk_op("ANT_FP4A", FP4A_SPEC, _ref_fp4a)
SIGNSCALE = _mk_op("ANT_SIGNSCALE", SIGNSCALE_SPEC, _ref_signscale)
FP4B = _mk_op("ANT_FP4B", FP4B_SPEC, _ref_fp4b)

def _register_ops():
    names = [o.name for o in dve_ops.OPS]
    if "ANT_FP4A" in names:
        return
    i0 = names.index("DVEHASH32L_SEED")
    for j, op in enumerate([FP4A, SIGNSCALE, FP4B]):
        old = dve_ops.OPS[i0 + j]
        dve_ops.OPS[i0 + j] = op
        dve_ops._SUB_OPCODE_FOR_NAME[op.name] = dve_ops._SUB_OPCODE_FOR_NAME[old.name]
        dve_ops.CUSTOM_DVE_REFS[op.name] = op.reference
        dve_ops.CUSTOM_DVE_SPECS[op.name] = op.spec

_register_ops()

B, T, D = 2, 8192, 1024
II = 2816
H, DH = 16, 64
NCH = 4
CT = 512
NT = CT // 128
NCORE = 8
EPS = 1e-5
KD = D // 128
KI = II // 128
OC = [(0, 512), (512, 512), (1024, 512), (1536, 512), (2048, 512), (2560, 256)]

_DEBUG = False
_LIMIT = None
_FSTAGE = None
_ALIMIT = None


def build_nc():
    nc = bacc.Bacc("TRN2", target_bir_lowering=False, debug=False)

    x_d = nc.dram_tensor("x", [NCH * CT, D], F32, kind="ExternalInput").ap()
    wqkv_d = nc.dram_tensor("wqkv", [3 * D, D], F32, kind="ExternalInput").ap()
    wo_d = nc.dram_tensor("wo", [D, D], F32, kind="ExternalInput").ap()
    wg_d = nc.dram_tensor("wg", [3 * 128, D], F32, kind="ExternalInput").ap()
    wu_d = nc.dram_tensor("wu", [3 * 128, D], F32, kind="ExternalInput").ap()
    wd_d = nc.dram_tensor("wd", [128, II], F32, kind="ExternalInput").ap()
    rpos_d = nc.dram_tensor("rposd", [128, NCH * CT], F32, kind="ExternalInput").ap()
    triu_d = nc.dram_tensor("triu4", [128, 512], F32, kind="ExternalInput").ap()
    mska_d = nc.dram_tensor("maska", [128, 4], F32, kind="ExternalInput").ap()
    y_d = nc.dram_tensor("y", [NCH * CT, D], F32, kind="ExternalOutput").ap()

    wgT_d = nc.dram_tensor("wgT", [128, KD * II], BF16, kind="Internal").ap()
    wuT_d = nc.dram_tensor("wuT", [128, KD * II], BF16, kind="Internal").ap()
    wdT_d = nc.dram_tensor("wdT", [128, KI * D], BF16, kind="Internal").ap()
    wqkvT_d = nc.dram_tensor("wqkvT", [128, KD * 3 * D], BF16, kind="Internal").ap()
    woT_d = nc.dram_tensor("woT", [128, KD * D], BF16, kind="Internal").ap()
    cc_in = nc.dram_tensor("cc_in", [128, 512], F32, kind="Internal").ap()
    cc_out = nc.dram_tensor("cc_out", [4 * 128, 512], F32, kind="Internal").ap()
    QIN = 3 * KD * 128 * 2 + KI * 128
    wq_in = nc.dram_tensor("wq_in", [128, QIN], BF16, kind="Internal").ap()
    wq_out = nc.dram_tensor("wq_out", [8 * 128, QIN], BF16, kind="Internal").ap()

    dbg_aps = {}
    if _LIMIT in ("ffn1", "track", "attn", "main") or _FSTAGE or _ALIMIT:
        dbg_aps["dbg"] = nc.dram_tensor("dbg", [CT, D], F32, kind="ExternalOutput").ap()
        dbg_aps["dbg2"] = nc.dram_tensor("dbg2", [128, 4 * 512], F32, kind="ExternalOutput").ap()

    C = {}
    SEM = {}

    import contextlib
    st = contextlib.ExitStack()

    def sem(name):
        SEM[name] = st.enter_context(nc.semaphore(name))
        C[name] = 0

    for s in ["LD", "GP", "TP", "SP", "VE", "AC", "PE", "CC", "OUT"]:
        sem(s)

    def inc(instr, s, k=None):
        k = k if k is not None else (16 if s in ("LD", "GP", "TP", "SP", "OUT") else 1)
        instr.then_inc(SEM[s], k)
        C[s] += k

    def wt(eng, s, v=None):
        eng.wait_ge(SEM[s], C[s] if v is None else v)

    with st:
        xsb = st.enter_context(nc.sbuf_tensor("xsb", [128, NT * D], F32))
        xout = st.enter_context(nc.sbuf_tensor("xout", [128, D], F32))
        rpos = st.enter_context(nc.sbuf_tensor("rposc", [128, CT], F32))
        triu4 = st.enter_context(nc.sbuf_tensor("triu4s", [128, 512], F32))
        mska = st.enter_context(nc.sbuf_tensor("mskas", [128, 4], F32))
        sbase = st.enter_context(nc.sbuf_tensor("sbase", [128, 512], F32))
        sinc = st.enter_context(nc.sbuf_tensor("sinc", [128, 512], F32))
        ccs = st.enter_context(nc.sbuf_tensor("ccs", [128, 512], F32))
        stats = st.enter_context(nc.sbuf_tensor("stats", [128, 16], F32))
        scrub = st.enter_context(nc.sbuf_tensor("scrub", [128, 4], F32))
        block = st.enter_context(nc.Block())
        if True:
            def sec(engname, fn):
                getattr(block, engname)(fn)

            import contextlib as _ctx
            with _ctx.ExitStack() as es0:
                ec0 = es0.enter_context
                wf32 = ec0(nc.sbuf_tensor("wf32", [128, II], F32))
                wz = ec0(nc.sbuf_tensor("wz", [128, II], F32))
                wsbf = ec0(nc.sbuf_tensor("wsbf", [128, II], F32))
                wq = ec0(nc.sbuf_tensor("wq", [128, II], BF16))
                wtst = ec0(nc.sbuf_tensor("wtst", [128, II], BF16))
                wmb = ec0(nc.sbuf_tensor("wmb", [128, II // 16], F32))
                wrs = ec0(nc.sbuf_tensor("wrs", [128, II // 16], F32))
                wsb2 = ec0(nc.sbuf_tensor("wsb2", [128, II // 16], F32))
                def _(sy):
                    wt(sy, "LD")
                    wt(sy, "SP")
                    wt(sy, "OUT")
                    wt(sy, "GP")
                    wt(sy, "TP")
                    wt(sy, "LD")
                    inc(sy.dma_start(out=triu4[:, :], in_=triu_d), "LD")
                    wt(sy, "LD")
                    inc(sy.dma_start(out=mska[:, :], in_=mska_d), "LD")
                    wt(sy, "LD")
                    inc(sy.dma_start(
                        out=xsb[:, :].rearrange("p (t j) -> p t j", t=NT),
                        in_=x_d[0:CT, :].rearrange("(t p) j -> p t j", p=128)), "LD")
                sec("sync", _)

                def _(v):
                    v.memset(sbase[:, :], 0.0)
                    inc(v.memset(scrub[:, 1:2], EPS), "VE")
                sec("vector", _)

                def quant_weight(src_d, off, nrows, K):
                    npc = nrows // 128
                    kc = K // 128
                    NB = K // 16
                    for pc in range(npc):
                        def _(sy, pc=pc):
                            wt(sy, "VE")
                            wt(sy, "LD")
                            inc(sy.dma_start(out=wf32[:, 0:K],
                                             in_=src_d[pc * 128:(pc + 1) * 128, :]), "LD")
                        sec("sync", _)

                        def _(v, pc=pc, K=K, NB=NB):
                            wt(v, "LD")
                            wt(v, "TP")
                            rsb = wrs[:, 0:NB].unsqueeze(2).broadcast_to((128, NB, 16))
                            sbb = wsb2[:, 0:NB].unsqueeze(2).broadcast_to((128, NB, 16))
                            v.tensor_reduce(wmb[:, 0:NB],
                                            wf32[:, 0:K].rearrange("p (a b) -> p a b", b=16),
                                            AX, AluOpType.max, apply_absolute_value=True)
                            v.drain()
                            v.tensor_scalar(wmb[:, 0:NB], wmb[:, 0:NB], 6e-12, None, AluOpType.max)
                            v.drain()
                            v.reciprocal(wrs[:, 0:NB], wmb[:, 0:NB])
                            v.tensor_scalar(wsb2[:, 0:NB], wmb[:, 0:NB], 1.0 / 12.0, None, AluOpType.mult)
                            v.drain()
                            v.tensor_scalar(wrs[:, 0:NB], wrs[:, 0:NB], 6.0, None, AluOpType.mult)
                            v.tensor_copy(wsbf[:, 0:K].rearrange("p (a b) -> p a b", b=16), sbb)
                            v.drain()
                            v.tensor_tensor(wz[:, 0:K].rearrange("p (a b) -> p a b", b=16),
                                            wf32[:, 0:K].rearrange("p (a b) -> p a b", b=16),
                                            rsb, AluOpType.mult)
                            v._custom_dve(SIGNSCALE, out=wsbf[:, 0:K], in0=wf32[:, 0:K],
                                          in1=wsbf[:, 0:K], s0=0.0, s1=1.0)
                            v.drain()
                            v._custom_dve(FP4A, out=wz[:, 0:K], in0=wz[:, 0:K],
                                          s0=0.5, s1=4.0, imm2=MAGIC)
                            v.drain()
                            inc(v._custom_dve(FP4B, out=wq[:, 0:K], in0=wz[:, 0:K],
                                              in1=wsbf[:, 0:K], s0=2 * MAGIC + 4.0,
                                              s1=4 * MAGIC + 16.0, imm2=MAGIC), "VE")
                        sec("vector", _)

                        def _(sy, pc=pc, K=K, kc=kc, off=off):
                            wt(sy, "VE")
                            wt(sy, "LD")
                            wt(sy, "SP")
                            wt(sy, "OUT")
                            wt(sy, "GP")
                            wt(sy, "TP")
                            inc(sy.dma_start(
                                out=wtst[:, 0:K].rearrange("p (c j) -> p c j", c=kc),
                                in_=wq[:, 0:K], transpose=True), "TP")
                            wt(sy, "TP")
                            wt(sy, "SP")
                            inc(sy.dma_start(
                                out=wq_in[:, off + pc * K:off + (pc + 1) * K],
                                in_=wtst[:, 0:K]), "SP")
                        sec("sync", _)

                quant_weight(wg_d, 0, 3 * 128, D)
                quant_weight(wu_d, 3 * KD * 128, 3 * 128, D)
                quant_weight(wd_d, 6 * KD * 128, 128, II)

                def cast_weight(src_d, dst_d, nrows):
                    npc = nrows // 128
                    for pc in range(npc):
                        def _(g, pc=pc):
                            wt(g, "TP")
                            wt(g, "GP")
                            inc(g.dma_start(out=wq[:, 0:D],
                                            in_=src_d[pc * 128:(pc + 1) * 128, :]), "GP")
                        sec("gpsimd", _)

                        def _(sy, pc=pc, nrows=nrows):
                            wt(sy, "GP")
                            wt(sy, "LD")
                            wt(sy, "SP")
                            wt(sy, "OUT")
                            wt(sy, "TP")
                            inc(sy.dma_start(
                                out=wtst[:, 0:D].rearrange("p (c j) -> p c j", c=KD),
                                in_=wq[:, 0:D], transpose=True), "TP")
                            wt(sy, "TP")
                            dst = dst_d.rearrange("p (c r) -> p c r", c=KD)[:, :, pc * 128:(pc + 1) * 128]
                            wt(sy, "SP")
                            inc(sy.dma_start(out=dst,
                                             in_=wtst[:, 0:D].rearrange("p (c j) -> p c j", c=KD)), "SP")
                        sec("sync", _)

                def _(g):
                    wt(g, "SP")
                    wt(g, "CC")
                    inc(g.collective_compute(
                        "AllGather", AluOpType.bypass,
                        replica_groups=[[0, 1, 2, 3, 4, 5, 6, 7]],
                        ins=[wq_in], outs=[wq_out]), "CC")
                sec("gpsimd", _)

                cast_weight(wqkv_d, wqkvT_d, 3 * D)
                cast_weight(wo_d, woT_d, D)

                def _(sy):
                    wt(sy, "CC")
                    k = 0
                    for dst_d2, off, kc2, nt, spc in (
                            (wgT_d, 0, KD, 22, 3),
                            (wuT_d, 3 * KD * 128, KD, 22, 3),
                            (wdT_d, 6 * KD * 128, KI, 8, 1)):
                        for t in range(nt):
                            j, s = divmod(t, spc)
                            src = wq_out[j * 128:(j + 1) * 128,
                                         off + s * kc2 * 128:
                                         off + (s + 1) * kc2 * 128].rearrange(
                                "p (c r) -> p c r", c=kc2)
                            dst = dst_d2.rearrange(
                                "p (c r) -> p c r", c=kc2)[:, :, t * 128:(t + 1) * 128]
                            ch_sem = "SP" if k % 2 == 0 else "LD"
                            k += 1
                            wt(sy, ch_sem)
                            inc(sy.dma_start(out=dst, in_=src), ch_sem)
                sec("sync", _)

            nch = 0 if _LIMIT == "setup" else (1 if _LIMIT in ("ffn1", "track", "attn", "chunk1", "main") else NCH)
            for ch in range(nch):
                chunk(nc, block, sec, C, SEM, inc, wt, ch,
                      x_d, y_d, wgT_d, wuT_d, wdT_d, wqkvT_d, woT_d,
                      cc_in, cc_out, rpos_d,
                      xsb, xout, rpos, triu4, mska, sbase, sinc, ccs,
                      stats, scrub, dbg_aps)

            def _(sy):
                wt(sy, "OUT")
            sec("sync", _)

    nc.finalize()
    return nc


def chunk(nc, block, sec, C, SEM, inc, wt, ch,
          x_d, y_d, wgT_d, wuT_d, wdT_d, wqkvT_d, woT_d, cc_in, cc_out, rpos_d,
          xsb, xout, rpos, triu4, mska, sbase, sinc, ccs, stats, scrub,
          dbg_aps):
    F32 = mybir.dt.float32
    BF16 = mybir.dt.bfloat16

    def ffn_phase(tag, resid_scale_out):
        import contextlib
        with contextlib.ExitStack() as es:
            ec = es.enter_context
            WgT = ec(nc.sbuf_tensor(f"WgT{tag}", [128, KD * II], BF16))
            WuT = ec(nc.sbuf_tensor(f"WuT{tag}", [128, KD * II], BF16))
            WdT = ec(nc.sbuf_tensor(f"WdT{tag}", [128, KI * D], BF16))
            cbuf = ec(nc.sbuf_tensor(f"cbuf{tag}", [128, D], F32))
            zbuf = ec(nc.sbuf_tensor(f"zbuf{tag}", [128, D], F32))
            sbf = ec(nc.sbuf_tensor(f"sbf{tag}", [128, D], F32))
            xq = ec(nc.sbuf_tensor(f"xq{tag}", [128, D], BF16))
            xqT = ec(nc.sbuf_tensor(f"xqT{tag}", [128, KD * 128], BF16))
            sgb = ec(nc.sbuf_tensor(f"sg{tag}", [128, 512], F32))
            hb = ec(nc.sbuf_tensor(f"hb{tag}", [128, 512], F32))
            zh = ec(nc.sbuf_tensor(f"zh{tag}", [128, 512], F32))
            hq = ec(nc.sbuf_tensor(f"hq{tag}", [128, II], BF16))
            hqT = ec(nc.sbuf_tensor(f"hqT{tag}", [128, KI * 128], BF16))
            mb = ec(nc.sbuf_tensor(f"mb{tag}", [128, 64], F32))
            rs = ec(nc.sbuf_tensor(f"rs{tag}", [128, 64], F32))
            sb2 = ec(nc.sbuf_tensor(f"sb2{tag}", [128, 64], F32))
            psg0 = ec(nc.psum_tensor(f"psg{tag}", [128, 512], F32))
            psg1 = ec(nc.psum_tensor(f"psg2{tag}", [128, 512], F32))
            psu0 = ec(nc.psum_tensor(f"psu{tag}", [128, 512], F32))
            psu1 = ec(nc.psum_tensor(f"psu2{tag}", [128, 512], F32))
            psd = ec(nc.psum_tensor(f"psd{tag}", [128, 1024], F32))
            psg = [psg0, psg1]
            psu = [psu0, psu1]

            def _(sy):
                wt(sy, "VE")
                wt(sy, "AC")
                wt(sy, "PE")
                wt(sy, "SP")
                wt(sy, "LD")
                inc(sy.dma_start(out=WgT[:, :], in_=wgT_d), "LD")
                wt(sy, "LD")
                inc(sy.dma_start(out=WuT[:, :], in_=wuT_d), "LD")
                wt(sy, "LD")
                inc(sy.dma_start(out=WdT[:, :], in_=wdT_d), "LD")
            sec("sync", _)
            w_ld = C["LD"]

            fstage = _FSTAGE if tag.startswith("f1c0") else None
            ntt = 1 if fstage else NT
            for t in range(ntt):
                xt = xsb[:, t * D:(t + 1) * D]

                def _(a, t=t):
                    wt(a, "VE")
                    if t == 0:
                        wt(a, "LD")
                    a.activation(zbuf[:, :], xt, AF.Square, accum_out=stats[:, 1:2])
                    a.activation(cbuf[:, :], xt, AF.Identity, accum_out=stats[:, 0:1])
                    a.drain()
                    a.activation(stats[:, 2:3], stats[:, 0:1], AF.Identity, scale=1.0 / D)
                    a.activation(stats[:, 3:4], stats[:, 1:2], AF.Identity,
                                 scale=1.0 / D, bias=scrub[:, 1:2])
                    a.drain()
                    a.activation(stats[:, 8:9], stats[:, 2:3], AF.Square)
                    a.drain()
                    a.activation(stats[:, 4:5], stats[:, 8:9], AF.Identity,
                                 scale=-1.0, bias=stats[:, 3:4])
                    a.drain()
                    a.activation(stats[:, 5:6], stats[:, 4:5], AF.Ln)
                    a.drain()
                    inc(a.activation(stats[:, 6:7], stats[:, 5:6], AF.Exp, scale=-0.5), "AC")
                sec("scalar", _)
                if fstage == "ln":
                    def _(sy):
                        wt(sy, "AC")
                        wt(sy, "OUT")
                        inc(sy.dma_start(out=dbg_aps["dbg2"][:, 0:16], in_=stats[:, :]), "OUT")
                    sec("sync", _)
                    return

                def _(v, t=t):
                    wt(v, "AC")
                    rsb = rs[:, :].unsqueeze(2).broadcast_to((128, 64, 16))
                    sbb = sb2[:, :].unsqueeze(2).broadcast_to((128, 64, 16))
                    v.tensor_scalar(cbuf[:, :], xt, stats[:, 2:3], None, AluOpType.subtract)
                    v.drain()
                    v.tensor_reduce(mb[:, :], cbuf[:, :].rearrange("p (a b) -> p a b", b=16),
                                    AX, AluOpType.max, apply_absolute_value=True)
                    v.drain()
                    v.tensor_scalar(mb[:, :], mb[:, :], 6e-12, None, AluOpType.max)
                    v.drain()
                    v.reciprocal(rs[:, :], mb[:, :])
                    v.tensor_scalar(sb2[:, :], mb[:, :], stats[:, 6:7], 1.0 / 12.0,
                                    AluOpType.mult, AluOpType.mult)
                    v.drain()
                    v.tensor_scalar(rs[:, :], rs[:, :], 6.0, None, AluOpType.mult)
                    v.tensor_copy(sbf[:, :].rearrange("p (a b) -> p a b", b=16), sbb)
                    v.drain()
                    v.tensor_tensor(zbuf[:, :].rearrange("p (a b) -> p a b", b=16),
                                    cbuf[:, :].rearrange("p (a b) -> p a b", b=16),
                                    rsb, AluOpType.mult)
                    v._custom_dve(SIGNSCALE, out=cbuf[:, :], in0=cbuf[:, :], in1=sbf[:, :],
                                  s0=0.0, s1=1.0)
                    v.drain()
                    v._custom_dve(FP4A, out=zbuf[:, :], in0=zbuf[:, :], s0=0.5, s1=4.0, imm2=MAGIC)
                    v.drain()
                    inc(v._custom_dve(FP4B, out=xq[:, :], in0=zbuf[:, :], in1=cbuf[:, :],
                                      s0=2 * MAGIC + 4.0, s1=4 * MAGIC + 16.0, imm2=MAGIC), "VE")
                sec("vector", _)

                def _(sy, t=t):
                    wt(sy, "VE")
                    wt(sy, "LD")
                    wt(sy, "SP")
                    wt(sy, "OUT")
                    wt(sy, "GP")
                    wt(sy, "TP")
                    inc(sy.dma_start(out=xqT[:, :].rearrange("p (c j) -> p c j", c=KD),
                                     in_=xq[:, :], transpose=True), "TP")
                    wt(sy, "TP")
                sec("sync", _)
                tp_xq = C["TP"]
                if fstage == "quant":
                    def _(v):
                        wt(v, "TP")
                        inc(v.tensor_copy(sbf[:, :], xq[:, :]), "VE")
                    sec("vector", _)
                    def _(sy):
                        wt(sy, "VE")
                        wt(sy, "OUT")
                        inc(sy.dma_start(out=dbg_aps["dbg"][0:128, :], in_=sbf[:, :]), "OUT")
                        wt(sy, "OUT")
                        inc(sy.dma_start(out=dbg_aps["dbg2"][:, 0:64], in_=mb[:, :]), "OUT")
                    sec("sync", _)
                    return

                if fstage == "wload":
                    def _(a, t=t):
                        wt(a, "LD", w_ld)
                        wt(a, "TP", tp_xq)
                        inc(a.activation(sgb[:, :], WgT[:, 0:512], AF.Identity), "AC")
                    sec("scalar", _)
                    def _(v, t=t):
                        wt(v, "TP", tp_xq)
                        inc(v.tensor_copy(hb[:, 0:128], xqT[:, 0:128]), "VE")
                    sec("vector", _)
                    def _(sy):
                        wt(sy, "AC")
                        wt(sy, "VE")
                        wt(sy, "OUT")
                        inc(sy.dma_start(out=dbg_aps["dbg"][0:128, 0:512], in_=sgb[:, :]), "OUT")
                        wt(sy, "OUT")
                        inc(sy.dma_start(out=dbg_aps["dbg"][128:256, 0:128], in_=hb[:, 0:128]), "OUT")
                    sec("sync", _)
                    return
                if fstage in ("mm", "mm1", "mmc"):
                    if fstage == "mmc":
                        def _(v, t=t):
                            wt(v, "LD", w_ld)
                            inc(v.tensor_copy(hq[:, 0:512], WgT[:, 0:512]), "VE")
                        sec("vector", _)
                    def _(te, t=t):
                        wt(te, "TP", tp_xq)
                        wt(te, "LD", w_ld)
                        wt(te, "VE")
                        nk = 1 if fstage == "mm1" else KD
                        for k in range(nk):
                            rhs = hq[:, 0:512] if fstage == "mmc" else WgT[:, k * II: k * II + 512]
                            ins_ = te.matmul(psg[0][:, :], xqT[:, k * 128:(k + 1) * 128], rhs,
                                             start=(k == 0), stop=(k == nk - 1))
                        inc(ins_, "PE")
                    sec("tensor", _)
                    def _(a, t=t):
                        wt(a, "PE")
                        inc(a.activation(sgb[:, :], psg[0][:, :], AF.Identity), "AC")
                    sec("scalar", _)
                    def _(sy):
                        wt(sy, "AC")
                        wt(sy, "OUT")
                        inc(sy.dma_start(out=dbg_aps["dbg"][0:128, 0:512], in_=sgb[:, :]), "OUT")
                    sec("sync", _)
                    return

                for oi, (o0, ow) in enumerate(OC):
                    bk = oi % 2
                    nb = ow // 16
                    def _(te, t=t, oi=oi, o0=o0, ow=ow, bk=bk):
                        wt(te, "TP", tp_xq)
                        wt(te, "LD", w_ld)
                        wt(te, "VE")
                        wt(te, "AC")
                        for k in range(KD):
                            ins_ = te.matmul(psg[bk][:, 0:ow], xqT[:, k * 128:(k + 1) * 128],
                                             WgT[:, k * II + o0:k * II + o0 + ow],
                                             start=(k == 0), stop=(k == KD - 1))
                        inc(ins_, "PE")
                        for k in range(KD):
                            ins_ = te.matmul(psu[bk][:, 0:ow], xqT[:, k * 128:(k + 1) * 128],
                                             WuT[:, k * II + o0:k * II + o0 + ow],
                                             start=(k == 0), stop=(k == KD - 1))
                        inc(ins_, "PE")
                    sec("tensor", _)

                    def _(a, t=t, oi=oi, o0=o0, ow=ow, bk=bk):
                        wt(a, "PE")
                        inc(a.activation(sgb[:, 0:ow], psg[bk][:, 0:ow], AF.Silu), "AC")
                    sec("scalar", _)

                    def _(v, t=t, oi=oi, o0=o0, ow=ow, bk=bk, nb=nb):
                        wt(v, "AC")
                        rsb = rs[:, 0:nb].unsqueeze(2).broadcast_to((128, nb, 16))
                        sbb = sb2[:, 0:nb].unsqueeze(2).broadcast_to((128, nb, 16))
                        v.tensor_tensor(hb[:, 0:ow], psu[bk][:, 0:ow], sgb[:, 0:ow], AluOpType.mult)
                        v.drain()
                        if fstage == "gu0" and oi == 0:
                            inc(v.tensor_copy(zh[:, 0:ow], hb[:, 0:ow]), "VE")
                            return
                        v.tensor_reduce(mb[:, 0:nb], hb[:, 0:ow].rearrange("p (a b) -> p a b", b=16),
                                        AX, AluOpType.max, apply_absolute_value=True)
                        v.drain()
                        v.tensor_scalar(mb[:, 0:64], mb[:, 0:64], 6e-12, None, AluOpType.max)
                        v.drain()
                        v.reciprocal(rs[:, 0:64], mb[:, 0:64])
                        v.tensor_scalar(sb2[:, 0:64], mb[:, 0:64], 1.0 / 12.0, None, AluOpType.mult)
                        v.drain()
                        v.tensor_scalar(rs[:, 0:64], rs[:, 0:64], 6.0, None, AluOpType.mult)
                        v.tensor_copy(sbf[:, 0:ow].rearrange("p (a b) -> p a b", b=16), sbb)
                        v.drain()
                        v.tensor_tensor(zh[:, 0:ow].rearrange("p (a b) -> p a b", b=16),
                                        hb[:, 0:ow].rearrange("p (a b) -> p a b", b=16),
                                        rsb, AluOpType.mult)
                        v._custom_dve(SIGNSCALE, out=hb[:, 0:ow], in0=hb[:, 0:ow],
                                      in1=sbf[:, 0:ow], s0=0.0, s1=1.0)
                        v.drain()
                        v._custom_dve(FP4A, out=zh[:, 0:ow], in0=zh[:, 0:ow],
                                      s0=0.5, s1=4.0, imm2=MAGIC)
                        v.drain()
                        inc(v._custom_dve(FP4B, out=hq[:, o0:o0 + ow], in0=zh[:, 0:ow],
                                          in1=hb[:, 0:ow], s0=2 * MAGIC + 4.0,
                                          s1=4 * MAGIC + 16.0, imm2=MAGIC), "VE")
                    sec("vector", _)
                    if fstage == "gu0" and oi == 0:
                        def _(sy):
                            wt(sy, "VE")
                            wt(sy, "OUT")
                            inc(sy.dma_start(out=dbg_aps["dbg"][0:128, 0:512], in_=zh[:, 0:512]), "OUT")
                            wt(sy, "OUT")
                            inc(sy.dma_start(out=dbg_aps["dbg"][128:256, 0:512], in_=sgb[:, 0:512]), "OUT")
                        sec("sync", _)
                        return
                    if fstage == "gu" and oi == 0:
                        def _(v):
                            inc(v.tensor_copy(zh[:, :], hq[:, 0:512]), "VE")
                        sec("vector", _)
                        def _(sy):
                            wt(sy, "VE")
                            wt(sy, "OUT")
                            inc(sy.dma_start(out=dbg_aps["dbg"][0:128, 0:512], in_=zh[:, :]), "OUT")
                            wt(sy, "OUT")
                            inc(sy.dma_start(out=dbg_aps["dbg"][128:256, 0:512], in_=sgb[:, :]), "OUT")
                        sec("sync", _)
                        return

                def _(sy, t=t):
                    wt(sy, "VE")
                    wt(sy, "LD")
                    wt(sy, "SP")
                    wt(sy, "OUT")
                    wt(sy, "GP")
                    wt(sy, "TP")
                    inc(sy.dma_start(out=hqT[:, :].rearrange("p (c j) -> p c j", c=KI),
                                     in_=hq[:, :], transpose=True), "TP")
                    wt(sy, "TP")
                sec("sync", _)
                tp_hq = C["TP"]

                def _(te, t=t):
                    wt(te, "TP", tp_hq)
                    wt(te, "VE")
                    for half in range(2):
                        for k in range(KI):
                            ins_ = te.matmul(psd[:, half * 512:(half + 1) * 512],
                                             hqT[:, k * 128:(k + 1) * 128],
                                             WdT[:, k * D + half * 512:k * D + half * 512 + 512],
                                             start=(k == 0), stop=(k == KI - 1))
                    inc(ins_, "PE")
                sec("tensor", _)

                if resid_scale_out is None:
                    def _(v, t=t):
                        wt(v, "PE")
                        inc(v.scalar_tensor_tensor(xt, psd[:, 0:D], 0.5, xt,
                                                   AluOpType.mult, AluOpType.add), "VE")
                    sec("vector", _)
                    if fstage == "down":
                        def _(sy):
                            wt(sy, "VE")
                            wt(sy, "OUT")
                            inc(sy.dma_start(out=dbg_aps["dbg"][0:128, :], in_=xt), "OUT")
                        sec("sync", _)
                        return
                else:
                    def _(v, t=t):
                        wt(v, "PE")
                        wt(v, "OUT")
                        inc(v.scalar_tensor_tensor(xout[:, :], psd[:, 0:D], 0.5, xt,
                                                   AluOpType.mult, AluOpType.add), "VE")
                    sec("vector", _)

                    def _(sy, t=t):
                        wt(sy, "VE")
                        rows = y_d[ch * CT + t * 128: ch * CT + (t + 1) * 128, :]
                        wt(sy, "LD")
                        wt(sy, "SP")
                        wt(sy, "OUT")
                        wt(sy, "GP")
                        wt(sy, "TP")
                        wt(sy, "OUT")
                        inc(sy.dma_start(out=rows, in_=xout[:, :]), "OUT")
                    sec("sync", _)

    ffn_phase(f"f1c{ch}", None)
    if _FSTAGE:
        return
    if _LIMIT == "ffn1":
        def _(sy):
            wt(sy, "VE")
            wt(sy, "OUT")
            inc(sy.dma_start(out=dbg_aps["dbg"].rearrange("(t p) j -> p t j", p=128),
                             in_=xsb[:, :].rearrange("p (t j) -> p t j", t=NT)), "OUT")
        sec("sync", _)
        return

    import contextlib
    with contextlib.ExitStack() as es:
        ec = es.enter_context
        gath = ec(nc.sbuf_tensor(f"gath{ch}", [128, 4 * 512], F32))
        u2T = ec(nc.sbuf_tensor(f"u2T{ch}", [128, KD * CT], BF16))
        u2b = ec(nc.sbuf_tensor(f"u2b{ch}", [128, D], BF16))
        qdm = ec(nc.sbuf_tensor(f"qdm{ch}", [128, 8 * CT], BF16))
        kdm = ec(nc.sbuf_tensor(f"kdm{ch}", [128, 8 * CT], BF16))
        ktok = ec(nc.sbuf_tensor(f"ktok{ch}", [128, NT * D], BF16))
        vtok = ec(nc.sbuf_tensor(f"vtok{ch}", [128, NT * D], BF16))
        wfc = ec(nc.sbuf_tensor(f"wfc{ch}", [128, 2 * KD * 128], BF16))
        wv = ec(nc.sbuf_tensor(f"wv{ch}", [128, KD * D], BF16))
        wow = ec(nc.sbuf_tensor(f"wow{ch}", [128, KD * D], BF16))
        scb = ec(nc.sbuf_tensor(f"scb{ch}", [128, 512], BF16))
        stot = ec(nc.sbuf_tensor(f"stot{ch}", [128, NT * 512], BF16))
        sprt = ec(nc.sbuf_tensor(f"sprt{ch}", [128, NT * 512], F32))
        attb = ec(nc.sbuf_tensor(f"attb{ch}", [128, NT * KD * 128], BF16))
        cbufa = ec(nc.sbuf_tensor(f"cbufa{ch}", [128, D], F32))
        def _(sy):
            wt(sy, "VE")
            wt(sy, "AC")
            wt(sy, "PE")
            wt(sy, "LD")
            inc(sy.dma_start(out=rpos[:, :], in_=rpos_d[:, ch * CT:(ch + 1) * CT]), "LD")
        sec("sync", _)

        for t in range(NT):
            xt = xsb[:, t * D:(t + 1) * D]

            def _(a, t=t):
                wt(a, "VE")
                wt(a, "TP")
                a.activation(cbufa[:, :], xt, AF.Square, accum_out=stats[:, 1:2])
                a.activation(u2b[:, :], xt, AF.Identity, accum_out=stats[:, 0:1])
                a.drain()
                a.activation(stats[:, 2:3], stats[:, 0:1], AF.Identity, scale=1.0 / D)
                a.activation(stats[:, 9:10], stats[:, 0:1], AF.Identity, scale=-1.0 / D)
                a.activation(stats[:, 3:4], stats[:, 1:2], AF.Identity,
                             scale=1.0 / D, bias=scrub[:, 1:2])
                a.drain()
                a.activation(stats[:, 8:9], stats[:, 2:3], AF.Square)
                a.drain()
                a.activation(stats[:, 4:5], stats[:, 8:9], AF.Identity,
                             scale=-1.0, bias=stats[:, 3:4])
                a.drain()
                a.activation(stats[:, 5:6], stats[:, 4:5], AF.Ln)
                a.drain()
                a.activation(stats[:, 6:7], stats[:, 5:6], AF.Exp, scale=-0.5)
                a.drain()
                a.activation(stats[:, 7:8], stats[:, 6:7], AF.Identity,
                             scale=stats[:, 9:10])
                a.drain()
                inc(a.activation(u2b[:, :], xt, AF.Identity,
                                 bias=stats[:, 7:8], scale=stats[:, 6:7]), "AC")
            sec("scalar", _)

            def _(sy, t=t):
                wt(sy, "AC")
                wt(sy, "LD")
                wt(sy, "SP")
                wt(sy, "OUT")
                wt(sy, "GP")
                dst = u2T[:, :].rearrange("p (c j) -> p c j", c=KD)[:, :, t * 128:(t + 1) * 128]
                wt(sy, "TP")
                inc(sy.dma_start(out=dst, in_=u2b[:, :], transpose=True), "TP")
                wt(sy, "TP")
            sec("sync", _)

        def _(sy):
            wt(sy, "AC")
            wt(sy, "LD")
            wt(sy, "SP")
            wt(sy, "OUT")
            wt(sy, "GP")
            wt(sy, "TP")
            wt(sy, "LD")
            inc(sy.dma_start(out=wv[:, :].rearrange("p (c j) -> p c j", c=KD),
                             in_=wqkvT_d.rearrange("p (c j) -> p c j", c=KD)[:, :, 2 * D:3 * D]), "LD")
            wt(sy, "LD")
            inc(sy.dma_start(out=wow[:, :], in_=woT_d), "LD")
        sec("sync", _)
        vw_ld = C["LD"]
        tp_u2 = C["TP"]

        with contextlib.ExitStack() as es2:
            ec2 = es2.enter_context
            psqk0 = ec2(nc.psum_tensor(f"psqk{ch}", [128, 512], F32))
            psqk1 = ec2(nc.psum_tensor(f"psqk2{ch}", [128, 512], F32))
            psv = ec2(nc.psum_tensor(f"psv{ch}", [128, 1024], F32))
            psst = ec2(nc.psum_tensor(f"psst{ch}", [128, 512], F32))
            psqk = [psqk0, psqk1]
            for fc in range(16):
                bk = fc % 2
                def _(sy, fc=fc, bk=bk):
                    if fc >= 2:
                        wt(sy, "PE")
                    src = wqkvT_d.rearrange("p (c j) -> p c j", c=KD)[:, :, fc * 128:(fc + 1) * 128]
                    wt(sy, "LD")
                    inc(sy.dma_start(out=wfc[:, bk * KD * 128:(bk + 1) * KD * 128]
                                     .rearrange("p (c j) -> p c j", c=KD), in_=src), "LD")
                sec("sync", _)

                def _(te, fc=fc, bk=bk):
                    wt(te, "LD")
                    wt(te, "TP", tp_u2)
                    wt(te, "AC")
                    for k in range(KD):
                        ins_ = te.matmul(psqk[bk][:, :],
                                         wfc[:, bk * KD * 128 + k * 128: bk * KD * 128 + (k + 1) * 128],
                                         u2T[:, k * CT:(k + 1) * CT],
                                         start=(k == 0), stop=(k == KD - 1))
                    inc(ins_, "PE")
                sec("tensor", _)

                def _(a, fc=fc, bk=bk):
                    wt(a, "PE")
                    if fc < 8:
                        inc(a.activation(qdm[:, fc * CT:(fc + 1) * CT], psqk[bk][:, :],
                                         AF.Identity, scale=0.125), "AC")
                    else:
                        dst = kdm[:, :].rearrange("p (t j) -> p t j", t=NT)[
                            :, :, (fc - 8) * 128:(fc - 7) * 128]
                        inc(a.activation(dst, psqk[bk][:, :], AF.Sign), "AC")
                sec("scalar", _)
            qk_ac = C["AC"]

            for t in range(NT):
                def _(te, t=t):
                    wt(te, "LD", vw_ld)
                    wt(te, "TP", tp_u2)
                    if t > 0:
                        wt(te, "AC")
                    for half in range(2):
                        for k in range(KD):
                            ins_ = te.matmul(psv[:, half * 512:(half + 1) * 512],
                                             u2T[:, k * CT + t * 128:k * CT + (t + 1) * 128],
                                             wv[:, k * D + half * 512:k * D + half * 512 + 512],
                                             start=(k == 0), stop=(k == KD - 1))
                    inc(ins_, "PE")
                sec("tensor", _)

                def _(a, t=t):
                    wt(a, "PE")
                    inc(a.activation(vtok[:, t * D:(t + 1) * D], psv[:, :], AF.Sign), "AC")
                sec("scalar", _)

            def _(sy):
                wt(sy, "AC", qk_ac)
                wt(sy, "LD")
                wt(sy, "SP")
                wt(sy, "OUT")
                wt(sy, "GP")
                for t in range(NT):
                    wt(sy, "TP")
                    inc(sy.dma_start(out=ktok[:, t * D:(t + 1) * D]
                                     .rearrange("p (c j) -> p c j", c=8),
                                     in_=kdm[:, t * D:(t + 1) * D], transpose=True), "TP")
                    wt(sy, "TP")
            sec("sync", _)
            tp_k = C["TP"]

            with contextlib.ExitStack() as esp:
                spre = esp.enter_context(nc.sbuf_tensor(f"spre{ch}", [128, 512], F32))
                def _(v):
                    inc(v.memset(spre[:, :], 0.0), "VE")
                sec("vector", _)

                for t in range(NT):
                    def _(te, t=t):
                        wt(te, "TP", tp_k)
                        wt(te, "AC")
                        wt(te, "VE")
                        for h in range(H):
                            pb = 64 * (h % 2)
                            fs = (h // 2) * 64
                            ins_ = te.matmul(psst[pb:pb + 64, fs:fs + 64],
                                             ktok[:, t * D + h * 64:t * D + (h + 1) * 64],
                                             vtok[:, t * D + h * 64:t * D + (h + 1) * 64],
                                             start=True, stop=True,
                                             skip_group_check=True)
                        inc(ins_, "PE")
                    sec("tensor", _)

                    def _(v, t=t):
                        wt(v, "PE")
                        v.tensor_copy(sprt[:, t * 512:(t + 1) * 512], spre[:, :])
                        v.drain()
                        inc(v.tensor_tensor(spre[:, :], psst[:, :], spre[:, :],
                                            AluOpType.add), "VE")
                    sec("vector", _)

                def _(v):
                    v.drain()
                    inc(v.tensor_copy(ccs[:, :], spre[:, :]), "VE")
                sec("vector", _)

                def _(sy):
                    wt(sy, "VE")
                    wt(sy, "CC")
                    wt(sy, "SP")
                    inc(sy.dma_start(out=cc_in, in_=ccs[:, :]), "SP")
                sec("sync", _)

                def _(g):
                    wt(g, "SP")
                    wt(g, "LD")
                    wt(g, "CC")
                    inc(g.collective_compute(
                        "AllGather", AluOpType.bypass,
                        replica_groups=[[0, 1, 2, 3], [4, 5, 6, 7]],
                        ins=[cc_in], outs=[cc_out]), "CC")
                sec("gpsimd", _)

                def _(sy):
                    wt(sy, "CC")
                    wt(sy, "LD")
                    inc(sy.dma_start(out=gath[:, :].rearrange("p (j f) -> p j f", j=4),
                                     in_=cc_out.rearrange("(j p) f -> p j f", p=128)), "LD")
                sec("sync", _)

                def _(v):
                    wt(v, "LD")
                    v.tensor_copy(sinc[:, :], sbase[:, :])
                    for j in range(4):
                        v.drain()
                        v.scalar_tensor_tensor(sinc[:, :], gath[:, j * 512:(j + 1) * 512],
                                               mska[:, j:j + 1], sinc[:, :],
                                               AluOpType.mult, AluOpType.add)
                    for j in range(4):
                        v.drain()
                        v.scalar_tensor_tensor(sbase[:, :], gath[:, j * 512:(j + 1) * 512],
                                               1.0, sbase[:, :], AluOpType.mult, AluOpType.add)
                    v.drain()
                    for t in range(NT):
                        ins_ = v.tensor_tensor(stot[:, t * 512:(t + 1) * 512],
                                               sprt[:, t * 512:(t + 1) * 512], sinc[:, :],
                                               AluOpType.add)
                    inc(ins_, "VE")
                sec("vector", _)
                s_ready = C["VE"]
                es2.close()
                if _LIMIT == "track":
                    def _(sy):
                        wt(sy, "VE")
                        wt(sy, "OUT")
                        inc(sy.dma_start(out=dbg_aps["dbg"].rearrange("(t p) j -> p t j", p=128),
                                         in_=xsb[:, :].rearrange("p (t j) -> p t j", t=NT)), "OUT")
                        wt(sy, "OUT")
                        inc(sy.dma_start(out=dbg_aps["dbg2"][:, 0:512], in_=sprt[:, 0:512]), "OUT")
                    sec("sync", _)
                    return

                with contextlib.ExitStack() as es3:
                    ec3 = es3.enter_context
                    pssc0 = ec3(nc.psum_tensor(f"pssc{ch}", [128, 512], F32))
                    pssc1 = ec3(nc.psum_tensor(f"pssc2{ch}", [128, 512], F32))
                    pso4 = [ec3(nc.psum_tensor(f"pso{ch}_{i}", [128, 512], F32))
                            for i in range(4)]
                    pssc = [pssc0, pssc1]
                    for t in range(NT):
                        for p in range(2):
                            for q in range(2):
                                def _(te, t=t, p=p, q=q):
                                    wt(te, "VE")
                                    if p == 0 and q == 0:
                                        wt(te, "AC")
                                    for jj in range(4):
                                        j = 4 * q + jj
                                        h = 2 * j + p
                                        pb = 64 * p
                                        fq = j * CT + t * 128
                                        fk = t * D + j * 128
                                        ins_ = te.matmul(pssc[p][:, jj * 128:(jj + 1) * 128],
                                                         kdm[pb:pb + 64, fk:fk + 128],
                                                         qdm[pb:pb + 64, fq:fq + 128],
                                                         start=True, stop=True,
                                                         skip_group_check=True)
                                    inc(ins_, "PE")
                                sec("tensor", _)

                                def _(v, t=t, p=p, q=q):
                                    wt(v, "PE")
                                    inc(v.tensor_tensor(scb[:, :], pssc[p][:, :], triu4[:, :],
                                                        AluOpType.mult), "VE")
                                sec("vector", _)
                                if _ALIMIT == "sc" and t == 0 and p == 0 and q == 0:
                                    def _(g):
                                        wt(g, "VE")
                                        inc(g.dma_start(out=dbg_aps["dbg"][0:128, 0:512], in_=scb[:, :]), "OUT")
                                    sec("gpsimd", _)
                                    return

                                def _(te, t=t, p=p, q=q):
                                    wt(te, "VE")
                                    for jj in range(4):
                                        j = 4 * q + jj
                                        h = 2 * j + p
                                        pb = 64 * p
                                        fq = j * CT + t * 128
                                        te.matmul(pso4[2 * p + q][pb:pb + 64, jj * 128:(jj + 1) * 128],
                                                  vtok[:, t * D + h * 64:t * D + (h + 1) * 64],
                                                  scb[:, jj * 128:(jj + 1) * 128],
                                                  start=True, stop=False,
                                                  skip_group_check=True)
                                        ins_ = te.matmul(pso4[2 * p + q][pb:pb + 64, jj * 128:(jj + 1) * 128],
                                                  stot[pb:pb + 64, t * 512 + j * 64:
                                                       t * 512 + j * 64 + 64],
                                                  qdm[pb:pb + 64, fq:fq + 128],
                                                  start=False, stop=True,
                                                  skip_group_check=True)
                                    inc(ins_, "PE")
                                sec("tensor", _)
                                if _ALIMIT == "ic" and t == 0 and p == 0 and q == 0:
                                    def _(v):
                                        wt(v, "PE")
                                        inc(v.tensor_copy(sprt[0:64, 0:512], pso4[0][0:64, :]), "VE")
                                    sec("vector", _)
                                    def _(g):
                                        wt(g, "VE")
                                        inc(g.dma_start(out=dbg_aps["dbg"][0:64, 0:512], in_=sprt[0:64, 0:512]), "OUT")
                                    sec("gpsimd", _)
                                    return

                        def _(a, t=t):
                            wt(a, "PE")
                            inc(a.nop(), "AC")
                        sec("scalar", _)

                        def _(v, t=t):
                            wt(v, "AC")
                            rp = rpos[:, t * 128:(t + 1) * 128]
                            for p in range(2):
                                for q in range(2):
                                    pb = 64 * p
                                    rpb = rp[pb:pb + 64, :].unsqueeze(1).broadcast_to((64, 4, 128))
                                    dst = attb[pb:pb + 64,
                                               t * KD * 128 + q * 4 * 128:
                                               t * KD * 128 + (q * 4 + 4) * 128]
                                    ins_ = v.tensor_tensor(
                                        dst.rearrange("p (s i) -> p s i", s=4),
                                        pso4[2 * p + q][pb:pb + 64, :].rearrange("p (s i) -> p s i", s=4),
                                        rpb, AluOpType.mult)
                            inc(ins_, "VE")
                        sec("vector", _)
                        def _(a, t=t):
                            wt(a, "VE")
                            inc(a.nop(), "AC")
                        sec("scalar", _)

            if _LIMIT == "main":
                def _(g):
                    wt(g, "VE")
                    wt(g, "OUT")
                    inc(g.dma_start(out=dbg_aps["dbg"].rearrange("(a p) j -> p a j", p=128),
                                    in_=attb[:, :].rearrange("p (a j) -> p a j", a=NT)), "OUT")
                sec("gpsimd", _)
                return

            with contextlib.ExitStack() as esy:
                psy = esy.enter_context(nc.psum_tensor(f"psy{ch}", [128, 1024], F32))
                for t in range(NT):
                    def _(te, t=t):
                        wt(te, "VE")
                        wt(te, "LD", vw_ld)
                        if t > 0:
                            wt(te, "AC")
                        for half in range(2):
                            for k in range(KD):
                                ins_ = te.matmul(psy[:, half * 512:(half + 1) * 512],
                                                 attb[:, (t * KD + k) * 128:(t * KD + k + 1) * 128],
                                                 wow[:, k * D + half * 512:k * D + half * 512 + 512],
                                                 start=(k == 0), stop=(k == KD - 1))
                        inc(ins_, "PE")
                    sec("tensor", _)

                    def _(v, t=t):
                        wt(v, "PE")
                        xt = xsb[:, t * D:(t + 1) * D]
                        inc(v.tensor_tensor(xt, psy[:, :], xt, AluOpType.add), "VE")
                    sec("vector", _)

                    def _(a, t=t):
                        wt(a, "VE")
                        inc(a.nop(), "AC")
                    sec("scalar", _)

    if _LIMIT == "attn":
        def _(sy):
            wt(sy, "VE")
            wt(sy, "OUT")
            inc(sy.dma_start(out=dbg_aps["dbg"].rearrange("(t p) j -> p t j", p=128),
                             in_=xsb[:, :].rearrange("p (t j) -> p t j", t=NT)), "OUT")
        sec("sync", _)
        return
    ffn_phase(f"f2c{ch}", "out")

    if ch + 1 < NCH:
        def _(sy):
            wt(sy, "VE")
            wt(sy, "LD")
            inc(sy.dma_start(
                out=xsb[:, :].rearrange("p (t j) -> p t j", t=NT),
                in_=x_d[(ch + 1) * CT:(ch + 2) * CT, :].rearrange("(t p) j -> p t j", p=128)), "LD")
        sec("sync", _)

        def _(v):
            wt(v, "LD")
            inc(v.nop(), "VE")
        sec("vector", _)



def _spc(v, scrub, src):
    v.tensor_copy(scrub[:, 2:3], src)

_NC_CACHE = None
_NC_KEY = 'x'


def kernel(L0_attn_x, L0_attn_W_qkv, L0_attn_W_o, L0_ffn_w_gate, L0_ffn_w_up,
           L0_ffn_w_down):
    global _NC_CACHE
    x = np.ascontiguousarray(np.asarray(L0_attn_x, dtype=np.float32)).reshape(B * T, D)
    wqkv = np.ascontiguousarray(np.asarray(L0_attn_W_qkv, dtype=np.float32))
    wo = np.ascontiguousarray(np.asarray(L0_attn_W_o, dtype=np.float32))
    wg = np.ascontiguousarray(np.asarray(L0_ffn_w_gate, dtype=np.float32))
    wu = np.ascontiguousarray(np.asarray(L0_ffn_w_up, dtype=np.float32))
    wd = np.ascontiguousarray(np.asarray(L0_ffn_w_down, dtype=np.float32))

    global _NC_KEY
    if _NC_CACHE is None or _NC_KEY != _LIMIT:
        _NC_CACHE = build_nc()
        _NC_KEY = _LIMIT
    nc = _NC_CACHE

    triu = np.tile(np.triu(np.ones((128, 128), dtype=np.float32)), (1, 4))
    wg_pad = np.zeros((3 * 128 * NCORE, D), dtype=np.float32)
    wg_pad[:II] = wg
    wu_pad = np.zeros((3 * 128 * NCORE, D), dtype=np.float32)
    wu_pad[:II] = wu
    in_maps = []
    spans = []
    for c in range(NCORE):
        seq, q = divmod(c, 4)
        starts = [seq * T + ch * (4 * CT) + q * CT for ch in range(NCH)]
        spans.append(starts)
        xo = np.concatenate([x[s:s + CT] for s in starts], axis=0)
        pos = np.concatenate(
            [np.arange(ch * 4 * CT + q * CT + 1, ch * 4 * CT + q * CT + CT + 1)
             for ch in range(NCH)]).astype(np.float32)
        rpos = np.broadcast_to((1.0 / pos)[None, :], (128, NCH * CT)).copy()
        mask = np.broadcast_to((np.arange(4) < q).astype(np.float32)[None, :], (128, 4)).copy()
        in_maps.append({
            "x": xo, "wqkv": wqkv, "wo": wo,
            "wg": wg_pad[c * 384:(c + 1) * 384],
            "wu": wu_pad[c * 384:(c + 1) * 384],
            "wd": wd[c * 128:(c + 1) * 128],
            "rposd": rpos, "triu4": triu, "maska": mask,
        })

    res = run_bass_kernel_spmd(nc, in_maps, core_ids=list(range(NCORE)))
    global _RES
    _RES = res
    out = np.empty((B * T, D), dtype=np.float32)
    for c in range(NCORE):
        yc = res.results[c]["y"]
        for ch in range(NCH):
            out[spans[c][ch]:spans[c][ch] + CT] = yc[ch * CT:(ch + 1) * CT]
    return out.reshape(B, T, D)


# version 04
# speedup vs baseline: 1.1303x; 1.0264x over previous
import numpy as np
import ml_dtypes
import concourse.bacc as bacc
import concourse.mybir as mybir
import concourse.dve_ops as dve_ops
from concourse.dve_op import DveOp
from concourse.dve_spec import (
    Spec, Src0, Src1, C0, C1, C2, Zero, One, maxx, minn, lower, Bin, AluOp,
)
from concourse.alu_op_type import AluOpType
from concourse.bass_utils import run_bass_kernel_spmd

F32 = mybir.dt.float32
BF16 = mybir.dt.bfloat16
AF = mybir.ActivationFunctionType
AX = mybir.AxisListType.X
MAGIC = float(2 ** 23)

def _mk_op(name, spec, reference):
    sha = lower(spec, ver="v3", perf_en=False, name=name).sha("v3")
    return DveOp(name, spec, subdim=False, uops_sha={"v3": sha}, reference=reference)

_t = Bin(AluOp.ABSOLUTE_VALUE, Src0, Src0)
_two = One + One
_w = minn(minn(_t + _t, _t + _two), _t * C0 + C1)
FP4A_SPEC = Spec(body=_w + C2)

def _ref_fp4a(in0, in1, s0, s1, imm2):
    t = np.abs(in0)
    w = np.minimum(np.minimum(t + t, t + 2.0), t * 0.5 + 4.0)
    return (np.ceil(w - 0.5) + imm2).astype(in0.dtype)

_g = Src0 >= C0
SIGNSCALE_SPEC = Spec(body=((_g + _g) - C1) * Src1)

def _ref_signscale(in0, in1, s0, s1, imm2):
    return (np.where(in0 >= 0, 1.0, -1.0) * in1).astype(in0.dtype)

_p1 = Src0 + Src0
_p2 = _p1 + _p1
FP4B_SPEC = Spec(body=maxx(maxx(Src0 - C2, _p1 - C0), _p2 - C1) * Src1)

def _ref_fp4b(in0, in1, s0, s1, imm2):
    i = in0 - imm2
    q2 = np.maximum(np.maximum(i, i + i - 4.0), 4.0 * i - 16.0)
    return (q2 * in1).astype(in0.dtype)

FP4A = _mk_op("ANT_FP4A", FP4A_SPEC, _ref_fp4a)
SIGNSCALE = _mk_op("ANT_SIGNSCALE", SIGNSCALE_SPEC, _ref_signscale)
FP4B = _mk_op("ANT_FP4B", FP4B_SPEC, _ref_fp4b)

def _register_ops():
    names = [o.name for o in dve_ops.OPS]
    if "ANT_FP4A" in names:
        return
    i0 = names.index("DVEHASH32L_SEED")
    for j, op in enumerate([FP4A, SIGNSCALE, FP4B]):
        old = dve_ops.OPS[i0 + j]
        dve_ops.OPS[i0 + j] = op
        dve_ops._SUB_OPCODE_FOR_NAME[op.name] = dve_ops._SUB_OPCODE_FOR_NAME[old.name]
        dve_ops.CUSTOM_DVE_REFS[op.name] = op.reference
        dve_ops.CUSTOM_DVE_SPECS[op.name] = op.spec

_register_ops()

B, T, D = 2, 8192, 1024
II = 2816
H, DH = 16, 64
NCH = 4
CT = 512
NT = CT // 128
NCORE = 8
EPS = 1e-5
KD = D // 128
KI = II // 128
OC = [(0, 512), (512, 512), (1024, 512), (1536, 512), (2048, 512), (2560, 256)]

_DEBUG = False
_LIMIT = None
_FSTAGE = None
_ALIMIT = None


def build_nc():
    nc = bacc.Bacc("TRN2", target_bir_lowering=False, debug=False)

    x_d = nc.dram_tensor("x", [NCH * CT, D], F32, kind="ExternalInput").ap()
    wqkv_d = nc.dram_tensor("wqkv", [3 * D, D], F32, kind="ExternalInput").ap()
    wo_d = nc.dram_tensor("wo", [D, D], F32, kind="ExternalInput").ap()
    wg_d = nc.dram_tensor("wg", [3 * 128, D], F32, kind="ExternalInput").ap()
    wu_d = nc.dram_tensor("wu", [3 * 128, D], F32, kind="ExternalInput").ap()
    wd_d = nc.dram_tensor("wd", [128, II], F32, kind="ExternalInput").ap()
    rpos_d = nc.dram_tensor("rposd", [128, NCH * CT], F32, kind="ExternalInput").ap()
    triu_d = nc.dram_tensor("triu4", [128, 512], F32, kind="ExternalInput").ap()
    mska_d = nc.dram_tensor("maska", [128, 4], F32, kind="ExternalInput").ap()
    y_d = nc.dram_tensor("y", [NCH * CT, D], F32, kind="ExternalOutput").ap()

    wgT_d = nc.dram_tensor("wgT", [128, KD * II], BF16, kind="Internal").ap()
    wuT_d = nc.dram_tensor("wuT", [128, KD * II], BF16, kind="Internal").ap()
    wdT_d = nc.dram_tensor("wdT", [128, KI * D], BF16, kind="Internal").ap()
    wqkvT_d = nc.dram_tensor("wqkvT", [128, KD * 3 * D], BF16, kind="Internal").ap()
    woT_d = nc.dram_tensor("woT", [128, KD * D], BF16, kind="Internal").ap()
    cc_in = nc.dram_tensor("cc_in", [128, 512], F32, kind="Internal").ap()
    cc_out = nc.dram_tensor("cc_out", [4 * 128, 512], F32, kind="Internal").ap()
    QIN = 3 * KD * 128 * 2 + KI * 128
    wq_in = nc.dram_tensor("wq_in", [128, QIN], BF16, kind="Internal").ap()
    wq_out = nc.dram_tensor("wq_out", [8 * 128, QIN], BF16, kind="Internal").ap()

    dbg_aps = {}
    if _LIMIT in ("ffn1", "track", "attn", "main") or _FSTAGE or _ALIMIT:
        dbg_aps["dbg"] = nc.dram_tensor("dbg", [CT, D], F32, kind="ExternalOutput").ap()
        dbg_aps["dbg2"] = nc.dram_tensor("dbg2", [128, 4 * 512], F32, kind="ExternalOutput").ap()

    C = {}
    SEM = {}

    import contextlib
    st = contextlib.ExitStack()

    def sem(name):
        SEM[name] = st.enter_context(nc.semaphore(name))
        C[name] = 0

    for s in ["LD", "GP", "TP", "SP", "VE", "AC", "PE", "CC", "OUT"]:
        sem(s)

    def inc(instr, s, k=None):
        k = k if k is not None else (16 if s in ("LD", "GP", "TP", "SP", "OUT") else 1)
        instr.then_inc(SEM[s], k)
        C[s] += k

    def wt(eng, s, v=None):
        eng.wait_ge(SEM[s], C[s] if v is None else v)

    with st:
        xsb = st.enter_context(nc.sbuf_tensor("xsb", [128, NT * D], F32))
        xout = st.enter_context(nc.sbuf_tensor("xout", [128, D], F32))
        rpos = st.enter_context(nc.sbuf_tensor("rposc", [128, CT], F32))
        triu4 = st.enter_context(nc.sbuf_tensor("triu4s", [128, 512], F32))
        mska = st.enter_context(nc.sbuf_tensor("mskas", [128, 4], F32))
        sbase = st.enter_context(nc.sbuf_tensor("sbase", [128, 512], F32))
        sinc = st.enter_context(nc.sbuf_tensor("sinc", [128, 512], F32))
        ccs = st.enter_context(nc.sbuf_tensor("ccs", [128, 512], F32))
        stats = st.enter_context(nc.sbuf_tensor("stats", [128, 16], F32))
        scrub = st.enter_context(nc.sbuf_tensor("scrub", [128, 4], F32))
        block = st.enter_context(nc.Block())
        if True:
            def sec(engname, fn):
                getattr(block, engname)(fn)

            import contextlib as _ctx
            with _ctx.ExitStack() as es0:
                ec0 = es0.enter_context
                wf32 = ec0(nc.sbuf_tensor("wf32", [128, II], F32))
                wz = ec0(nc.sbuf_tensor("wz", [128, II], F32))
                wsbf = ec0(nc.sbuf_tensor("wsbf", [128, II], F32))
                wq = ec0(nc.sbuf_tensor("wq", [128, II], BF16))
                wtst = ec0(nc.sbuf_tensor("wtst", [128, II], BF16))
                wmb = ec0(nc.sbuf_tensor("wmb", [128, II // 16], F32))
                wrs = ec0(nc.sbuf_tensor("wrs", [128, II // 16], F32))
                wsb2 = ec0(nc.sbuf_tensor("wsb2", [128, II // 16], F32))
                def _(sy):
                    wt(sy, "LD")
                    wt(sy, "SP")
                    wt(sy, "OUT")
                    wt(sy, "GP")
                    wt(sy, "TP")
                    wt(sy, "LD")
                    inc(sy.dma_start(out=triu4[:, :], in_=triu_d), "LD")
                    wt(sy, "LD")
                    inc(sy.dma_start(out=mska[:, :], in_=mska_d), "LD")
                    wt(sy, "LD")
                    inc(sy.dma_start(
                        out=xsb[:, :].rearrange("p (t j) -> p t j", t=NT),
                        in_=x_d[0:CT, :].rearrange("(t p) j -> p t j", p=128)), "LD")
                sec("sync", _)

                def _(v):
                    v.memset(sbase[:, :], 0.0)
                    inc(v.memset(scrub[:, 1:2], EPS), "VE")
                sec("vector", _)

                def quant_weight(src_d, off, nrows, K):
                    npc = nrows // 128
                    kc = K // 128
                    NB = K // 16
                    for pc in range(npc):
                        def _(sy, pc=pc):
                            wt(sy, "VE")
                            wt(sy, "LD")
                            inc(sy.dma_start(out=wf32[:, 0:K],
                                             in_=src_d[pc * 128:(pc + 1) * 128, :]), "LD")
                        sec("sync", _)

                        def _(v, pc=pc, K=K, NB=NB):
                            wt(v, "LD")
                            wt(v, "TP")
                            rsb = wrs[:, 0:NB].unsqueeze(2).broadcast_to((128, NB, 16))
                            sbb = wsb2[:, 0:NB].unsqueeze(2).broadcast_to((128, NB, 16))
                            v.tensor_reduce(wmb[:, 0:NB],
                                            wf32[:, 0:K].rearrange("p (a b) -> p a b", b=16),
                                            AX, AluOpType.max, apply_absolute_value=True)
                            v.drain()
                            v.tensor_scalar(wmb[:, 0:NB], wmb[:, 0:NB], 6e-12, None, AluOpType.max)
                            v.drain()
                            v.reciprocal(wrs[:, 0:NB], wmb[:, 0:NB])
                            v.tensor_scalar(wsb2[:, 0:NB], wmb[:, 0:NB], 1.0 / 12.0, None, AluOpType.mult)
                            v.drain()
                            v.tensor_scalar(wrs[:, 0:NB], wrs[:, 0:NB], 6.0, None, AluOpType.mult)
                            v.tensor_copy(wsbf[:, 0:K].rearrange("p (a b) -> p a b", b=16), sbb)
                            v.drain()
                            v.tensor_tensor(wz[:, 0:K].rearrange("p (a b) -> p a b", b=16),
                                            wf32[:, 0:K].rearrange("p (a b) -> p a b", b=16),
                                            rsb, AluOpType.mult)
                            v._custom_dve(SIGNSCALE, out=wsbf[:, 0:K], in0=wf32[:, 0:K],
                                          in1=wsbf[:, 0:K], s0=0.0, s1=1.0)
                            v.drain()
                            v._custom_dve(FP4A, out=wz[:, 0:K], in0=wz[:, 0:K],
                                          s0=0.5, s1=4.0, imm2=MAGIC)
                            v.drain()
                            inc(v._custom_dve(FP4B, out=wq[:, 0:K], in0=wz[:, 0:K],
                                              in1=wsbf[:, 0:K], s0=2 * MAGIC + 4.0,
                                              s1=4 * MAGIC + 16.0, imm2=MAGIC), "VE")
                        sec("vector", _)

                        def _(sy, pc=pc, K=K, kc=kc, off=off):
                            wt(sy, "VE")
                            wt(sy, "LD")
                            wt(sy, "SP")
                            wt(sy, "OUT")
                            wt(sy, "GP")
                            wt(sy, "TP")
                            inc(sy.dma_start(
                                out=wtst[:, 0:K].rearrange("p (c j) -> p c j", c=kc),
                                in_=wq[:, 0:K], transpose=True), "TP")
                            wt(sy, "TP")
                            wt(sy, "SP")
                            inc(sy.dma_start(
                                out=wq_in[:, off + pc * K:off + (pc + 1) * K],
                                in_=wtst[:, 0:K]), "SP")
                        sec("sync", _)

                quant_weight(wg_d, 0, 3 * 128, D)
                quant_weight(wu_d, 3 * KD * 128, 3 * 128, D)
                quant_weight(wd_d, 6 * KD * 128, 128, II)

                def cast_weight(src_d, dst_d, nrows):
                    npc = nrows // 128
                    for pc in range(npc):
                        def _(g, pc=pc):
                            wt(g, "TP")
                            wt(g, "GP")
                            inc(g.dma_start(out=wq[:, 0:D],
                                            in_=src_d[pc * 128:(pc + 1) * 128, :]), "GP")
                        sec("gpsimd", _)

                        def _(sy, pc=pc, nrows=nrows):
                            wt(sy, "GP")
                            wt(sy, "LD")
                            wt(sy, "SP")
                            wt(sy, "OUT")
                            wt(sy, "TP")
                            inc(sy.dma_start(
                                out=wtst[:, 0:D].rearrange("p (c j) -> p c j", c=KD),
                                in_=wq[:, 0:D], transpose=True), "TP")
                            wt(sy, "TP")
                            dst = dst_d.rearrange("p (c r) -> p c r", c=KD)[:, :, pc * 128:(pc + 1) * 128]
                            wt(sy, "SP")
                            inc(sy.dma_start(out=dst,
                                             in_=wtst[:, 0:D].rearrange("p (c j) -> p c j", c=KD)), "SP")
                        sec("sync", _)

                def _(g):
                    wt(g, "SP")
                    wt(g, "CC")
                    inc(g.collective_compute(
                        "AllGather", AluOpType.bypass,
                        replica_groups=[[0, 1, 2, 3, 4, 5, 6, 7]],
                        ins=[wq_in], outs=[wq_out]), "CC")
                sec("gpsimd", _)

                cast_weight(wqkv_d, wqkvT_d, 3 * D)
                cast_weight(wo_d, woT_d, D)

                def _(sy):
                    wt(sy, "CC")
                    k = 0
                    for dst_d2, off, kc2, nt, spc in (
                            (wgT_d, 0, KD, 22, 3),
                            (wuT_d, 3 * KD * 128, KD, 22, 3),
                            (wdT_d, 6 * KD * 128, KI, 8, 1)):
                        for t in range(nt):
                            j, s = divmod(t, spc)
                            src = wq_out[j * 128:(j + 1) * 128,
                                         off + s * kc2 * 128:
                                         off + (s + 1) * kc2 * 128].rearrange(
                                "p (c r) -> p c r", c=kc2)
                            dst = dst_d2.rearrange(
                                "p (c r) -> p c r", c=kc2)[:, :, t * 128:(t + 1) * 128]
                            ch_sem = "SP" if k % 2 == 0 else "LD"
                            k += 1
                            wt(sy, ch_sem)
                            inc(sy.dma_start(out=dst, in_=src), ch_sem)
                sec("sync", _)

            nch = 0 if _LIMIT == "setup" else (1 if _LIMIT in ("ffn1", "track", "attn", "chunk1", "main") else NCH)
            for ch in range(nch):
                chunk(nc, block, sec, C, SEM, inc, wt, ch,
                      x_d, y_d, wgT_d, wuT_d, wdT_d, wqkvT_d, woT_d,
                      cc_in, cc_out, rpos_d,
                      xsb, xout, rpos, triu4, mska, sbase, sinc, ccs,
                      stats, scrub, dbg_aps)

            def _(sy):
                wt(sy, "OUT")
            sec("sync", _)

    nc.finalize()
    return nc


def chunk(nc, block, sec, C, SEM, inc, wt, ch,
          x_d, y_d, wgT_d, wuT_d, wdT_d, wqkvT_d, woT_d, cc_in, cc_out, rpos_d,
          xsb, xout, rpos, triu4, mska, sbase, sinc, ccs, stats, scrub,
          dbg_aps):
    F32 = mybir.dt.float32
    BF16 = mybir.dt.bfloat16

    def ffn_phase(tag, resid_scale_out):
        import contextlib
        with contextlib.ExitStack() as es:
            ec = es.enter_context
            WgT = ec(nc.sbuf_tensor(f"WgT{tag}", [128, KD * II], BF16))
            WuT = ec(nc.sbuf_tensor(f"WuT{tag}", [128, KD * II], BF16))
            WdT = ec(nc.sbuf_tensor(f"WdT{tag}", [128, KI * D], BF16))
            cbuf = ec(nc.sbuf_tensor(f"cbuf{tag}", [128, D], F32))
            zbuf = ec(nc.sbuf_tensor(f"zbuf{tag}", [128, D], F32))
            sbf = ec(nc.sbuf_tensor(f"sbf{tag}", [128, D], F32))
            xq = ec(nc.sbuf_tensor(f"xq{tag}", [128, D], BF16))
            xqT = ec(nc.sbuf_tensor(f"xqT{tag}", [128, KD * 128], BF16))
            sgb = ec(nc.sbuf_tensor(f"sg{tag}", [128, 512], F32))
            hb = ec(nc.sbuf_tensor(f"hb{tag}", [128, 512], F32))
            zh = ec(nc.sbuf_tensor(f"zh{tag}", [128, 512], F32))
            hq = ec(nc.sbuf_tensor(f"hq{tag}", [128, II], BF16))
            hqT = ec(nc.sbuf_tensor(f"hqT{tag}", [128, KI * 128], BF16))
            mb = ec(nc.sbuf_tensor(f"mb{tag}", [128, 64], F32))
            rs = ec(nc.sbuf_tensor(f"rs{tag}", [128, 64], F32))
            sb2 = ec(nc.sbuf_tensor(f"sb2{tag}", [128, 64], F32))
            psg0 = ec(nc.psum_tensor(f"psg{tag}", [128, 512], F32))
            psg1 = ec(nc.psum_tensor(f"psg2{tag}", [128, 512], F32))
            psu0 = ec(nc.psum_tensor(f"psu{tag}", [128, 512], F32))
            psu1 = ec(nc.psum_tensor(f"psu2{tag}", [128, 512], F32))
            psd = ec(nc.psum_tensor(f"psd{tag}", [128, 1024], F32))
            psg = [psg0, psg1]
            psu = [psu0, psu1]

            x_ld = C["LD"]
            def _(sy):
                wt(sy, "VE")
                wt(sy, "AC")
                wt(sy, "PE")
                wt(sy, "SP")
                wt(sy, "LD")
                inc(sy.dma_start(out=WgT[:, :], in_=wgT_d), "LD")
                wt(sy, "LD")
                inc(sy.dma_start(out=WuT[:, :], in_=wuT_d), "LD")
                wt(sy, "LD")
                inc(sy.dma_start(out=WdT[:, :], in_=wdT_d), "LD")
            sec("sync", _)
            w_ld = C["LD"]

            fstage = _FSTAGE if tag.startswith("f1c0") else None
            ntt = 1 if fstage else NT
            for t in range(ntt):
                xt = xsb[:, t * D:(t + 1) * D]

                def _(a, t=t):
                    wt(a, "VE")
                    if t == 0:
                        wt(a, "LD", x_ld)
                    a.activation(zbuf[:, :], xt, AF.Square, accum_out=stats[:, 1:2])
                    a.activation(cbuf[:, :], xt, AF.Identity, accum_out=stats[:, 0:1])
                    a.drain()
                    a.activation(stats[:, 2:3], stats[:, 0:1], AF.Identity, scale=1.0 / D)
                    a.activation(stats[:, 3:4], stats[:, 1:2], AF.Identity,
                                 scale=1.0 / D, bias=scrub[:, 1:2])
                    a.drain()
                    a.activation(stats[:, 8:9], stats[:, 2:3], AF.Square)
                    a.drain()
                    a.activation(stats[:, 4:5], stats[:, 8:9], AF.Identity,
                                 scale=-1.0, bias=stats[:, 3:4])
                    a.drain()
                    a.activation(stats[:, 5:6], stats[:, 4:5], AF.Ln)
                    a.drain()
                    inc(a.activation(stats[:, 6:7], stats[:, 5:6], AF.Exp, scale=-0.5), "AC")
                sec("scalar", _)
                if fstage == "ln":
                    def _(sy):
                        wt(sy, "AC")
                        wt(sy, "OUT")
                        inc(sy.dma_start(out=dbg_aps["dbg2"][:, 0:16], in_=stats[:, :]), "OUT")
                    sec("sync", _)
                    return

                def _(v, t=t):
                    wt(v, "AC")
                    rsb = rs[:, :].unsqueeze(2).broadcast_to((128, 64, 16))
                    sbb = sb2[:, :].unsqueeze(2).broadcast_to((128, 64, 16))
                    v.tensor_scalar(cbuf[:, :], xt, stats[:, 2:3], None, AluOpType.subtract)
                    v.drain()
                    v.tensor_reduce(mb[:, :], cbuf[:, :].rearrange("p (a b) -> p a b", b=16),
                                    AX, AluOpType.max, apply_absolute_value=True)
                    v.drain()
                    v.tensor_scalar(mb[:, :], mb[:, :], 6e-12, None, AluOpType.max)
                    v.drain()
                    v.reciprocal(rs[:, :], mb[:, :])
                    v.tensor_scalar(sb2[:, :], mb[:, :], stats[:, 6:7], 1.0 / 12.0,
                                    AluOpType.mult, AluOpType.mult)
                    v.drain()
                    v.tensor_scalar(rs[:, :], rs[:, :], 6.0, None, AluOpType.mult)
                    v.tensor_copy(sbf[:, :].rearrange("p (a b) -> p a b", b=16), sbb)
                    v.drain()
                    v.tensor_tensor(zbuf[:, :].rearrange("p (a b) -> p a b", b=16),
                                    cbuf[:, :].rearrange("p (a b) -> p a b", b=16),
                                    rsb, AluOpType.mult)
                    v._custom_dve(SIGNSCALE, out=cbuf[:, :], in0=cbuf[:, :], in1=sbf[:, :],
                                  s0=0.0, s1=1.0)
                    v.drain()
                    v._custom_dve(FP4A, out=zbuf[:, :], in0=zbuf[:, :], s0=0.5, s1=4.0, imm2=MAGIC)
                    v.drain()
                    inc(v._custom_dve(FP4B, out=xq[:, :], in0=zbuf[:, :], in1=cbuf[:, :],
                                      s0=2 * MAGIC + 4.0, s1=4 * MAGIC + 16.0, imm2=MAGIC), "VE")
                sec("vector", _)

                def _(sy, t=t):
                    wt(sy, "VE")
                    wt(sy, "LD")
                    wt(sy, "SP")
                    wt(sy, "OUT")
                    wt(sy, "GP")
                    wt(sy, "TP")
                    inc(sy.dma_start(out=xqT[:, :].rearrange("p (c j) -> p c j", c=KD),
                                     in_=xq[:, :], transpose=True), "TP")
                    wt(sy, "TP")
                sec("sync", _)
                tp_xq = C["TP"]
                if fstage == "quant":
                    def _(v):
                        wt(v, "TP")
                        inc(v.tensor_copy(sbf[:, :], xq[:, :]), "VE")
                    sec("vector", _)
                    def _(sy):
                        wt(sy, "VE")
                        wt(sy, "OUT")
                        inc(sy.dma_start(out=dbg_aps["dbg"][0:128, :], in_=sbf[:, :]), "OUT")
                        wt(sy, "OUT")
                        inc(sy.dma_start(out=dbg_aps["dbg2"][:, 0:64], in_=mb[:, :]), "OUT")
                    sec("sync", _)
                    return

                if fstage == "wload":
                    def _(a, t=t):
                        wt(a, "LD", w_ld)
                        wt(a, "TP", tp_xq)
                        inc(a.activation(sgb[:, :], WgT[:, 0:512], AF.Identity), "AC")
                    sec("scalar", _)
                    def _(v, t=t):
                        wt(v, "TP", tp_xq)
                        inc(v.tensor_copy(hb[:, 0:128], xqT[:, 0:128]), "VE")
                    sec("vector", _)
                    def _(sy):
                        wt(sy, "AC")
                        wt(sy, "VE")
                        wt(sy, "OUT")
                        inc(sy.dma_start(out=dbg_aps["dbg"][0:128, 0:512], in_=sgb[:, :]), "OUT")
                        wt(sy, "OUT")
                        inc(sy.dma_start(out=dbg_aps["dbg"][128:256, 0:128], in_=hb[:, 0:128]), "OUT")
                    sec("sync", _)
                    return
                if fstage in ("mm", "mm1", "mmc"):
                    if fstage == "mmc":
                        def _(v, t=t):
                            wt(v, "LD", w_ld)
                            inc(v.tensor_copy(hq[:, 0:512], WgT[:, 0:512]), "VE")
                        sec("vector", _)
                    def _(te, t=t):
                        wt(te, "TP", tp_xq)
                        wt(te, "LD", w_ld)
                        wt(te, "VE")
                        nk = 1 if fstage == "mm1" else KD
                        for k in range(nk):
                            rhs = hq[:, 0:512] if fstage == "mmc" else WgT[:, k * II: k * II + 512]
                            ins_ = te.matmul(psg[0][:, :], xqT[:, k * 128:(k + 1) * 128], rhs,
                                             start=(k == 0), stop=(k == nk - 1))
                        inc(ins_, "PE")
                    sec("tensor", _)
                    def _(a, t=t):
                        wt(a, "PE")
                        inc(a.activation(sgb[:, :], psg[0][:, :], AF.Identity), "AC")
                    sec("scalar", _)
                    def _(sy):
                        wt(sy, "AC")
                        wt(sy, "OUT")
                        inc(sy.dma_start(out=dbg_aps["dbg"][0:128, 0:512], in_=sgb[:, :]), "OUT")
                    sec("sync", _)
                    return

                for oi, (o0, ow) in enumerate(OC):
                    bk = oi % 2
                    nb = ow // 16
                    def _(te, t=t, oi=oi, o0=o0, ow=ow, bk=bk):
                        wt(te, "TP", tp_xq)
                        wt(te, "LD", w_ld)
                        wt(te, "VE")
                        wt(te, "AC")
                        for k in range(KD):
                            ins_ = te.matmul(psg[bk][:, 0:ow], xqT[:, k * 128:(k + 1) * 128],
                                             WgT[:, k * II + o0:k * II + o0 + ow],
                                             start=(k == 0), stop=(k == KD - 1))
                        inc(ins_, "PE")
                        for k in range(KD):
                            ins_ = te.matmul(psu[bk][:, 0:ow], xqT[:, k * 128:(k + 1) * 128],
                                             WuT[:, k * II + o0:k * II + o0 + ow],
                                             start=(k == 0), stop=(k == KD - 1))
                        inc(ins_, "PE")
                    sec("tensor", _)

                    def _(a, t=t, oi=oi, o0=o0, ow=ow, bk=bk):
                        wt(a, "PE")
                        inc(a.activation(sgb[:, 0:ow], psg[bk][:, 0:ow], AF.Silu), "AC")
                    sec("scalar", _)

                    def _(v, t=t, oi=oi, o0=o0, ow=ow, bk=bk, nb=nb):
                        wt(v, "AC")
                        rsb = rs[:, 0:nb].unsqueeze(2).broadcast_to((128, nb, 16))
                        sbb = sb2[:, 0:nb].unsqueeze(2).broadcast_to((128, nb, 16))
                        v.tensor_tensor(hb[:, 0:ow], psu[bk][:, 0:ow], sgb[:, 0:ow], AluOpType.mult)
                        v.drain()
                        if fstage == "gu0" and oi == 0:
                            inc(v.tensor_copy(zh[:, 0:ow], hb[:, 0:ow]), "VE")
                            return
                        v.tensor_reduce(mb[:, 0:nb], hb[:, 0:ow].rearrange("p (a b) -> p a b", b=16),
                                        AX, AluOpType.max, apply_absolute_value=True)
                        v.drain()
                        v.tensor_scalar(mb[:, 0:64], mb[:, 0:64], 6e-12, None, AluOpType.max)
                        v.drain()
                        v.reciprocal(rs[:, 0:64], mb[:, 0:64])
                        v.tensor_scalar(sb2[:, 0:64], mb[:, 0:64], 1.0 / 12.0, None, AluOpType.mult)
                        v.drain()
                        v.tensor_scalar(rs[:, 0:64], rs[:, 0:64], 6.0, None, AluOpType.mult)
                        v.tensor_copy(sbf[:, 0:ow].rearrange("p (a b) -> p a b", b=16), sbb)
                        v.drain()
                        v.tensor_tensor(zh[:, 0:ow].rearrange("p (a b) -> p a b", b=16),
                                        hb[:, 0:ow].rearrange("p (a b) -> p a b", b=16),
                                        rsb, AluOpType.mult)
                        v._custom_dve(SIGNSCALE, out=hb[:, 0:ow], in0=hb[:, 0:ow],
                                      in1=sbf[:, 0:ow], s0=0.0, s1=1.0)
                        v.drain()
                        v._custom_dve(FP4A, out=zh[:, 0:ow], in0=zh[:, 0:ow],
                                      s0=0.5, s1=4.0, imm2=MAGIC)
                        v.drain()
                        inc(v._custom_dve(FP4B, out=hq[:, o0:o0 + ow], in0=zh[:, 0:ow],
                                          in1=hb[:, 0:ow], s0=2 * MAGIC + 4.0,
                                          s1=4 * MAGIC + 16.0, imm2=MAGIC), "VE")
                    sec("vector", _)
                    if fstage == "gu0" and oi == 0:
                        def _(sy):
                            wt(sy, "VE")
                            wt(sy, "OUT")
                            inc(sy.dma_start(out=dbg_aps["dbg"][0:128, 0:512], in_=zh[:, 0:512]), "OUT")
                            wt(sy, "OUT")
                            inc(sy.dma_start(out=dbg_aps["dbg"][128:256, 0:512], in_=sgb[:, 0:512]), "OUT")
                        sec("sync", _)
                        return
                    if fstage == "gu" and oi == 0:
                        def _(v):
                            inc(v.tensor_copy(zh[:, :], hq[:, 0:512]), "VE")
                        sec("vector", _)
                        def _(sy):
                            wt(sy, "VE")
                            wt(sy, "OUT")
                            inc(sy.dma_start(out=dbg_aps["dbg"][0:128, 0:512], in_=zh[:, :]), "OUT")
                            wt(sy, "OUT")
                            inc(sy.dma_start(out=dbg_aps["dbg"][128:256, 0:512], in_=sgb[:, :]), "OUT")
                        sec("sync", _)
                        return

                def _(sy, t=t):
                    wt(sy, "VE")
                    wt(sy, "LD")
                    wt(sy, "SP")
                    wt(sy, "OUT")
                    wt(sy, "GP")
                    wt(sy, "TP")
                    inc(sy.dma_start(out=hqT[:, :].rearrange("p (c j) -> p c j", c=KI),
                                     in_=hq[:, :], transpose=True), "TP")
                    wt(sy, "TP")
                sec("sync", _)
                tp_hq = C["TP"]

                def _(te, t=t):
                    wt(te, "TP", tp_hq)
                    wt(te, "VE")
                    for half in range(2):
                        for k in range(KI):
                            ins_ = te.matmul(psd[:, half * 512:(half + 1) * 512],
                                             hqT[:, k * 128:(k + 1) * 128],
                                             WdT[:, k * D + half * 512:k * D + half * 512 + 512],
                                             start=(k == 0), stop=(k == KI - 1))
                    inc(ins_, "PE")
                sec("tensor", _)

                if resid_scale_out is None:
                    def _(v, t=t):
                        wt(v, "PE")
                        inc(v.scalar_tensor_tensor(xt, psd[:, 0:D], 0.5, xt,
                                                   AluOpType.mult, AluOpType.add), "VE")
                    sec("vector", _)
                    if fstage == "down":
                        def _(sy):
                            wt(sy, "VE")
                            wt(sy, "OUT")
                            inc(sy.dma_start(out=dbg_aps["dbg"][0:128, :], in_=xt), "OUT")
                        sec("sync", _)
                        return
                else:
                    def _(v, t=t):
                        wt(v, "PE")
                        wt(v, "OUT")
                        inc(v.scalar_tensor_tensor(xout[:, :], psd[:, 0:D], 0.5, xt,
                                                   AluOpType.mult, AluOpType.add), "VE")
                    sec("vector", _)

                    def _(sy, t=t):
                        wt(sy, "VE")
                        rows = y_d[ch * CT + t * 128: ch * CT + (t + 1) * 128, :]
                        wt(sy, "LD")
                        wt(sy, "SP")
                        wt(sy, "OUT")
                        wt(sy, "GP")
                        wt(sy, "TP")
                        wt(sy, "OUT")
                        inc(sy.dma_start(out=rows, in_=xout[:, :]), "OUT")
                    sec("sync", _)

    ffn_phase(f"f1c{ch}", None)
    if _FSTAGE:
        return
    if _LIMIT == "ffn1":
        def _(sy):
            wt(sy, "VE")
            wt(sy, "OUT")
            inc(sy.dma_start(out=dbg_aps["dbg"].rearrange("(t p) j -> p t j", p=128),
                             in_=xsb[:, :].rearrange("p (t j) -> p t j", t=NT)), "OUT")
        sec("sync", _)
        return

    import contextlib
    with contextlib.ExitStack() as es:
        ec = es.enter_context
        gath = ec(nc.sbuf_tensor(f"gath{ch}", [128, 4 * 512], F32))
        u2T = ec(nc.sbuf_tensor(f"u2T{ch}", [128, KD * CT], BF16))
        u2b = ec(nc.sbuf_tensor(f"u2b{ch}", [128, D], BF16))
        qdm = ec(nc.sbuf_tensor(f"qdm{ch}", [128, 8 * CT], BF16))
        kdm = ec(nc.sbuf_tensor(f"kdm{ch}", [128, 8 * CT], BF16))
        ktok = ec(nc.sbuf_tensor(f"ktok{ch}", [128, NT * D], BF16))
        vtok = ec(nc.sbuf_tensor(f"vtok{ch}", [128, NT * D], BF16))
        wfc = ec(nc.sbuf_tensor(f"wfc{ch}", [128, 2 * KD * 128], BF16))
        wv = ec(nc.sbuf_tensor(f"wv{ch}", [128, KD * D], BF16))
        wow = ec(nc.sbuf_tensor(f"wow{ch}", [128, KD * D], BF16))
        scb = ec(nc.sbuf_tensor(f"scb{ch}", [128, 512], BF16))
        stot = ec(nc.sbuf_tensor(f"stot{ch}", [128, NT * 512], BF16))
        sprt = ec(nc.sbuf_tensor(f"sprt{ch}", [128, NT * 512], F32))
        attb = ec(nc.sbuf_tensor(f"attb{ch}", [128, NT * KD * 128], BF16))
        cbufa = ec(nc.sbuf_tensor(f"cbufa{ch}", [128, D], F32))
        def _(sy):
            wt(sy, "VE")
            wt(sy, "AC")
            wt(sy, "PE")
            wt(sy, "LD")
            inc(sy.dma_start(out=rpos[:, :], in_=rpos_d[:, ch * CT:(ch + 1) * CT]), "LD")
        sec("sync", _)

        for t in range(NT):
            xt = xsb[:, t * D:(t + 1) * D]

            def _(a, t=t):
                wt(a, "VE")
                wt(a, "TP")
                a.activation(cbufa[:, :], xt, AF.Square, accum_out=stats[:, 1:2])
                a.activation(u2b[:, :], xt, AF.Identity, accum_out=stats[:, 0:1])
                a.drain()
                a.activation(stats[:, 2:3], stats[:, 0:1], AF.Identity, scale=1.0 / D)
                a.activation(stats[:, 9:10], stats[:, 0:1], AF.Identity, scale=-1.0 / D)
                a.activation(stats[:, 3:4], stats[:, 1:2], AF.Identity,
                             scale=1.0 / D, bias=scrub[:, 1:2])
                a.drain()
                a.activation(stats[:, 8:9], stats[:, 2:3], AF.Square)
                a.drain()
                a.activation(stats[:, 4:5], stats[:, 8:9], AF.Identity,
                             scale=-1.0, bias=stats[:, 3:4])
                a.drain()
                a.activation(stats[:, 5:6], stats[:, 4:5], AF.Ln)
                a.drain()
                a.activation(stats[:, 6:7], stats[:, 5:6], AF.Exp, scale=-0.5)
                a.drain()
                a.activation(stats[:, 7:8], stats[:, 6:7], AF.Identity,
                             scale=stats[:, 9:10])
                a.drain()
                inc(a.activation(u2b[:, :], xt, AF.Identity,
                                 bias=stats[:, 7:8], scale=stats[:, 6:7]), "AC")
            sec("scalar", _)

            def _(sy, t=t):
                wt(sy, "AC")
                wt(sy, "LD")
                wt(sy, "SP")
                wt(sy, "OUT")
                wt(sy, "GP")
                dst = u2T[:, :].rearrange("p (c j) -> p c j", c=KD)[:, :, t * 128:(t + 1) * 128]
                wt(sy, "TP")
                inc(sy.dma_start(out=dst, in_=u2b[:, :], transpose=True), "TP")
                wt(sy, "TP")
            sec("sync", _)

        def _(sy):
            wt(sy, "AC")
            wt(sy, "LD")
            wt(sy, "SP")
            wt(sy, "OUT")
            wt(sy, "GP")
            wt(sy, "TP")
            wt(sy, "LD")
            inc(sy.dma_start(out=wv[:, :].rearrange("p (c j) -> p c j", c=KD),
                             in_=wqkvT_d.rearrange("p (c j) -> p c j", c=KD)[:, :, 2 * D:3 * D]), "LD")
            wt(sy, "LD")
            inc(sy.dma_start(out=wow[:, :], in_=woT_d), "LD")
        sec("sync", _)
        vw_ld = C["LD"]
        tp_u2 = C["TP"]

        with contextlib.ExitStack() as es2:
            ec2 = es2.enter_context
            psqk0 = ec2(nc.psum_tensor(f"psqk{ch}", [128, 512], F32))
            psqk1 = ec2(nc.psum_tensor(f"psqk2{ch}", [128, 512], F32))
            psv = ec2(nc.psum_tensor(f"psv{ch}", [128, 1024], F32))
            psst = ec2(nc.psum_tensor(f"psst{ch}", [128, 512], F32))
            psqk = [psqk0, psqk1]
            for fc in range(16):
                bk = fc % 2
                def _(sy, fc=fc, bk=bk):
                    if fc >= 2:
                        wt(sy, "PE")
                    src = wqkvT_d.rearrange("p (c j) -> p c j", c=KD)[:, :, fc * 128:(fc + 1) * 128]
                    wt(sy, "LD")
                    inc(sy.dma_start(out=wfc[:, bk * KD * 128:(bk + 1) * KD * 128]
                                     .rearrange("p (c j) -> p c j", c=KD), in_=src), "LD")
                sec("sync", _)

                def _(te, fc=fc, bk=bk):
                    wt(te, "LD")
                    wt(te, "TP", tp_u2)
                    wt(te, "AC")
                    for k in range(KD):
                        ins_ = te.matmul(psqk[bk][:, :],
                                         wfc[:, bk * KD * 128 + k * 128: bk * KD * 128 + (k + 1) * 128],
                                         u2T[:, k * CT:(k + 1) * CT],
                                         start=(k == 0), stop=(k == KD - 1))
                    inc(ins_, "PE")
                sec("tensor", _)

                def _(a, fc=fc, bk=bk):
                    wt(a, "PE")
                    if fc < 8:
                        inc(a.activation(qdm[:, fc * CT:(fc + 1) * CT], psqk[bk][:, :],
                                         AF.Identity, scale=0.125), "AC")
                    else:
                        dst = kdm[:, :].rearrange("p (t j) -> p t j", t=NT)[
                            :, :, (fc - 8) * 128:(fc - 7) * 128]
                        inc(a.activation(dst, psqk[bk][:, :], AF.Sign), "AC")
                sec("scalar", _)
            qk_ac = C["AC"]

            for t in range(NT):
                def _(te, t=t):
                    wt(te, "LD", vw_ld)
                    wt(te, "TP", tp_u2)
                    if t > 0:
                        wt(te, "AC")
                    for half in range(2):
                        for k in range(KD):
                            ins_ = te.matmul(psv[:, half * 512:(half + 1) * 512],
                                             u2T[:, k * CT + t * 128:k * CT + (t + 1) * 128],
                                             wv[:, k * D + half * 512:k * D + half * 512 + 512],
                                             start=(k == 0), stop=(k == KD - 1))
                    inc(ins_, "PE")
                sec("tensor", _)

                def _(a, t=t):
                    wt(a, "PE")
                    inc(a.activation(vtok[:, t * D:(t + 1) * D], psv[:, :], AF.Sign), "AC")
                sec("scalar", _)

            def _(sy):
                wt(sy, "AC", qk_ac)
                wt(sy, "LD")
                wt(sy, "SP")
                wt(sy, "OUT")
                wt(sy, "GP")
                for t in range(NT):
                    wt(sy, "TP")
                    inc(sy.dma_start(out=ktok[:, t * D:(t + 1) * D]
                                     .rearrange("p (c j) -> p c j", c=8),
                                     in_=kdm[:, t * D:(t + 1) * D], transpose=True), "TP")
                    wt(sy, "TP")
            sec("sync", _)
            tp_k = C["TP"]

            with contextlib.ExitStack() as esp:
                spre = esp.enter_context(nc.sbuf_tensor(f"spre{ch}", [128, 512], F32))
                def _(v):
                    inc(v.memset(spre[:, :], 0.0), "VE")
                sec("vector", _)

                for t in range(NT):
                    def _(te, t=t):
                        wt(te, "TP", tp_k)
                        wt(te, "AC")
                        wt(te, "VE")
                        for h in range(H):
                            pb = 64 * (h % 2)
                            fs = (h // 2) * 64
                            ins_ = te.matmul(psst[pb:pb + 64, fs:fs + 64],
                                             ktok[:, t * D + h * 64:t * D + (h + 1) * 64],
                                             vtok[:, t * D + h * 64:t * D + (h + 1) * 64],
                                             start=True, stop=True,
                                             skip_group_check=True)
                        inc(ins_, "PE")
                    sec("tensor", _)

                    def _(v, t=t):
                        wt(v, "PE")
                        v.tensor_copy(sprt[:, t * 512:(t + 1) * 512], spre[:, :])
                        v.drain()
                        inc(v.tensor_tensor(spre[:, :], psst[:, :], spre[:, :],
                                            AluOpType.add), "VE")
                    sec("vector", _)

                def _(v):
                    v.drain()
                    inc(v.tensor_copy(ccs[:, :], spre[:, :]), "VE")
                sec("vector", _)

                def _(sy):
                    wt(sy, "VE")
                    wt(sy, "CC")
                    wt(sy, "SP")
                    inc(sy.dma_start(out=cc_in, in_=ccs[:, :]), "SP")
                sec("sync", _)

                def _(g):
                    wt(g, "SP")
                    wt(g, "LD")
                    wt(g, "CC")
                    inc(g.collective_compute(
                        "AllGather", AluOpType.bypass,
                        replica_groups=[[0, 1, 2, 3], [4, 5, 6, 7]],
                        ins=[cc_in], outs=[cc_out]), "CC")
                sec("gpsimd", _)

                def _(sy):
                    wt(sy, "CC")
                    wt(sy, "LD")
                    inc(sy.dma_start(out=gath[:, :].rearrange("p (j f) -> p j f", j=4),
                                     in_=cc_out.rearrange("(j p) f -> p j f", p=128)), "LD")
                sec("sync", _)

                def _(v):
                    wt(v, "LD")
                    v.tensor_copy(sinc[:, :], sbase[:, :])
                    for j in range(4):
                        v.drain()
                        v.scalar_tensor_tensor(sinc[:, :], gath[:, j * 512:(j + 1) * 512],
                                               mska[:, j:j + 1], sinc[:, :],
                                               AluOpType.mult, AluOpType.add)
                    for j in range(4):
                        v.drain()
                        v.scalar_tensor_tensor(sbase[:, :], gath[:, j * 512:(j + 1) * 512],
                                               1.0, sbase[:, :], AluOpType.mult, AluOpType.add)
                    v.drain()
                    for t in range(NT):
                        ins_ = v.tensor_tensor(stot[:, t * 512:(t + 1) * 512],
                                               sprt[:, t * 512:(t + 1) * 512], sinc[:, :],
                                               AluOpType.add)
                    inc(ins_, "VE")
                sec("vector", _)
                s_ready = C["VE"]
                es2.close()
                if _LIMIT == "track":
                    def _(sy):
                        wt(sy, "VE")
                        wt(sy, "OUT")
                        inc(sy.dma_start(out=dbg_aps["dbg"].rearrange("(t p) j -> p t j", p=128),
                                         in_=xsb[:, :].rearrange("p (t j) -> p t j", t=NT)), "OUT")
                        wt(sy, "OUT")
                        inc(sy.dma_start(out=dbg_aps["dbg2"][:, 0:512], in_=sprt[:, 0:512]), "OUT")
                    sec("sync", _)
                    return

                with contextlib.ExitStack() as es3:
                    ec3 = es3.enter_context
                    pssc0 = ec3(nc.psum_tensor(f"pssc{ch}", [128, 512], F32))
                    pssc1 = ec3(nc.psum_tensor(f"pssc2{ch}", [128, 512], F32))
                    pso4 = [ec3(nc.psum_tensor(f"pso{ch}_{i}", [128, 512], F32))
                            for i in range(4)]
                    pssc = [pssc0, pssc1]
                    for t in range(NT):
                        for p in range(2):
                            for q in range(2):
                                def _(te, t=t, p=p, q=q):
                                    wt(te, "VE")
                                    if p == 0 and q == 0:
                                        wt(te, "AC")
                                    for jj in range(4):
                                        j = 4 * q + jj
                                        h = 2 * j + p
                                        pb = 64 * p
                                        fq = j * CT + t * 128
                                        fk = t * D + j * 128
                                        ins_ = te.matmul(pssc[p][:, jj * 128:(jj + 1) * 128],
                                                         kdm[pb:pb + 64, fk:fk + 128],
                                                         qdm[pb:pb + 64, fq:fq + 128],
                                                         start=True, stop=True,
                                                         skip_group_check=True)
                                    inc(ins_, "PE")
                                sec("tensor", _)

                                def _(v, t=t, p=p, q=q):
                                    wt(v, "PE")
                                    inc(v.tensor_tensor(scb[:, :], pssc[p][:, :], triu4[:, :],
                                                        AluOpType.mult), "VE")
                                sec("vector", _)
                                if _ALIMIT == "sc" and t == 0 and p == 0 and q == 0:
                                    def _(g):
                                        wt(g, "VE")
                                        inc(g.dma_start(out=dbg_aps["dbg"][0:128, 0:512], in_=scb[:, :]), "OUT")
                                    sec("gpsimd", _)
                                    return

                                def _(te, t=t, p=p, q=q):
                                    wt(te, "VE")
                                    for jj in range(4):
                                        j = 4 * q + jj
                                        h = 2 * j + p
                                        pb = 64 * p
                                        fq = j * CT + t * 128
                                        te.matmul(pso4[2 * p + q][pb:pb + 64, jj * 128:(jj + 1) * 128],
                                                  vtok[:, t * D + h * 64:t * D + (h + 1) * 64],
                                                  scb[:, jj * 128:(jj + 1) * 128],
                                                  start=True, stop=False,
                                                  skip_group_check=True)
                                        ins_ = te.matmul(pso4[2 * p + q][pb:pb + 64, jj * 128:(jj + 1) * 128],
                                                  stot[pb:pb + 64, t * 512 + j * 64:
                                                       t * 512 + j * 64 + 64],
                                                  qdm[pb:pb + 64, fq:fq + 128],
                                                  start=False, stop=True,
                                                  skip_group_check=True)
                                    inc(ins_, "PE")
                                sec("tensor", _)
                                if _ALIMIT == "ic" and t == 0 and p == 0 and q == 0:
                                    def _(v):
                                        wt(v, "PE")
                                        inc(v.tensor_copy(sprt[0:64, 0:512], pso4[0][0:64, :]), "VE")
                                    sec("vector", _)
                                    def _(g):
                                        wt(g, "VE")
                                        inc(g.dma_start(out=dbg_aps["dbg"][0:64, 0:512], in_=sprt[0:64, 0:512]), "OUT")
                                    sec("gpsimd", _)
                                    return

                        def _(a, t=t):
                            wt(a, "PE")
                            inc(a.nop(), "AC")
                        sec("scalar", _)

                        def _(v, t=t):
                            wt(v, "AC")
                            rp = rpos[:, t * 128:(t + 1) * 128]
                            for p in range(2):
                                for q in range(2):
                                    pb = 64 * p
                                    rpb = rp[pb:pb + 64, :].unsqueeze(1).broadcast_to((64, 4, 128))
                                    dst = attb[pb:pb + 64,
                                               t * KD * 128 + q * 4 * 128:
                                               t * KD * 128 + (q * 4 + 4) * 128]
                                    ins_ = v.tensor_tensor(
                                        dst.rearrange("p (s i) -> p s i", s=4),
                                        pso4[2 * p + q][pb:pb + 64, :].rearrange("p (s i) -> p s i", s=4),
                                        rpb, AluOpType.mult)
                            inc(ins_, "VE")
                        sec("vector", _)
                        def _(a, t=t):
                            wt(a, "VE")
                            inc(a.nop(), "AC")
                        sec("scalar", _)

            if _LIMIT == "main":
                def _(g):
                    wt(g, "VE")
                    wt(g, "OUT")
                    inc(g.dma_start(out=dbg_aps["dbg"].rearrange("(a p) j -> p a j", p=128),
                                    in_=attb[:, :].rearrange("p (a j) -> p a j", a=NT)), "OUT")
                sec("gpsimd", _)
                return

            with contextlib.ExitStack() as esy:
                psy = esy.enter_context(nc.psum_tensor(f"psy{ch}", [128, 1024], F32))
                for t in range(NT):
                    def _(te, t=t):
                        wt(te, "VE")
                        wt(te, "LD", vw_ld)
                        if t > 0:
                            wt(te, "AC")
                        for half in range(2):
                            for k in range(KD):
                                ins_ = te.matmul(psy[:, half * 512:(half + 1) * 512],
                                                 attb[:, (t * KD + k) * 128:(t * KD + k + 1) * 128],
                                                 wow[:, k * D + half * 512:k * D + half * 512 + 512],
                                                 start=(k == 0), stop=(k == KD - 1))
                        inc(ins_, "PE")
                    sec("tensor", _)

                    def _(v, t=t):
                        wt(v, "PE")
                        xt = xsb[:, t * D:(t + 1) * D]
                        inc(v.tensor_tensor(xt, psy[:, :], xt, AluOpType.add), "VE")
                    sec("vector", _)

                    def _(a, t=t):
                        wt(a, "VE")
                        inc(a.nop(), "AC")
                    sec("scalar", _)

    if _LIMIT == "attn":
        def _(sy):
            wt(sy, "VE")
            wt(sy, "OUT")
            inc(sy.dma_start(out=dbg_aps["dbg"].rearrange("(t p) j -> p t j", p=128),
                             in_=xsb[:, :].rearrange("p (t j) -> p t j", t=NT)), "OUT")
        sec("sync", _)
        return
    ffn_phase(f"f2c{ch}", "out")

    if ch + 1 < NCH:
        def _(sy):
            wt(sy, "VE")
            wt(sy, "LD")
            inc(sy.dma_start(
                out=xsb[:, :].rearrange("p (t j) -> p t j", t=NT),
                in_=x_d[(ch + 1) * CT:(ch + 2) * CT, :].rearrange("(t p) j -> p t j", p=128)), "LD")
        sec("sync", _)

        def _(v):
            wt(v, "LD")
            inc(v.nop(), "VE")
        sec("vector", _)



def _spc(v, scrub, src):
    v.tensor_copy(scrub[:, 2:3], src)

_NC_CACHE = None
_NC_KEY = 'x'


def kernel(L0_attn_x, L0_attn_W_qkv, L0_attn_W_o, L0_ffn_w_gate, L0_ffn_w_up,
           L0_ffn_w_down):
    global _NC_CACHE
    x = np.ascontiguousarray(np.asarray(L0_attn_x, dtype=np.float32)).reshape(B * T, D)
    wqkv = np.ascontiguousarray(np.asarray(L0_attn_W_qkv, dtype=np.float32))
    wo = np.ascontiguousarray(np.asarray(L0_attn_W_o, dtype=np.float32))
    wg = np.ascontiguousarray(np.asarray(L0_ffn_w_gate, dtype=np.float32))
    wu = np.ascontiguousarray(np.asarray(L0_ffn_w_up, dtype=np.float32))
    wd = np.ascontiguousarray(np.asarray(L0_ffn_w_down, dtype=np.float32))

    global _NC_KEY
    if _NC_CACHE is None or _NC_KEY != _LIMIT:
        _NC_CACHE = build_nc()
        _NC_KEY = _LIMIT
    nc = _NC_CACHE

    triu = np.tile(np.triu(np.ones((128, 128), dtype=np.float32)), (1, 4))
    wg_pad = np.zeros((3 * 128 * NCORE, D), dtype=np.float32)
    wg_pad[:II] = wg
    wu_pad = np.zeros((3 * 128 * NCORE, D), dtype=np.float32)
    wu_pad[:II] = wu
    in_maps = []
    spans = []
    for c in range(NCORE):
        seq, q = divmod(c, 4)
        starts = [seq * T + ch * (4 * CT) + q * CT for ch in range(NCH)]
        spans.append(starts)
        xo = np.concatenate([x[s:s + CT] for s in starts], axis=0)
        pos = np.concatenate(
            [np.arange(ch * 4 * CT + q * CT + 1, ch * 4 * CT + q * CT + CT + 1)
             for ch in range(NCH)]).astype(np.float32)
        rpos = np.broadcast_to((1.0 / pos)[None, :], (128, NCH * CT)).copy()
        mask = np.broadcast_to((np.arange(4) < q).astype(np.float32)[None, :], (128, 4)).copy()
        in_maps.append({
            "x": xo, "wqkv": wqkv, "wo": wo,
            "wg": wg_pad[c * 384:(c + 1) * 384],
            "wu": wu_pad[c * 384:(c + 1) * 384],
            "wd": wd[c * 128:(c + 1) * 128],
            "rposd": rpos, "triu4": triu, "maska": mask,
        })

    res = run_bass_kernel_spmd(nc, in_maps, core_ids=list(range(NCORE)))
    global _RES
    _RES = res
    out = np.empty((B * T, D), dtype=np.float32)
    for c in range(NCORE):
        yc = res.results[c]["y"]
        for ch in range(NCH):
            out[spans[c][ch]:spans[c][ch] + CT] = yc[ch * CT:(ch + 1) * CT]
    return out.reshape(B, T, D)


# version 05
# speedup vs baseline: 1.1485x; 1.0161x over previous
import numpy as np
import ml_dtypes
import concourse.bacc as bacc
import concourse.mybir as mybir
import concourse.dve_ops as dve_ops
from concourse.dve_op import DveOp
from concourse.dve_spec import (
    Spec, Src0, Src1, C0, C1, C2, Zero, One, maxx, minn, lower, Bin, AluOp,
)
from concourse.alu_op_type import AluOpType
from concourse.bass_utils import run_bass_kernel_spmd

F32 = mybir.dt.float32
BF16 = mybir.dt.bfloat16
AF = mybir.ActivationFunctionType
AX = mybir.AxisListType.X
MAGIC = float(2 ** 23)

def _mk_op(name, spec, reference):
    sha = lower(spec, ver="v3", perf_en=False, name=name).sha("v3")
    return DveOp(name, spec, subdim=False, uops_sha={"v3": sha}, reference=reference)

_t = Bin(AluOp.ABSOLUTE_VALUE, Src0, Src0)
_two = One + One
_w = minn(minn(_t + _t, _t + _two), _t * C0 + C1)
FP4A_SPEC = Spec(body=_w + C2)

def _ref_fp4a(in0, in1, s0, s1, imm2):
    t = np.abs(in0)
    w = np.minimum(np.minimum(t + t, t + 2.0), t * 0.5 + 4.0)
    return (np.ceil(w - 0.5) + imm2).astype(in0.dtype)

_g = Src0 >= C0
SIGNSCALE_SPEC = Spec(body=((_g + _g) - C1) * Src1)

def _ref_signscale(in0, in1, s0, s1, imm2):
    return (np.where(in0 >= 0, 1.0, -1.0) * in1).astype(in0.dtype)

_p1 = Src0 + Src0
_p2 = _p1 + _p1
FP4B_SPEC = Spec(body=maxx(maxx(Src0 - C2, _p1 - C0), _p2 - C1) * Src1)

def _ref_fp4b(in0, in1, s0, s1, imm2):
    i = in0 - imm2
    q2 = np.maximum(np.maximum(i, i + i - 4.0), 4.0 * i - 16.0)
    return (q2 * in1).astype(in0.dtype)

FP4A = _mk_op("ANT_FP4A", FP4A_SPEC, _ref_fp4a)
SIGNSCALE = _mk_op("ANT_SIGNSCALE", SIGNSCALE_SPEC, _ref_signscale)
FP4B = _mk_op("ANT_FP4B", FP4B_SPEC, _ref_fp4b)

def _register_ops():
    names = [o.name for o in dve_ops.OPS]
    if "ANT_FP4A" in names:
        return
    i0 = names.index("DVEHASH32L_SEED")
    for j, op in enumerate([FP4A, SIGNSCALE, FP4B]):
        old = dve_ops.OPS[i0 + j]
        dve_ops.OPS[i0 + j] = op
        dve_ops._SUB_OPCODE_FOR_NAME[op.name] = dve_ops._SUB_OPCODE_FOR_NAME[old.name]
        dve_ops.CUSTOM_DVE_REFS[op.name] = op.reference
        dve_ops.CUSTOM_DVE_SPECS[op.name] = op.spec

_register_ops()

B, T, D = 2, 8192, 1024
II = 2816
H, DH = 16, 64
NCH = 4
CT = 512
NT = CT // 128
NCORE = 8
EPS = 1e-5
KD = D // 128
KI = II // 128
OC = [(0, 512), (512, 512), (1024, 512), (1536, 512), (2048, 512), (2560, 256)]

_DEBUG = False
_LIMIT = None
_FSTAGE = None
_ALIMIT = None


def build_nc():
    nc = bacc.Bacc("TRN2", target_bir_lowering=False, debug=False)

    x_d = nc.dram_tensor("x", [NCH * CT, D], F32, kind="ExternalInput").ap()
    wqkv_d = nc.dram_tensor("wqkv", [3 * D, D], F32, kind="ExternalInput").ap()
    wo_d = nc.dram_tensor("wo", [D, D], F32, kind="ExternalInput").ap()
    wg_d = nc.dram_tensor("wg", [3 * 128, D], F32, kind="ExternalInput").ap()
    wu_d = nc.dram_tensor("wu", [3 * 128, D], F32, kind="ExternalInput").ap()
    wd_d = nc.dram_tensor("wd", [128, II], F32, kind="ExternalInput").ap()
    rpos_d = nc.dram_tensor("rposd", [128, NCH * CT], F32, kind="ExternalInput").ap()
    triu_d = nc.dram_tensor("triu4", [128, 512], F32, kind="ExternalInput").ap()
    mska_d = nc.dram_tensor("maska", [128, 4], F32, kind="ExternalInput").ap()
    y_d = nc.dram_tensor("y", [NCH * CT, D], F32, kind="ExternalOutput").ap()

    wgT_d = nc.dram_tensor("wgT", [128, KD * II], BF16, kind="Internal").ap()
    wuT_d = nc.dram_tensor("wuT", [128, KD * II], BF16, kind="Internal").ap()
    wdT_d = nc.dram_tensor("wdT", [128, KI * D], BF16, kind="Internal").ap()
    wqkvT_d = nc.dram_tensor("wqkvT", [128, KD * 3 * D], BF16, kind="Internal").ap()
    woT_d = nc.dram_tensor("woT", [128, KD * D], BF16, kind="Internal").ap()
    cc_in = nc.dram_tensor("cc_in", [128, 512], F32, kind="Internal").ap()
    cc_out = nc.dram_tensor("cc_out", [4 * 128, 512], F32, kind="Internal").ap()
    QIN = 3 * KD * 128 * 2 + KI * 128
    wq_in = nc.dram_tensor("wq_in", [128, QIN], BF16, kind="Internal").ap()
    wq_out = nc.dram_tensor("wq_out", [8 * 128, QIN], BF16, kind="Internal").ap()

    dbg_aps = {}
    if _LIMIT in ("ffn1", "track", "attn", "main") or _FSTAGE or _ALIMIT:
        dbg_aps["dbg"] = nc.dram_tensor("dbg", [CT, D], F32, kind="ExternalOutput").ap()
        dbg_aps["dbg2"] = nc.dram_tensor("dbg2", [128, 4 * 512], F32, kind="ExternalOutput").ap()

    C = {}
    SEM = {}

    import contextlib
    st = contextlib.ExitStack()

    def sem(name):
        SEM[name] = st.enter_context(nc.semaphore(name))
        C[name] = 0

    for s in ["LD", "GP", "TP", "SP", "VE", "AC", "PE", "CC", "OUT"]:
        sem(s)

    def inc(instr, s, k=None):
        k = k if k is not None else (16 if s in ("LD", "GP", "TP", "SP", "OUT") else 1)
        instr.then_inc(SEM[s], k)
        C[s] += k

    def wt(eng, s, v=None):
        eng.wait_ge(SEM[s], C[s] if v is None else v)

    with st:
        xsb = st.enter_context(nc.sbuf_tensor("xsb", [128, NT * D], F32))
        xout = st.enter_context(nc.sbuf_tensor("xout", [128, D], F32))
        rpos = st.enter_context(nc.sbuf_tensor("rposc", [128, CT], F32))
        triu4 = st.enter_context(nc.sbuf_tensor("triu4s", [128, 512], F32))
        mska = st.enter_context(nc.sbuf_tensor("mskas", [128, 4], F32))
        sbase = st.enter_context(nc.sbuf_tensor("sbase", [128, 512], F32))
        sinc = st.enter_context(nc.sbuf_tensor("sinc", [128, 512], F32))
        ccs = st.enter_context(nc.sbuf_tensor("ccs", [128, 512], F32))
        stats = st.enter_context(nc.sbuf_tensor("stats", [128, 16], F32))
        scrub = st.enter_context(nc.sbuf_tensor("scrub", [128, 4], F32))
        WgTp = st.enter_context(nc.sbuf_tensor("WgTp", [128, KD * II], BF16))
        WuTp = None
        block = st.enter_context(nc.Block())
        if True:
            def sec(engname, fn):
                getattr(block, engname)(fn)

            import contextlib as _ctx
            with _ctx.ExitStack() as es0:
                ec0 = es0.enter_context
                wf32 = ec0(nc.sbuf_tensor("wf32", [128, II], F32))
                wz = ec0(nc.sbuf_tensor("wz", [128, II], F32))
                wsbf = ec0(nc.sbuf_tensor("wsbf", [128, II], F32))
                wq = ec0(nc.sbuf_tensor("wq", [128, II], BF16))
                wtst = ec0(nc.sbuf_tensor("wtst", [128, II], BF16))
                wmb = ec0(nc.sbuf_tensor("wmb", [128, II // 16], F32))
                wrs = ec0(nc.sbuf_tensor("wrs", [128, II // 16], F32))
                wsb2 = ec0(nc.sbuf_tensor("wsb2", [128, II // 16], F32))
                def _(sy):
                    wt(sy, "LD")
                    wt(sy, "SP")
                    wt(sy, "OUT")
                    wt(sy, "GP")
                    wt(sy, "TP")
                    wt(sy, "LD")
                    inc(sy.dma_start(out=triu4[:, :], in_=triu_d), "LD")
                    wt(sy, "LD")
                    inc(sy.dma_start(out=mska[:, :], in_=mska_d), "LD")
                    wt(sy, "LD")
                    inc(sy.dma_start(
                        out=xsb[:, :].rearrange("p (t j) -> p t j", t=NT),
                        in_=x_d[0:CT, :].rearrange("(t p) j -> p t j", p=128)), "LD")
                sec("sync", _)

                def _(v):
                    v.memset(sbase[:, :], 0.0)
                    inc(v.memset(scrub[:, 1:2], EPS), "VE")
                sec("vector", _)

                def quant_weight(src_d, off, nrows, K):
                    npc = nrows // 128
                    kc = K // 128
                    NB = K // 16
                    for pc in range(npc):
                        def _(sy, pc=pc):
                            wt(sy, "VE")
                            wt(sy, "LD")
                            inc(sy.dma_start(out=wf32[:, 0:K],
                                             in_=src_d[pc * 128:(pc + 1) * 128, :]), "LD")
                        sec("sync", _)

                        def _(v, pc=pc, K=K, NB=NB):
                            wt(v, "LD")
                            wt(v, "TP")
                            rsb = wrs[:, 0:NB].unsqueeze(2).broadcast_to((128, NB, 16))
                            sbb = wsb2[:, 0:NB].unsqueeze(2).broadcast_to((128, NB, 16))
                            v.tensor_reduce(wmb[:, 0:NB],
                                            wf32[:, 0:K].rearrange("p (a b) -> p a b", b=16),
                                            AX, AluOpType.max, apply_absolute_value=True)
                            v.drain()
                            v.tensor_scalar(wmb[:, 0:NB], wmb[:, 0:NB], 6e-12, None, AluOpType.max)
                            v.drain()
                            v.reciprocal(wrs[:, 0:NB], wmb[:, 0:NB])
                            v.tensor_scalar(wsb2[:, 0:NB], wmb[:, 0:NB], 1.0 / 12.0, None, AluOpType.mult)
                            v.drain()
                            v.tensor_scalar(wrs[:, 0:NB], wrs[:, 0:NB], 6.0, None, AluOpType.mult)
                            v.tensor_copy(wsbf[:, 0:K].rearrange("p (a b) -> p a b", b=16), sbb)
                            v.drain()
                            v.tensor_tensor(wz[:, 0:K].rearrange("p (a b) -> p a b", b=16),
                                            wf32[:, 0:K].rearrange("p (a b) -> p a b", b=16),
                                            rsb, AluOpType.mult)
                            v._custom_dve(SIGNSCALE, out=wsbf[:, 0:K], in0=wf32[:, 0:K],
                                          in1=wsbf[:, 0:K], s0=0.0, s1=1.0)
                            v.drain()
                            v._custom_dve(FP4A, out=wz[:, 0:K], in0=wz[:, 0:K],
                                          s0=0.5, s1=4.0, imm2=MAGIC)
                            v.drain()
                            inc(v._custom_dve(FP4B, out=wq[:, 0:K], in0=wz[:, 0:K],
                                              in1=wsbf[:, 0:K], s0=2 * MAGIC + 4.0,
                                              s1=4 * MAGIC + 16.0, imm2=MAGIC), "VE")
                        sec("vector", _)

                        def _(sy, pc=pc, K=K, kc=kc, off=off):
                            wt(sy, "VE")
                            wt(sy, "LD")
                            wt(sy, "SP")
                            wt(sy, "OUT")
                            wt(sy, "GP")
                            wt(sy, "TP")
                            inc(sy.dma_start(
                                out=wtst[:, 0:K].rearrange("p (c j) -> p c j", c=kc),
                                in_=wq[:, 0:K], transpose=True), "TP")
                            wt(sy, "TP")
                            wt(sy, "SP")
                            inc(sy.dma_start(
                                out=wq_in[:, off + pc * K:off + (pc + 1) * K],
                                in_=wtst[:, 0:K]), "SP")
                        sec("sync", _)

                quant_weight(wg_d, 0, 3 * 128, D)
                quant_weight(wu_d, 3 * KD * 128, 3 * 128, D)
                quant_weight(wd_d, 6 * KD * 128, 128, II)

                def cast_weight(src_d, dst_d, nrows):
                    npc = nrows // 128
                    for pc in range(npc):
                        def _(g, pc=pc):
                            wt(g, "TP")
                            wt(g, "GP")
                            inc(g.dma_start(out=wq[:, 0:D],
                                            in_=src_d[pc * 128:(pc + 1) * 128, :]), "GP")
                        sec("gpsimd", _)

                        def _(sy, pc=pc, nrows=nrows):
                            wt(sy, "GP")
                            wt(sy, "LD")
                            wt(sy, "SP")
                            wt(sy, "OUT")
                            wt(sy, "TP")
                            inc(sy.dma_start(
                                out=wtst[:, 0:D].rearrange("p (c j) -> p c j", c=KD),
                                in_=wq[:, 0:D], transpose=True), "TP")
                            wt(sy, "TP")
                            dst = dst_d.rearrange("p (c r) -> p c r", c=KD)[:, :, pc * 128:(pc + 1) * 128]
                            wt(sy, "SP")
                            inc(sy.dma_start(out=dst,
                                             in_=wtst[:, 0:D].rearrange("p (c j) -> p c j", c=KD)), "SP")
                        sec("sync", _)

                def _(g):
                    wt(g, "SP")
                    wt(g, "CC")
                    inc(g.collective_compute(
                        "AllGather", AluOpType.bypass,
                        replica_groups=[[0, 1, 2, 3, 4, 5, 6, 7]],
                        ins=[wq_in], outs=[wq_out]), "CC")
                sec("gpsimd", _)

                cast_weight(wqkv_d, wqkvT_d, 3 * D)
                cast_weight(wo_d, woT_d, D)

                def _(sy):
                    wt(sy, "CC")
                    k = 0
                    for dst_d2, off, kc2, nt, spc in (
                            (WgTp, 0, KD, 22, 3),
                            (wuT_d, 3 * KD * 128, KD, 22, 3),
                            (wdT_d, 6 * KD * 128, KI, 8, 1)):
                        for t in range(nt):
                            j, s = divmod(t, spc)
                            src = wq_out[j * 128:(j + 1) * 128,
                                         off + s * kc2 * 128:
                                         off + (s + 1) * kc2 * 128].rearrange(
                                "p (c r) -> p c r", c=kc2)
                            if dst_d2 is WgTp:
                                dst = dst_d2[:, :].rearrange(
                                    "p (c r) -> p c r", c=kc2)[:, :, t * 128:(t + 1) * 128]
                            else:
                                dst = dst_d2.rearrange(
                                    "p (c r) -> p c r", c=kc2)[:, :, t * 128:(t + 1) * 128]
                            ch_sem = "SP" if k % 2 == 0 else "LD"
                            k += 1
                            wt(sy, ch_sem)
                            inc(sy.dma_start(out=dst, in_=src), ch_sem)
                sec("sync", _)
                C["_WGU_SP"] = C["SP"]
                C["_WGU_LD"] = C["LD"]

            nch = 0 if _LIMIT == "setup" else (1 if _LIMIT in ("ffn1", "track", "attn", "chunk1", "main") else NCH)
            for ch in range(nch):
                chunk(nc, block, sec, C, SEM, inc, wt, ch,
                      x_d, y_d, wgT_d, wuT_d, wdT_d, wqkvT_d, woT_d,
                      cc_in, cc_out, rpos_d,
                      xsb, xout, rpos, triu4, mska, sbase, sinc, ccs,
                      stats, scrub, dbg_aps, WgTp, WuTp)

            def _(sy):
                wt(sy, "OUT")
            sec("sync", _)

    nc.finalize()
    return nc


def chunk(nc, block, sec, C, SEM, inc, wt, ch,
          x_d, y_d, wgT_d, wuT_d, wdT_d, wqkvT_d, woT_d, cc_in, cc_out, rpos_d,
          xsb, xout, rpos, triu4, mska, sbase, sinc, ccs, stats, scrub,
          dbg_aps, WgTp, WuTp):
    F32 = mybir.dt.float32
    BF16 = mybir.dt.bfloat16

    def ffn_phase(tag, resid_scale_out):
        import contextlib
        with contextlib.ExitStack() as es:
            ec = es.enter_context
            WgT = WgTp
            WuT = ec(nc.sbuf_tensor(f"WuT{tag}", [128, KD * II], BF16))
            WdT = ec(nc.sbuf_tensor(f"WdT{tag}", [128, KI * D], BF16))
            cbuf = ec(nc.sbuf_tensor(f"cbuf{tag}", [128, D], F32))
            zbuf = ec(nc.sbuf_tensor(f"zbuf{tag}", [128, D], F32))
            sbf = ec(nc.sbuf_tensor(f"sbf{tag}", [128, D], F32))
            xq = ec(nc.sbuf_tensor(f"xq{tag}", [128, D], BF16))
            xqT = ec(nc.sbuf_tensor(f"xqT{tag}", [128, KD * 128], BF16))
            sgb = ec(nc.sbuf_tensor(f"sg{tag}", [128, 512], F32))
            hb = ec(nc.sbuf_tensor(f"hb{tag}", [128, 512], F32))
            zh = ec(nc.sbuf_tensor(f"zh{tag}", [128, 512], F32))
            hq = ec(nc.sbuf_tensor(f"hq{tag}", [128, II], BF16))
            hqT = ec(nc.sbuf_tensor(f"hqT{tag}", [128, KI * 128], BF16))
            mb = ec(nc.sbuf_tensor(f"mb{tag}", [128, 64], F32))
            rs = ec(nc.sbuf_tensor(f"rs{tag}", [128, 64], F32))
            sb2 = ec(nc.sbuf_tensor(f"sb2{tag}", [128, 64], F32))
            psg0 = ec(nc.psum_tensor(f"psg{tag}", [128, 512], F32))
            psg1 = ec(nc.psum_tensor(f"psg2{tag}", [128, 512], F32))
            psu0 = ec(nc.psum_tensor(f"psu{tag}", [128, 512], F32))
            psu1 = ec(nc.psum_tensor(f"psu2{tag}", [128, 512], F32))
            psd = ec(nc.psum_tensor(f"psd{tag}", [128, 1024], F32))
            psg = [psg0, psg1]
            psu = [psu0, psu1]

            x_ld = C["LD"]
            def _(sy):
                wt(sy, "VE")
                wt(sy, "AC")
                wt(sy, "PE")
                wt(sy, "SP")
                wt(sy, "LD")
                inc(sy.dma_start(out=WuT[:, :], in_=wuT_d), "LD")
                wt(sy, "LD")
                inc(sy.dma_start(out=WdT[:, :], in_=wdT_d), "LD")
            sec("sync", _)
            w_ld = C["LD"]

            fstage = _FSTAGE if tag.startswith("f1c0") else None
            ntt = 1 if fstage else NT
            for t in range(ntt):
                xt = xsb[:, t * D:(t + 1) * D]

                def _(a, t=t):
                    wt(a, "VE")
                    if t == 0:
                        wt(a, "LD", x_ld)
                    a.activation(zbuf[:, :], xt, AF.Square, accum_out=stats[:, 1:2])
                    a.activation(cbuf[:, :], xt, AF.Identity, accum_out=stats[:, 0:1])
                    a.drain()
                    a.activation(stats[:, 2:3], stats[:, 0:1], AF.Identity, scale=1.0 / D)
                    a.activation(stats[:, 3:4], stats[:, 1:2], AF.Identity,
                                 scale=1.0 / D, bias=scrub[:, 1:2])
                    a.drain()
                    a.activation(stats[:, 8:9], stats[:, 2:3], AF.Square)
                    a.drain()
                    a.activation(stats[:, 4:5], stats[:, 8:9], AF.Identity,
                                 scale=-1.0, bias=stats[:, 3:4])
                    a.drain()
                    a.activation(stats[:, 5:6], stats[:, 4:5], AF.Ln)
                    a.drain()
                    inc(a.activation(stats[:, 6:7], stats[:, 5:6], AF.Exp, scale=-0.5), "AC")
                sec("scalar", _)
                if fstage == "ln":
                    def _(sy):
                        wt(sy, "AC")
                        wt(sy, "OUT")
                        inc(sy.dma_start(out=dbg_aps["dbg2"][:, 0:16], in_=stats[:, :]), "OUT")
                    sec("sync", _)
                    return

                def _(v, t=t):
                    wt(v, "AC")
                    rsb = rs[:, :].unsqueeze(2).broadcast_to((128, 64, 16))
                    sbb = sb2[:, :].unsqueeze(2).broadcast_to((128, 64, 16))
                    v.tensor_scalar(cbuf[:, :], xt, stats[:, 2:3], None, AluOpType.subtract)
                    v.drain()
                    v.tensor_reduce(mb[:, :], cbuf[:, :].rearrange("p (a b) -> p a b", b=16),
                                    AX, AluOpType.max, apply_absolute_value=True)
                    v.drain()
                    v.tensor_scalar(mb[:, :], mb[:, :], 6e-12, None, AluOpType.max)
                    v.drain()
                    v.reciprocal(rs[:, :], mb[:, :])
                    v.tensor_scalar(sb2[:, :], mb[:, :], stats[:, 6:7], 1.0 / 12.0,
                                    AluOpType.mult, AluOpType.mult)
                    v.drain()
                    v.tensor_scalar(rs[:, :], rs[:, :], 6.0, None, AluOpType.mult)
                    v.tensor_copy(sbf[:, :].rearrange("p (a b) -> p a b", b=16), sbb)
                    v.drain()
                    v.tensor_tensor(zbuf[:, :].rearrange("p (a b) -> p a b", b=16),
                                    cbuf[:, :].rearrange("p (a b) -> p a b", b=16),
                                    rsb, AluOpType.mult)
                    v._custom_dve(SIGNSCALE, out=cbuf[:, :], in0=cbuf[:, :], in1=sbf[:, :],
                                  s0=0.0, s1=1.0)
                    v.drain()
                    v._custom_dve(FP4A, out=zbuf[:, :], in0=zbuf[:, :], s0=0.5, s1=4.0, imm2=MAGIC)
                    v.drain()
                    inc(v._custom_dve(FP4B, out=xq[:, :], in0=zbuf[:, :], in1=cbuf[:, :],
                                      s0=2 * MAGIC + 4.0, s1=4 * MAGIC + 16.0, imm2=MAGIC), "VE")
                sec("vector", _)

                def _(sy, t=t):
                    wt(sy, "VE")
                    wt(sy, "LD")
                    wt(sy, "SP")
                    wt(sy, "OUT")
                    wt(sy, "GP")
                    wt(sy, "TP")
                    inc(sy.dma_start(out=xqT[:, :].rearrange("p (c j) -> p c j", c=KD),
                                     in_=xq[:, :], transpose=True), "TP")
                    wt(sy, "TP")
                sec("sync", _)
                tp_xq = C["TP"]
                if fstage == "quant":
                    def _(v):
                        wt(v, "TP")
                        inc(v.tensor_copy(sbf[:, :], xq[:, :]), "VE")
                    sec("vector", _)
                    def _(sy):
                        wt(sy, "VE")
                        wt(sy, "OUT")
                        inc(sy.dma_start(out=dbg_aps["dbg"][0:128, :], in_=sbf[:, :]), "OUT")
                        wt(sy, "OUT")
                        inc(sy.dma_start(out=dbg_aps["dbg2"][:, 0:64], in_=mb[:, :]), "OUT")
                    sec("sync", _)
                    return

                if fstage == "wload":
                    def _(a, t=t):
                        wt(a, "LD", w_ld)
                        wt(a, "TP", tp_xq)
                        inc(a.activation(sgb[:, :], WgT[:, 0:512], AF.Identity), "AC")
                    sec("scalar", _)
                    def _(v, t=t):
                        wt(v, "TP", tp_xq)
                        inc(v.tensor_copy(hb[:, 0:128], xqT[:, 0:128]), "VE")
                    sec("vector", _)
                    def _(sy):
                        wt(sy, "AC")
                        wt(sy, "VE")
                        wt(sy, "OUT")
                        inc(sy.dma_start(out=dbg_aps["dbg"][0:128, 0:512], in_=sgb[:, :]), "OUT")
                        wt(sy, "OUT")
                        inc(sy.dma_start(out=dbg_aps["dbg"][128:256, 0:128], in_=hb[:, 0:128]), "OUT")
                    sec("sync", _)
                    return
                if fstage in ("mm", "mm1", "mmc"):
                    if fstage == "mmc":
                        def _(v, t=t):
                            wt(v, "LD", w_ld)
                            inc(v.tensor_copy(hq[:, 0:512], WgT[:, 0:512]), "VE")
                        sec("vector", _)
                    def _(te, t=t):
                        wt(te, "TP", tp_xq)
                        wt(te, "LD", w_ld)
                        wt(te, "VE")
                        nk = 1 if fstage == "mm1" else KD
                        for k in range(nk):
                            rhs = hq[:, 0:512] if fstage == "mmc" else WgT[:, k * II: k * II + 512]
                            ins_ = te.matmul(psg[0][:, :], xqT[:, k * 128:(k + 1) * 128], rhs,
                                             start=(k == 0), stop=(k == nk - 1))
                        inc(ins_, "PE")
                    sec("tensor", _)
                    def _(a, t=t):
                        wt(a, "PE")
                        inc(a.activation(sgb[:, :], psg[0][:, :], AF.Identity), "AC")
                    sec("scalar", _)
                    def _(sy):
                        wt(sy, "AC")
                        wt(sy, "OUT")
                        inc(sy.dma_start(out=dbg_aps["dbg"][0:128, 0:512], in_=sgb[:, :]), "OUT")
                    sec("sync", _)
                    return

                for oi, (o0, ow) in enumerate(OC):
                    bk = oi % 2
                    nb = ow // 16
                    def _(te, t=t, oi=oi, o0=o0, ow=ow, bk=bk):
                        wt(te, "TP", tp_xq)
                        wt(te, "LD", w_ld)
                        wt(te, "SP", C["_WGU_SP"])
                        wt(te, "VE")
                        wt(te, "AC")
                        for k in range(KD):
                            ins_ = te.matmul(psg[bk][:, 0:ow], xqT[:, k * 128:(k + 1) * 128],
                                             WgT[:, k * II + o0:k * II + o0 + ow],
                                             start=(k == 0), stop=(k == KD - 1))
                        inc(ins_, "PE")
                        for k in range(KD):
                            ins_ = te.matmul(psu[bk][:, 0:ow], xqT[:, k * 128:(k + 1) * 128],
                                             WuT[:, k * II + o0:k * II + o0 + ow],
                                             start=(k == 0), stop=(k == KD - 1))
                        inc(ins_, "PE")
                    sec("tensor", _)

                    def _(a, t=t, oi=oi, o0=o0, ow=ow, bk=bk):
                        wt(a, "PE")
                        inc(a.activation(sgb[:, 0:ow], psg[bk][:, 0:ow], AF.Silu), "AC")
                    sec("scalar", _)

                    def _(v, t=t, oi=oi, o0=o0, ow=ow, bk=bk, nb=nb):
                        wt(v, "AC")
                        rsb = rs[:, 0:nb].unsqueeze(2).broadcast_to((128, nb, 16))
                        sbb = sb2[:, 0:nb].unsqueeze(2).broadcast_to((128, nb, 16))
                        v.tensor_tensor(hb[:, 0:ow], psu[bk][:, 0:ow], sgb[:, 0:ow], AluOpType.mult)
                        v.drain()
                        if fstage == "gu0" and oi == 0:
                            inc(v.tensor_copy(zh[:, 0:ow], hb[:, 0:ow]), "VE")
                            return
                        v.tensor_reduce(mb[:, 0:nb], hb[:, 0:ow].rearrange("p (a b) -> p a b", b=16),
                                        AX, AluOpType.max, apply_absolute_value=True)
                        v.drain()
                        v.tensor_scalar(mb[:, 0:64], mb[:, 0:64], 6e-12, None, AluOpType.max)
                        v.drain()
                        v.reciprocal(rs[:, 0:64], mb[:, 0:64])
                        v.tensor_scalar(sb2[:, 0:64], mb[:, 0:64], 1.0 / 12.0, None, AluOpType.mult)
                        v.drain()
                        v.tensor_scalar(rs[:, 0:64], rs[:, 0:64], 6.0, None, AluOpType.mult)
                        v.tensor_copy(sbf[:, 0:ow].rearrange("p (a b) -> p a b", b=16), sbb)
                        v.drain()
                        v.tensor_tensor(zh[:, 0:ow].rearrange("p (a b) -> p a b", b=16),
                                        hb[:, 0:ow].rearrange("p (a b) -> p a b", b=16),
                                        rsb, AluOpType.mult)
                        v._custom_dve(SIGNSCALE, out=hb[:, 0:ow], in0=hb[:, 0:ow],
                                      in1=sbf[:, 0:ow], s0=0.0, s1=1.0)
                        v.drain()
                        v._custom_dve(FP4A, out=zh[:, 0:ow], in0=zh[:, 0:ow],
                                      s0=0.5, s1=4.0, imm2=MAGIC)
                        v.drain()
                        inc(v._custom_dve(FP4B, out=hq[:, o0:o0 + ow], in0=zh[:, 0:ow],
                                          in1=hb[:, 0:ow], s0=2 * MAGIC + 4.0,
                                          s1=4 * MAGIC + 16.0, imm2=MAGIC), "VE")
                    sec("vector", _)
                    if fstage == "gu0" and oi == 0:
                        def _(sy):
                            wt(sy, "VE")
                            wt(sy, "OUT")
                            inc(sy.dma_start(out=dbg_aps["dbg"][0:128, 0:512], in_=zh[:, 0:512]), "OUT")
                            wt(sy, "OUT")
                            inc(sy.dma_start(out=dbg_aps["dbg"][128:256, 0:512], in_=sgb[:, 0:512]), "OUT")
                        sec("sync", _)
                        return
                    if fstage == "gu" and oi == 0:
                        def _(v):
                            inc(v.tensor_copy(zh[:, :], hq[:, 0:512]), "VE")
                        sec("vector", _)
                        def _(sy):
                            wt(sy, "VE")
                            wt(sy, "OUT")
                            inc(sy.dma_start(out=dbg_aps["dbg"][0:128, 0:512], in_=zh[:, :]), "OUT")
                            wt(sy, "OUT")
                            inc(sy.dma_start(out=dbg_aps["dbg"][128:256, 0:512], in_=sgb[:, :]), "OUT")
                        sec("sync", _)
                        return

                def _(sy, t=t):
                    wt(sy, "VE")
                    wt(sy, "LD")
                    wt(sy, "SP")
                    wt(sy, "OUT")
                    wt(sy, "GP")
                    wt(sy, "TP")
                    inc(sy.dma_start(out=hqT[:, :].rearrange("p (c j) -> p c j", c=KI),
                                     in_=hq[:, :], transpose=True), "TP")
                    wt(sy, "TP")
                sec("sync", _)
                tp_hq = C["TP"]

                def _(te, t=t):
                    wt(te, "TP", tp_hq)
                    wt(te, "VE")
                    for half in range(2):
                        for k in range(KI):
                            ins_ = te.matmul(psd[:, half * 512:(half + 1) * 512],
                                             hqT[:, k * 128:(k + 1) * 128],
                                             WdT[:, k * D + half * 512:k * D + half * 512 + 512],
                                             start=(k == 0), stop=(k == KI - 1))
                    inc(ins_, "PE")
                sec("tensor", _)

                if resid_scale_out is None:
                    def _(v, t=t):
                        wt(v, "PE")
                        inc(v.scalar_tensor_tensor(xt, psd[:, 0:D], 0.5, xt,
                                                   AluOpType.mult, AluOpType.add), "VE")
                    sec("vector", _)
                    if fstage == "down":
                        def _(sy):
                            wt(sy, "VE")
                            wt(sy, "OUT")
                            inc(sy.dma_start(out=dbg_aps["dbg"][0:128, :], in_=xt), "OUT")
                        sec("sync", _)
                        return
                else:
                    def _(v, t=t):
                        wt(v, "PE")
                        wt(v, "OUT")
                        inc(v.scalar_tensor_tensor(xout[:, :], psd[:, 0:D], 0.5, xt,
                                                   AluOpType.mult, AluOpType.add), "VE")
                    sec("vector", _)

                    def _(sy, t=t):
                        wt(sy, "VE")
                        rows = y_d[ch * CT + t * 128: ch * CT + (t + 1) * 128, :]
                        wt(sy, "LD")
                        wt(sy, "SP")
                        wt(sy, "OUT")
                        wt(sy, "GP")
                        wt(sy, "TP")
                        wt(sy, "OUT")
                        inc(sy.dma_start(out=rows, in_=xout[:, :]), "OUT")
                    sec("sync", _)

    ffn_phase(f"f1c{ch}", None)
    if _FSTAGE:
        return
    if _LIMIT == "ffn1":
        def _(sy):
            wt(sy, "VE")
            wt(sy, "OUT")
            inc(sy.dma_start(out=dbg_aps["dbg"].rearrange("(t p) j -> p t j", p=128),
                             in_=xsb[:, :].rearrange("p (t j) -> p t j", t=NT)), "OUT")
        sec("sync", _)
        return

    import contextlib
    with contextlib.ExitStack() as es:
        ec = es.enter_context
        gath = ec(nc.sbuf_tensor(f"gath{ch}", [128, 4 * 512], F32))
        u2T = ec(nc.sbuf_tensor(f"u2T{ch}", [128, KD * CT], BF16))
        u2b = ec(nc.sbuf_tensor(f"u2b{ch}", [128, D], BF16))
        qdm = ec(nc.sbuf_tensor(f"qdm{ch}", [128, 8 * CT], BF16))
        kdm = ec(nc.sbuf_tensor(f"kdm{ch}", [128, 8 * CT], BF16))
        ktok = ec(nc.sbuf_tensor(f"ktok{ch}", [128, NT * D], BF16))
        vtok = ec(nc.sbuf_tensor(f"vtok{ch}", [128, NT * D], BF16))
        wfc = ec(nc.sbuf_tensor(f"wfc{ch}", [128, 2 * KD * 128], BF16))
        wv = ec(nc.sbuf_tensor(f"wv{ch}", [128, KD * D], BF16))
        wow = ec(nc.sbuf_tensor(f"wow{ch}", [128, KD * D], BF16))
        scb = ec(nc.sbuf_tensor(f"scb{ch}", [128, 512], BF16))
        stot = ec(nc.sbuf_tensor(f"stot{ch}", [128, NT * 512], BF16))
        sprt = ec(nc.sbuf_tensor(f"sprt{ch}", [128, NT * 512], F32))
        attb = ec(nc.sbuf_tensor(f"attb{ch}", [128, NT * KD * 128], BF16))
        cbufa = ec(nc.sbuf_tensor(f"cbufa{ch}", [128, D], F32))
        def _(sy):
            wt(sy, "VE")
            wt(sy, "AC")
            wt(sy, "PE")
            wt(sy, "LD")
            inc(sy.dma_start(out=rpos[:, :], in_=rpos_d[:, ch * CT:(ch + 1) * CT]), "LD")
        sec("sync", _)

        for t in range(NT):
            xt = xsb[:, t * D:(t + 1) * D]

            def _(a, t=t):
                wt(a, "VE")
                wt(a, "TP")
                a.activation(cbufa[:, :], xt, AF.Square, accum_out=stats[:, 1:2])
                a.activation(u2b[:, :], xt, AF.Identity, accum_out=stats[:, 0:1])
                a.drain()
                a.activation(stats[:, 2:3], stats[:, 0:1], AF.Identity, scale=1.0 / D)
                a.activation(stats[:, 9:10], stats[:, 0:1], AF.Identity, scale=-1.0 / D)
                a.activation(stats[:, 3:4], stats[:, 1:2], AF.Identity,
                             scale=1.0 / D, bias=scrub[:, 1:2])
                a.drain()
                a.activation(stats[:, 8:9], stats[:, 2:3], AF.Square)
                a.drain()
                a.activation(stats[:, 4:5], stats[:, 8:9], AF.Identity,
                             scale=-1.0, bias=stats[:, 3:4])
                a.drain()
                a.activation(stats[:, 5:6], stats[:, 4:5], AF.Ln)
                a.drain()
                a.activation(stats[:, 6:7], stats[:, 5:6], AF.Exp, scale=-0.5)
                a.drain()
                a.activation(stats[:, 7:8], stats[:, 6:7], AF.Identity,
                             scale=stats[:, 9:10])
                a.drain()
                inc(a.activation(u2b[:, :], xt, AF.Identity,
                                 bias=stats[:, 7:8], scale=stats[:, 6:7]), "AC")
            sec("scalar", _)

            def _(sy, t=t):
                wt(sy, "AC")
                wt(sy, "LD")
                wt(sy, "SP")
                wt(sy, "OUT")
                wt(sy, "GP")
                dst = u2T[:, :].rearrange("p (c j) -> p c j", c=KD)[:, :, t * 128:(t + 1) * 128]
                wt(sy, "TP")
                inc(sy.dma_start(out=dst, in_=u2b[:, :], transpose=True), "TP")
                wt(sy, "TP")
            sec("sync", _)

        def _(sy):
            wt(sy, "AC")
            wt(sy, "LD")
            wt(sy, "SP")
            wt(sy, "OUT")
            wt(sy, "GP")
            wt(sy, "TP")
            wt(sy, "LD")
            inc(sy.dma_start(out=wv[:, :].rearrange("p (c j) -> p c j", c=KD),
                             in_=wqkvT_d.rearrange("p (c j) -> p c j", c=KD)[:, :, 2 * D:3 * D]), "LD")
            wt(sy, "LD")
            inc(sy.dma_start(out=wow[:, :], in_=woT_d), "LD")
        sec("sync", _)
        vw_ld = C["LD"]
        tp_u2 = C["TP"]

        with contextlib.ExitStack() as es2:
            ec2 = es2.enter_context
            psqk0 = ec2(nc.psum_tensor(f"psqk{ch}", [128, 512], F32))
            psqk1 = ec2(nc.psum_tensor(f"psqk2{ch}", [128, 512], F32))
            psv = ec2(nc.psum_tensor(f"psv{ch}", [128, 1024], F32))
            psst = ec2(nc.psum_tensor(f"psst{ch}", [128, 512], F32))
            psqk = [psqk0, psqk1]
            for fc in range(16):
                bk = fc % 2
                def _(sy, fc=fc, bk=bk):
                    if fc >= 2:
                        wt(sy, "PE")
                    src = wqkvT_d.rearrange("p (c j) -> p c j", c=KD)[:, :, fc * 128:(fc + 1) * 128]
                    wt(sy, "LD")
                    inc(sy.dma_start(out=wfc[:, bk * KD * 128:(bk + 1) * KD * 128]
                                     .rearrange("p (c j) -> p c j", c=KD), in_=src), "LD")
                sec("sync", _)

                def _(te, fc=fc, bk=bk):
                    wt(te, "LD")
                    wt(te, "TP", tp_u2)
                    wt(te, "AC")
                    for k in range(KD):
                        ins_ = te.matmul(psqk[bk][:, :],
                                         wfc[:, bk * KD * 128 + k * 128: bk * KD * 128 + (k + 1) * 128],
                                         u2T[:, k * CT:(k + 1) * CT],
                                         start=(k == 0), stop=(k == KD - 1))
                    inc(ins_, "PE")
                sec("tensor", _)

                def _(a, fc=fc, bk=bk):
                    wt(a, "PE")
                    if fc < 8:
                        inc(a.activation(qdm[:, fc * CT:(fc + 1) * CT], psqk[bk][:, :],
                                         AF.Identity, scale=0.125), "AC")
                    else:
                        dst = kdm[:, :].rearrange("p (t j) -> p t j", t=NT)[
                            :, :, (fc - 8) * 128:(fc - 7) * 128]
                        inc(a.activation(dst, psqk[bk][:, :], AF.Sign), "AC")
                sec("scalar", _)
            qk_ac = C["AC"]

            for t in range(NT):
                def _(te, t=t):
                    wt(te, "LD", vw_ld)
                    wt(te, "TP", tp_u2)
                    if t > 0:
                        wt(te, "AC")
                    for half in range(2):
                        for k in range(KD):
                            ins_ = te.matmul(psv[:, half * 512:(half + 1) * 512],
                                             u2T[:, k * CT + t * 128:k * CT + (t + 1) * 128],
                                             wv[:, k * D + half * 512:k * D + half * 512 + 512],
                                             start=(k == 0), stop=(k == KD - 1))
                    inc(ins_, "PE")
                sec("tensor", _)

                def _(a, t=t):
                    wt(a, "PE")
                    inc(a.activation(vtok[:, t * D:(t + 1) * D], psv[:, :], AF.Sign), "AC")
                sec("scalar", _)

            def _(sy):
                wt(sy, "AC", qk_ac)
                wt(sy, "LD")
                wt(sy, "SP")
                wt(sy, "OUT")
                wt(sy, "GP")
                for t in range(NT):
                    wt(sy, "TP")
                    inc(sy.dma_start(out=ktok[:, t * D:(t + 1) * D]
                                     .rearrange("p (c j) -> p c j", c=8),
                                     in_=kdm[:, t * D:(t + 1) * D], transpose=True), "TP")
                    wt(sy, "TP")
            sec("sync", _)
            tp_k = C["TP"]

            with contextlib.ExitStack() as esp:
                spre = esp.enter_context(nc.sbuf_tensor(f"spre{ch}", [128, 512], F32))
                def _(v):
                    inc(v.memset(spre[:, :], 0.0), "VE")
                sec("vector", _)

                for t in range(NT):
                    def _(te, t=t):
                        wt(te, "TP", tp_k)
                        wt(te, "AC")
                        wt(te, "VE")
                        for h in range(H):
                            pb = 64 * (h % 2)
                            fs = (h // 2) * 64
                            ins_ = te.matmul(psst[pb:pb + 64, fs:fs + 64],
                                             ktok[:, t * D + h * 64:t * D + (h + 1) * 64],
                                             vtok[:, t * D + h * 64:t * D + (h + 1) * 64],
                                             start=True, stop=True,
                                             skip_group_check=True)
                        inc(ins_, "PE")
                    sec("tensor", _)

                    def _(v, t=t):
                        wt(v, "PE")
                        v.tensor_copy(sprt[:, t * 512:(t + 1) * 512], spre[:, :])
                        v.drain()
                        inc(v.tensor_tensor(spre[:, :], psst[:, :], spre[:, :],
                                            AluOpType.add), "VE")
                    sec("vector", _)

                def _(v):
                    v.drain()
                    inc(v.tensor_copy(ccs[:, :], spre[:, :]), "VE")
                sec("vector", _)

                def _(sy):
                    wt(sy, "VE")
                    wt(sy, "CC")
                    wt(sy, "SP")
                    inc(sy.dma_start(out=cc_in, in_=ccs[:, :]), "SP")
                sec("sync", _)

                def _(g):
                    wt(g, "SP")
                    wt(g, "LD")
                    wt(g, "CC")
                    inc(g.collective_compute(
                        "AllGather", AluOpType.bypass,
                        replica_groups=[[0, 1, 2, 3], [4, 5, 6, 7]],
                        ins=[cc_in], outs=[cc_out]), "CC")
                sec("gpsimd", _)

                def _(sy):
                    wt(sy, "CC")
                    wt(sy, "LD")
                    inc(sy.dma_start(out=gath[:, :].rearrange("p (j f) -> p j f", j=4),
                                     in_=cc_out.rearrange("(j p) f -> p j f", p=128)), "LD")
                sec("sync", _)

                def _(v):
                    wt(v, "LD")
                    v.tensor_copy(sinc[:, :], sbase[:, :])
                    for j in range(4):
                        v.drain()
                        v.scalar_tensor_tensor(sinc[:, :], gath[:, j * 512:(j + 1) * 512],
                                               mska[:, j:j + 1], sinc[:, :],
                                               AluOpType.mult, AluOpType.add)
                    for j in range(4):
                        v.drain()
                        v.scalar_tensor_tensor(sbase[:, :], gath[:, j * 512:(j + 1) * 512],
                                               1.0, sbase[:, :], AluOpType.mult, AluOpType.add)
                    v.drain()
                    for t in range(NT):
                        ins_ = v.tensor_tensor(stot[:, t * 512:(t + 1) * 512],
                                               sprt[:, t * 512:(t + 1) * 512], sinc[:, :],
                                               AluOpType.add)
                    inc(ins_, "VE")
                sec("vector", _)
                s_ready = C["VE"]
                es2.close()
                if _LIMIT == "track":
                    def _(sy):
                        wt(sy, "VE")
                        wt(sy, "OUT")
                        inc(sy.dma_start(out=dbg_aps["dbg"].rearrange("(t p) j -> p t j", p=128),
                                         in_=xsb[:, :].rearrange("p (t j) -> p t j", t=NT)), "OUT")
                        wt(sy, "OUT")
                        inc(sy.dma_start(out=dbg_aps["dbg2"][:, 0:512], in_=sprt[:, 0:512]), "OUT")
                    sec("sync", _)
                    return

                with contextlib.ExitStack() as es3:
                    ec3 = es3.enter_context
                    pssc0 = ec3(nc.psum_tensor(f"pssc{ch}", [128, 512], F32))
                    pssc1 = ec3(nc.psum_tensor(f"pssc2{ch}", [128, 512], F32))
                    pso4 = [ec3(nc.psum_tensor(f"pso{ch}_{i}", [128, 512], F32))
                            for i in range(4)]
                    pssc = [pssc0, pssc1]
                    for t in range(NT):
                        for p in range(2):
                            for q in range(2):
                                def _(te, t=t, p=p, q=q):
                                    wt(te, "VE")
                                    if p == 0 and q == 0:
                                        wt(te, "AC")
                                    for jj in range(4):
                                        j = 4 * q + jj
                                        h = 2 * j + p
                                        pb = 64 * p
                                        fq = j * CT + t * 128
                                        fk = t * D + j * 128
                                        ins_ = te.matmul(pssc[p][:, jj * 128:(jj + 1) * 128],
                                                         kdm[pb:pb + 64, fk:fk + 128],
                                                         qdm[pb:pb + 64, fq:fq + 128],
                                                         start=True, stop=True,
                                                         skip_group_check=True)
                                    inc(ins_, "PE")
                                sec("tensor", _)

                                def _(v, t=t, p=p, q=q):
                                    wt(v, "PE")
                                    inc(v.tensor_tensor(scb[:, :], pssc[p][:, :], triu4[:, :],
                                                        AluOpType.mult), "VE")
                                sec("vector", _)
                                if _ALIMIT == "sc" and t == 0 and p == 0 and q == 0:
                                    def _(g):
                                        wt(g, "VE")
                                        inc(g.dma_start(out=dbg_aps["dbg"][0:128, 0:512], in_=scb[:, :]), "OUT")
                                    sec("gpsimd", _)
                                    return

                                def _(te, t=t, p=p, q=q):
                                    wt(te, "VE")
                                    for jj in range(4):
                                        j = 4 * q + jj
                                        h = 2 * j + p
                                        pb = 64 * p
                                        fq = j * CT + t * 128
                                        te.matmul(pso4[2 * p + q][pb:pb + 64, jj * 128:(jj + 1) * 128],
                                                  vtok[:, t * D + h * 64:t * D + (h + 1) * 64],
                                                  scb[:, jj * 128:(jj + 1) * 128],
                                                  start=True, stop=False,
                                                  skip_group_check=True)
                                        ins_ = te.matmul(pso4[2 * p + q][pb:pb + 64, jj * 128:(jj + 1) * 128],
                                                  stot[pb:pb + 64, t * 512 + j * 64:
                                                       t * 512 + j * 64 + 64],
                                                  qdm[pb:pb + 64, fq:fq + 128],
                                                  start=False, stop=True,
                                                  skip_group_check=True)
                                    inc(ins_, "PE")
                                sec("tensor", _)
                                if _ALIMIT == "ic" and t == 0 and p == 0 and q == 0:
                                    def _(v):
                                        wt(v, "PE")
                                        inc(v.tensor_copy(sprt[0:64, 0:512], pso4[0][0:64, :]), "VE")
                                    sec("vector", _)
                                    def _(g):
                                        wt(g, "VE")
                                        inc(g.dma_start(out=dbg_aps["dbg"][0:64, 0:512], in_=sprt[0:64, 0:512]), "OUT")
                                    sec("gpsimd", _)
                                    return

                        def _(a, t=t):
                            wt(a, "PE")
                            inc(a.nop(), "AC")
                        sec("scalar", _)

                        def _(v, t=t):
                            wt(v, "AC")
                            rp = rpos[:, t * 128:(t + 1) * 128]
                            for p in range(2):
                                for q in range(2):
                                    pb = 64 * p
                                    rpb = rp[pb:pb + 64, :].unsqueeze(1).broadcast_to((64, 4, 128))
                                    dst = attb[pb:pb + 64,
                                               t * KD * 128 + q * 4 * 128:
                                               t * KD * 128 + (q * 4 + 4) * 128]
                                    ins_ = v.tensor_tensor(
                                        dst.rearrange("p (s i) -> p s i", s=4),
                                        pso4[2 * p + q][pb:pb + 64, :].rearrange("p (s i) -> p s i", s=4),
                                        rpb, AluOpType.mult)
                            inc(ins_, "VE")
                        sec("vector", _)
                        def _(a, t=t):
                            wt(a, "VE")
                            inc(a.nop(), "AC")
                        sec("scalar", _)

            if _LIMIT == "main":
                def _(g):
                    wt(g, "VE")
                    wt(g, "OUT")
                    inc(g.dma_start(out=dbg_aps["dbg"].rearrange("(a p) j -> p a j", p=128),
                                    in_=attb[:, :].rearrange("p (a j) -> p a j", a=NT)), "OUT")
                sec("gpsimd", _)
                return

            with contextlib.ExitStack() as esy:
                psy = esy.enter_context(nc.psum_tensor(f"psy{ch}", [128, 1024], F32))
                for t in range(NT):
                    def _(te, t=t):
                        wt(te, "VE")
                        wt(te, "LD", vw_ld)
                        if t > 0:
                            wt(te, "AC")
                        for half in range(2):
                            for k in range(KD):
                                ins_ = te.matmul(psy[:, half * 512:(half + 1) * 512],
                                                 attb[:, (t * KD + k) * 128:(t * KD + k + 1) * 128],
                                                 wow[:, k * D + half * 512:k * D + half * 512 + 512],
                                                 start=(k == 0), stop=(k == KD - 1))
                        inc(ins_, "PE")
                    sec("tensor", _)

                    def _(v, t=t):
                        wt(v, "PE")
                        xt = xsb[:, t * D:(t + 1) * D]
                        inc(v.tensor_tensor(xt, psy[:, :], xt, AluOpType.add), "VE")
                    sec("vector", _)

                    def _(a, t=t):
                        wt(a, "VE")
                        inc(a.nop(), "AC")
                    sec("scalar", _)

    if _LIMIT == "attn":
        def _(sy):
            wt(sy, "VE")
            wt(sy, "OUT")
            inc(sy.dma_start(out=dbg_aps["dbg"].rearrange("(t p) j -> p t j", p=128),
                             in_=xsb[:, :].rearrange("p (t j) -> p t j", t=NT)), "OUT")
        sec("sync", _)
        return
    ffn_phase(f"f2c{ch}", "out")

    if ch + 1 < NCH:
        def _(sy):
            wt(sy, "VE")
            wt(sy, "LD")
            inc(sy.dma_start(
                out=xsb[:, :].rearrange("p (t j) -> p t j", t=NT),
                in_=x_d[(ch + 1) * CT:(ch + 2) * CT, :].rearrange("(t p) j -> p t j", p=128)), "LD")
        sec("sync", _)

        def _(v):
            wt(v, "LD")
            inc(v.nop(), "VE")
        sec("vector", _)



def _spc(v, scrub, src):
    v.tensor_copy(scrub[:, 2:3], src)

_NC_CACHE = None
_NC_KEY = 'x'


def kernel(L0_attn_x, L0_attn_W_qkv, L0_attn_W_o, L0_ffn_w_gate, L0_ffn_w_up,
           L0_ffn_w_down):
    global _NC_CACHE
    x = np.ascontiguousarray(np.asarray(L0_attn_x, dtype=np.float32)).reshape(B * T, D)
    wqkv = np.ascontiguousarray(np.asarray(L0_attn_W_qkv, dtype=np.float32))
    wo = np.ascontiguousarray(np.asarray(L0_attn_W_o, dtype=np.float32))
    wg = np.ascontiguousarray(np.asarray(L0_ffn_w_gate, dtype=np.float32))
    wu = np.ascontiguousarray(np.asarray(L0_ffn_w_up, dtype=np.float32))
    wd = np.ascontiguousarray(np.asarray(L0_ffn_w_down, dtype=np.float32))

    global _NC_KEY
    if _NC_CACHE is None or _NC_KEY != _LIMIT:
        _NC_CACHE = build_nc()
        _NC_KEY = _LIMIT
    nc = _NC_CACHE

    triu = np.tile(np.triu(np.ones((128, 128), dtype=np.float32)), (1, 4))
    wg_pad = np.zeros((3 * 128 * NCORE, D), dtype=np.float32)
    wg_pad[:II] = wg
    wu_pad = np.zeros((3 * 128 * NCORE, D), dtype=np.float32)
    wu_pad[:II] = wu
    in_maps = []
    spans = []
    for c in range(NCORE):
        seq, q = divmod(c, 4)
        starts = [seq * T + ch * (4 * CT) + q * CT for ch in range(NCH)]
        spans.append(starts)
        xo = np.concatenate([x[s:s + CT] for s in starts], axis=0)
        pos = np.concatenate(
            [np.arange(ch * 4 * CT + q * CT + 1, ch * 4 * CT + q * CT + CT + 1)
             for ch in range(NCH)]).astype(np.float32)
        rpos = np.broadcast_to((1.0 / pos)[None, :], (128, NCH * CT)).copy()
        mask = np.broadcast_to((np.arange(4) < q).astype(np.float32)[None, :], (128, 4)).copy()
        in_maps.append({
            "x": xo, "wqkv": wqkv, "wo": wo,
            "wg": wg_pad[c * 384:(c + 1) * 384],
            "wu": wu_pad[c * 384:(c + 1) * 384],
            "wd": wd[c * 128:(c + 1) * 128],
            "rposd": rpos, "triu4": triu, "maska": mask,
        })

    res = run_bass_kernel_spmd(nc, in_maps, core_ids=list(range(NCORE)))
    global _RES
    _RES = res
    out = np.empty((B * T, D), dtype=np.float32)
    for c in range(NCORE):
        yc = res.results[c]["y"]
        for ch in range(NCH):
            out[spans[c][ch]:spans[c][ch] + CT] = yc[ch * CT:(ch + 1) * CT]
    return out.reshape(B, T, D)


# version 06 (final)
# speedup vs baseline: 1.1880x; 1.0343x over previous
import numpy as np
import ml_dtypes
import concourse.bacc as bacc
import concourse.mybir as mybir
import concourse.dve_ops as dve_ops
from concourse.dve_op import DveOp
from concourse.dve_spec import (
    Spec, Src0, Src1, C0, C1, C2, Zero, One, maxx, minn, lower, Bin, AluOp,
)
from concourse.alu_op_type import AluOpType
from concourse.bass_utils import run_bass_kernel_spmd

F32 = mybir.dt.float32
BF16 = mybir.dt.bfloat16
AF = mybir.ActivationFunctionType
AX = mybir.AxisListType.X
MAGIC = float(2 ** 23)

def _mk_op(name, spec, reference):
    sha = lower(spec, ver="v3", perf_en=False, name=name).sha("v3")
    return DveOp(name, spec, subdim=False, uops_sha={"v3": sha}, reference=reference)

_t = Bin(AluOp.ABSOLUTE_VALUE, Src0, Src0)
_two = One + One
_w = minn(minn(_t + _t, _t + _two), _t * C0 + C1)
FP4A_SPEC = Spec(body=_w + C2)

def _ref_fp4a(in0, in1, s0, s1, imm2):
    t = np.abs(in0)
    w = np.minimum(np.minimum(t + t, t + 2.0), t * 0.5 + 4.0)
    return (np.ceil(w - 0.5) + imm2).astype(in0.dtype)

_g = Src0 >= C0
SIGNSCALE_SPEC = Spec(body=((_g + _g) - C1) * Src1)

def _ref_signscale(in0, in1, s0, s1, imm2):
    return (np.where(in0 >= 0, 1.0, -1.0) * in1).astype(in0.dtype)

_p1 = Src0 + Src0
_p2 = _p1 + _p1
FP4B_SPEC = Spec(body=maxx(maxx(Src0 - C2, _p1 - C0), _p2 - C1) * Src1)

def _ref_fp4b(in0, in1, s0, s1, imm2):
    i = in0 - imm2
    q2 = np.maximum(np.maximum(i, i + i - 4.0), 4.0 * i - 16.0)
    return (q2 * in1).astype(in0.dtype)

FP4A = _mk_op("ANT_FP4A", FP4A_SPEC, _ref_fp4a)
SIGNSCALE = _mk_op("ANT_SIGNSCALE", SIGNSCALE_SPEC, _ref_signscale)
FP4B = _mk_op("ANT_FP4B", FP4B_SPEC, _ref_fp4b)

def _register_ops():
    names = [o.name for o in dve_ops.OPS]
    if "ANT_FP4A" in names:
        return
    i0 = names.index("DVEHASH32L_SEED")
    for j, op in enumerate([FP4A, SIGNSCALE, FP4B]):
        old = dve_ops.OPS[i0 + j]
        dve_ops.OPS[i0 + j] = op
        dve_ops._SUB_OPCODE_FOR_NAME[op.name] = dve_ops._SUB_OPCODE_FOR_NAME[old.name]
        dve_ops.CUSTOM_DVE_REFS[op.name] = op.reference
        dve_ops.CUSTOM_DVE_SPECS[op.name] = op.spec

_register_ops()

B, T, D = 2, 8192, 1024
II = 2816
H, DH = 16, 64
NCH = 4
CT = 512
NT = CT // 128
NCORE = 8
EPS = 1e-5
KD = D // 128
KI = II // 128
OC = [(0, 512), (512, 512), (1024, 512), (1536, 512), (2048, 512), (2560, 256)]

_DEBUG = False
_LIMIT = None
_FSTAGE = None
_ALIMIT = None


def build_nc():
    nc = bacc.Bacc("TRN2", target_bir_lowering=False, debug=False)

    x_d = nc.dram_tensor("x", [NCH * CT, D], F32, kind="ExternalInput").ap()
    wqkv_d = nc.dram_tensor("wqkv", [3 * D, D], F32, kind="ExternalInput").ap()
    wo_d = nc.dram_tensor("wo", [D, D], F32, kind="ExternalInput").ap()
    wg_d = nc.dram_tensor("wg", [3 * 128, D], F32, kind="ExternalInput").ap()
    wu_d = nc.dram_tensor("wu", [3 * 128, D], F32, kind="ExternalInput").ap()
    wd_d = nc.dram_tensor("wd", [128, II], F32, kind="ExternalInput").ap()
    rpos_d = nc.dram_tensor("rposd", [128, NCH * CT], F32, kind="ExternalInput").ap()
    triu_d = nc.dram_tensor("triu4", [128, 512], F32, kind="ExternalInput").ap()
    mska_d = nc.dram_tensor("maska", [128, 4], F32, kind="ExternalInput").ap()
    y_d = nc.dram_tensor("y", [NCH * CT, D], F32, kind="ExternalOutput").ap()

    wgT_d = nc.dram_tensor("wgT", [128, KD * II], BF16, kind="Internal").ap()
    wuT_d = nc.dram_tensor("wuT", [128, KD * II], BF16, kind="Internal").ap()
    wdT_d = nc.dram_tensor("wdT", [128, KI * D], BF16, kind="Internal").ap()
    wqkvT_d = nc.dram_tensor("wqkvT", [128, KD * 3 * D], BF16, kind="Internal").ap()
    woT_d = nc.dram_tensor("woT", [128, KD * D], BF16, kind="Internal").ap()
    cc_in = nc.dram_tensor("cc_in", [128, 512], F32, kind="Internal").ap()
    cc_out = nc.dram_tensor("cc_out", [4 * 128, 512], F32, kind="Internal").ap()
    QIN = 3 * KD * 128 * 2 + KI * 128
    wq_in = nc.dram_tensor("wq_in", [128, QIN], BF16, kind="Internal").ap()
    wq_out = nc.dram_tensor("wq_out", [8 * 128, QIN], BF16, kind="Internal").ap()

    dbg_aps = {}
    if _LIMIT in ("ffn1", "track", "attn", "main") or _FSTAGE or _ALIMIT:
        dbg_aps["dbg"] = nc.dram_tensor("dbg", [CT, D], F32, kind="ExternalOutput").ap()
        dbg_aps["dbg2"] = nc.dram_tensor("dbg2", [128, 4 * 512], F32, kind="ExternalOutput").ap()

    C = {}
    SEM = {}

    import contextlib
    st = contextlib.ExitStack()

    def sem(name):
        SEM[name] = st.enter_context(nc.semaphore(name))
        C[name] = 0

    for s in ["LD", "GP", "TP", "SP", "VE", "AC", "PE", "CC", "OUT"]:
        sem(s)

    def inc(instr, s, k=None):
        k = k if k is not None else (16 if s in ("LD", "GP", "TP", "SP", "OUT") else 1)
        instr.then_inc(SEM[s], k)
        C[s] += k

    def wt(eng, s, v=None):
        eng.wait_ge(SEM[s], C[s] if v is None else v)

    with st:
        xsb = st.enter_context(nc.sbuf_tensor("xsb", [128, NT * D], F32))
        xout = st.enter_context(nc.sbuf_tensor("xout", [128, D], F32))
        rpos = st.enter_context(nc.sbuf_tensor("rposc", [128, CT], F32))
        triu4 = st.enter_context(nc.sbuf_tensor("triu4s", [128, 512], F32))
        mska = st.enter_context(nc.sbuf_tensor("mskas", [128, 4], F32))
        sbase = st.enter_context(nc.sbuf_tensor("sbase", [128, 512], F32))
        sinc = st.enter_context(nc.sbuf_tensor("sinc", [128, 512], F32))
        ccs = st.enter_context(nc.sbuf_tensor("ccs", [128, 512], F32))
        stats = st.enter_context(nc.sbuf_tensor("stats", [128, 16], F32))
        scrub = st.enter_context(nc.sbuf_tensor("scrub", [128, 4], F32))
        WgTp = st.enter_context(nc.sbuf_tensor("WgTp", [128, KD * II], BF16))
        WuTp = None
        block = st.enter_context(nc.Block())
        if True:
            def sec(engname, fn):
                getattr(block, engname)(fn)

            import contextlib as _ctx
            with _ctx.ExitStack() as es0:
                ec0 = es0.enter_context
                wf32 = ec0(nc.sbuf_tensor("wf32", [128, II], F32))
                wz = ec0(nc.sbuf_tensor("wz", [128, II], F32))
                wsbf = ec0(nc.sbuf_tensor("wsbf", [128, II], F32))
                wq = ec0(nc.sbuf_tensor("wq", [128, II], BF16))
                wtst = ec0(nc.sbuf_tensor("wtst", [128, II], BF16))
                wmb = ec0(nc.sbuf_tensor("wmb", [128, II // 16], F32))
                wrs = ec0(nc.sbuf_tensor("wrs", [128, II // 16], F32))
                wsb2 = ec0(nc.sbuf_tensor("wsb2", [128, II // 16], F32))
                def _(sy):
                    wt(sy, "LD")
                    wt(sy, "SP")
                    wt(sy, "OUT")
                    wt(sy, "GP")
                    wt(sy, "TP")
                    wt(sy, "LD")
                    inc(sy.dma_start(out=triu4[:, :], in_=triu_d), "LD")
                    wt(sy, "LD")
                    inc(sy.dma_start(out=mska[:, :], in_=mska_d), "LD")
                    wt(sy, "LD")
                    inc(sy.dma_start(
                        out=xsb[:, :].rearrange("p (t j) -> p t j", t=NT),
                        in_=x_d[0:CT, :].rearrange("(t p) j -> p t j", p=128)), "LD")
                sec("sync", _)

                def _(v):
                    v.memset(sbase[:, :], 0.0)
                    inc(v.memset(scrub[:, 1:2], EPS), "VE")
                sec("vector", _)

                def quant_weight(src_d, off, nrows, K):
                    npc = nrows // 128
                    kc = K // 128
                    NB = K // 16
                    for pc in range(npc):
                        def _(sy, pc=pc):
                            wt(sy, "VE")
                            wt(sy, "LD")
                            inc(sy.dma_start(out=wf32[:, 0:K],
                                             in_=src_d[pc * 128:(pc + 1) * 128, :]), "LD")
                        sec("sync", _)

                        def _(v, pc=pc, K=K, NB=NB):
                            wt(v, "LD")
                            wt(v, "TP")
                            rsb = wrs[:, 0:NB].unsqueeze(2).broadcast_to((128, NB, 16))
                            sbb = wsb2[:, 0:NB].unsqueeze(2).broadcast_to((128, NB, 16))
                            v.tensor_reduce(wmb[:, 0:NB],
                                            wf32[:, 0:K].rearrange("p (a b) -> p a b", b=16),
                                            AX, AluOpType.max, apply_absolute_value=True)
                            v.drain()
                            v.tensor_scalar(wmb[:, 0:NB], wmb[:, 0:NB], 6e-12, None, AluOpType.max)
                            v.drain()
                            v.reciprocal(wrs[:, 0:NB], wmb[:, 0:NB])
                            v.tensor_scalar(wsb2[:, 0:NB], wmb[:, 0:NB], 1.0 / 12.0, None, AluOpType.mult)
                            v.drain()
                            v.tensor_scalar(wrs[:, 0:NB], wrs[:, 0:NB], 6.0, None, AluOpType.mult)
                            v.tensor_copy(wsbf[:, 0:K].rearrange("p (a b) -> p a b", b=16), sbb)
                            v.drain()
                            v.tensor_tensor(wz[:, 0:K].rearrange("p (a b) -> p a b", b=16),
                                            wf32[:, 0:K].rearrange("p (a b) -> p a b", b=16),
                                            rsb, AluOpType.mult)
                            v._custom_dve(SIGNSCALE, out=wsbf[:, 0:K], in0=wf32[:, 0:K],
                                          in1=wsbf[:, 0:K], s0=0.0, s1=1.0)
                            v.drain()
                            v._custom_dve(FP4A, out=wz[:, 0:K], in0=wz[:, 0:K],
                                          s0=0.5, s1=4.0, imm2=MAGIC)
                            v.drain()
                            inc(v._custom_dve(FP4B, out=wq[:, 0:K], in0=wz[:, 0:K],
                                              in1=wsbf[:, 0:K], s0=2 * MAGIC + 4.0,
                                              s1=4 * MAGIC + 16.0, imm2=MAGIC), "VE")
                        sec("vector", _)

                        def _(sy, pc=pc, K=K, kc=kc, off=off):
                            wt(sy, "VE")
                            wt(sy, "LD")
                            wt(sy, "SP")
                            wt(sy, "OUT")
                            wt(sy, "GP")
                            wt(sy, "TP")
                            inc(sy.dma_start(
                                out=wtst[:, 0:K].rearrange("p (c j) -> p c j", c=kc),
                                in_=wq[:, 0:K], transpose=True), "TP")
                            wt(sy, "TP")
                            wt(sy, "SP")
                            inc(sy.dma_start(
                                out=wq_in[:, off + pc * K:off + (pc + 1) * K],
                                in_=wtst[:, 0:K]), "SP")
                        sec("sync", _)

                quant_weight(wg_d, 0, 3 * 128, D)
                quant_weight(wu_d, 3 * KD * 128, 3 * 128, D)
                quant_weight(wd_d, 6 * KD * 128, 128, II)

                def cast_weight(src_d, dst_d, nrows):
                    npc = nrows // 128
                    for pc in range(npc):
                        def _(g, pc=pc):
                            wt(g, "TP")
                            wt(g, "GP")
                            inc(g.dma_start(out=wq[:, 0:D],
                                            in_=src_d[pc * 128:(pc + 1) * 128, :]), "GP")
                        sec("gpsimd", _)

                        def _(sy, pc=pc, nrows=nrows):
                            wt(sy, "GP")
                            wt(sy, "LD")
                            wt(sy, "SP")
                            wt(sy, "OUT")
                            wt(sy, "TP")
                            inc(sy.dma_start(
                                out=wtst[:, 0:D].rearrange("p (c j) -> p c j", c=KD),
                                in_=wq[:, 0:D], transpose=True), "TP")
                            wt(sy, "TP")
                            dst = dst_d.rearrange("p (c r) -> p c r", c=KD)[:, :, pc * 128:(pc + 1) * 128]
                            wt(sy, "SP")
                            inc(sy.dma_start(out=dst,
                                             in_=wtst[:, 0:D].rearrange("p (c j) -> p c j", c=KD)), "SP")
                        sec("sync", _)

                def _(g):
                    wt(g, "SP")
                    wt(g, "CC")
                    inc(g.collective_compute(
                        "AllGather", AluOpType.bypass,
                        replica_groups=[[0, 1, 2, 3, 4, 5, 6, 7]],
                        ins=[wq_in], outs=[wq_out]), "CC")
                sec("gpsimd", _)

                cast_weight(wqkv_d, wqkvT_d, 3 * D)
                cast_weight(wo_d, woT_d, D)

                def _(sy):
                    wt(sy, "CC")
                    k = 0
                    for dst_d2, off, kc2, nt, spc in (
                            (WgTp, 0, KD, 22, 3),
                            (wuT_d, 3 * KD * 128, KD, 22, 3),
                            (wdT_d, 6 * KD * 128, KI, 8, 1)):
                        for t in range(nt):
                            j, s = divmod(t, spc)
                            src = wq_out[j * 128:(j + 1) * 128,
                                         off + s * kc2 * 128:
                                         off + (s + 1) * kc2 * 128].rearrange(
                                "p (c r) -> p c r", c=kc2)
                            if dst_d2 is WgTp:
                                dst = dst_d2[:, :].rearrange(
                                    "p (c r) -> p c r", c=kc2)[:, :, t * 128:(t + 1) * 128]
                            else:
                                dst = dst_d2.rearrange(
                                    "p (c r) -> p c r", c=kc2)[:, :, t * 128:(t + 1) * 128]
                            ch_sem = "SP" if k % 2 == 0 else "LD"
                            k += 1
                            wt(sy, ch_sem)
                            inc(sy.dma_start(out=dst, in_=src), ch_sem)
                sec("sync", _)
                C["_WGU_SP"] = C["SP"]
                C["_WGU_LD"] = C["LD"]

            nch = 0 if _LIMIT == "setup" else (1 if _LIMIT in ("ffn1", "track", "attn", "chunk1", "main") else NCH)
            for ch in range(nch):
                chunk(nc, block, sec, C, SEM, inc, wt, ch,
                      x_d, y_d, wgT_d, wuT_d, wdT_d, wqkvT_d, woT_d,
                      cc_in, cc_out, rpos_d,
                      xsb, xout, rpos, triu4, mska, sbase, sinc, ccs,
                      stats, scrub, dbg_aps, WgTp, WuTp)

            def _(sy):
                wt(sy, "OUT")
            sec("sync", _)

    nc.finalize()
    return nc


def chunk(nc, block, sec, C, SEM, inc, wt, ch,
          x_d, y_d, wgT_d, wuT_d, wdT_d, wqkvT_d, woT_d, cc_in, cc_out, rpos_d,
          xsb, xout, rpos, triu4, mska, sbase, sinc, ccs, stats, scrub,
          dbg_aps, WgTp, WuTp):
    F32 = mybir.dt.float32
    BF16 = mybir.dt.bfloat16

    def ffn_phase(tag, resid_scale_out):
        import contextlib
        with contextlib.ExitStack() as es:
            ec = es.enter_context
            WgT = WgTp
            WuT = ec(nc.sbuf_tensor(f"WuT{tag}", [128, KD * II], BF16))
            WdT = ec(nc.sbuf_tensor(f"WdT{tag}", [128, KI * D], BF16))
            cbuf = ec(nc.sbuf_tensor(f"cbuf{tag}", [128, D], F32))
            zbuf = ec(nc.sbuf_tensor(f"zbuf{tag}", [128, D], F32))
            sbf = ec(nc.sbuf_tensor(f"sbf{tag}", [128, D], F32))
            xq = ec(nc.sbuf_tensor(f"xq{tag}", [128, D], BF16))
            xqT = ec(nc.sbuf_tensor(f"xqT{tag}", [128, KD * 128], BF16))
            sgb = ec(nc.sbuf_tensor(f"sg{tag}", [128, 512], F32))
            hb = ec(nc.sbuf_tensor(f"hb{tag}", [128, 512], F32))
            zh = ec(nc.sbuf_tensor(f"zh{tag}", [128, 512], F32))
            hq = ec(nc.sbuf_tensor(f"hq{tag}", [128, II], BF16))
            hqT = ec(nc.sbuf_tensor(f"hqT{tag}", [128, KI * 128], BF16))
            mb = ec(nc.sbuf_tensor(f"mb{tag}", [128, 64], F32))
            rs = ec(nc.sbuf_tensor(f"rs{tag}", [128, 64], F32))
            sb2 = ec(nc.sbuf_tensor(f"sb2{tag}", [128, 64], F32))
            psg0 = ec(nc.psum_tensor(f"psg{tag}", [128, 512], F32))
            psg1 = ec(nc.psum_tensor(f"psg2{tag}", [128, 512], F32))
            psu0 = ec(nc.psum_tensor(f"psu{tag}", [128, 512], F32))
            psu1 = ec(nc.psum_tensor(f"psu2{tag}", [128, 512], F32))
            psd = ec(nc.psum_tensor(f"psd{tag}", [128, 1024], F32))
            psg = [psg0, psg1]
            psu = [psu0, psu1]

            x_ld = C["LD"]
            def _(sy):
                wt(sy, "VE")
                wt(sy, "AC")
                wt(sy, "PE")
                wt(sy, "SP")
                wt(sy, "LD")
                inc(sy.dma_start(out=WuT[:, :], in_=wuT_d), "LD")
            sec("sync", _)
            w_ld = C["LD"]

            fstage = _FSTAGE if tag.startswith("f1c0") else None
            ntt = 1 if fstage else NT
            for t in range(ntt):
                xt = xsb[:, t * D:(t + 1) * D]

                def _(a, t=t):
                    wt(a, "VE")
                    if t == 0:
                        wt(a, "LD", x_ld)
                    a.activation(zbuf[:, :], xt, AF.Square, accum_out=stats[:, 1:2])
                    a.activation(cbuf[:, :], xt, AF.Identity, accum_out=stats[:, 0:1])
                    a.drain()
                    a.activation(stats[:, 2:3], stats[:, 0:1], AF.Identity, scale=1.0 / D)
                    a.activation(stats[:, 3:4], stats[:, 1:2], AF.Identity,
                                 scale=1.0 / D, bias=scrub[:, 1:2])
                    a.drain()
                    a.activation(stats[:, 8:9], stats[:, 2:3], AF.Square)
                    a.drain()
                    a.activation(stats[:, 4:5], stats[:, 8:9], AF.Identity,
                                 scale=-1.0, bias=stats[:, 3:4])
                    a.drain()
                    a.activation(stats[:, 5:6], stats[:, 4:5], AF.Ln)
                    a.drain()
                    inc(a.activation(stats[:, 6:7], stats[:, 5:6], AF.Exp, scale=-0.5), "AC")
                sec("scalar", _)
                if fstage == "ln":
                    def _(sy):
                        wt(sy, "AC")
                        wt(sy, "OUT")
                        inc(sy.dma_start(out=dbg_aps["dbg2"][:, 0:16], in_=stats[:, :]), "OUT")
                    sec("sync", _)
                    return

                def _(v, t=t):
                    wt(v, "AC")
                    rsb = rs[:, :].unsqueeze(2).broadcast_to((128, 64, 16))
                    sbb = sb2[:, :].unsqueeze(2).broadcast_to((128, 64, 16))
                    v.tensor_scalar(cbuf[:, :], xt, stats[:, 2:3], None, AluOpType.subtract)
                    v.drain()
                    v.tensor_reduce(mb[:, :], cbuf[:, :].rearrange("p (a b) -> p a b", b=16),
                                    AX, AluOpType.max, apply_absolute_value=True)
                    v.drain()
                    v.tensor_scalar(mb[:, :], mb[:, :], 6e-12, None, AluOpType.max)
                    v.drain()
                    v.reciprocal(rs[:, :], mb[:, :])
                    v.tensor_scalar(sb2[:, :], mb[:, :], stats[:, 6:7], 1.0 / 12.0,
                                    AluOpType.mult, AluOpType.mult)
                    v.drain()
                    v.tensor_scalar(rs[:, :], rs[:, :], 6.0, None, AluOpType.mult)
                    v.tensor_copy(sbf[:, :].rearrange("p (a b) -> p a b", b=16), sbb)
                    v.drain()
                    v.tensor_tensor(zbuf[:, :].rearrange("p (a b) -> p a b", b=16),
                                    cbuf[:, :].rearrange("p (a b) -> p a b", b=16),
                                    rsb, AluOpType.mult)
                    v._custom_dve(SIGNSCALE, out=cbuf[:, :], in0=cbuf[:, :], in1=sbf[:, :],
                                  s0=0.0, s1=1.0)
                    v.drain()
                    v._custom_dve(FP4A, out=zbuf[:, :], in0=zbuf[:, :], s0=0.5, s1=4.0, imm2=MAGIC)
                    v.drain()
                    inc(v._custom_dve(FP4B, out=xq[:, :], in0=zbuf[:, :], in1=cbuf[:, :],
                                      s0=2 * MAGIC + 4.0, s1=4 * MAGIC + 16.0, imm2=MAGIC), "VE")
                sec("vector", _)

                def _(sy, t=t):
                    wt(sy, "VE")
                    wt(sy, "LD")
                    wt(sy, "SP")
                    wt(sy, "OUT")
                    wt(sy, "GP")
                    wt(sy, "TP")
                    inc(sy.dma_start(out=xqT[:, :].rearrange("p (c j) -> p c j", c=KD),
                                     in_=xq[:, :], transpose=True), "TP")
                    wt(sy, "TP")
                sec("sync", _)
                tp_xq = C["TP"]

                if t == 0:
                    def _(sy):
                        wt(sy, "LD")
                        inc(sy.dma_start(out=WdT[:, :], in_=wdT_d), "LD")
                    sec("sync", _)
                if fstage == "quant":
                    def _(v):
                        wt(v, "TP")
                        inc(v.tensor_copy(sbf[:, :], xq[:, :]), "VE")
                    sec("vector", _)
                    def _(sy):
                        wt(sy, "VE")
                        wt(sy, "OUT")
                        inc(sy.dma_start(out=dbg_aps["dbg"][0:128, :], in_=sbf[:, :]), "OUT")
                        wt(sy, "OUT")
                        inc(sy.dma_start(out=dbg_aps["dbg2"][:, 0:64], in_=mb[:, :]), "OUT")
                    sec("sync", _)
                    return

                if fstage == "wload":
                    def _(a, t=t):
                        wt(a, "LD", w_ld)
                        wt(a, "TP", tp_xq)
                        inc(a.activation(sgb[:, :], WgT[:, 0:512], AF.Identity), "AC")
                    sec("scalar", _)
                    def _(v, t=t):
                        wt(v, "TP", tp_xq)
                        inc(v.tensor_copy(hb[:, 0:128], xqT[:, 0:128]), "VE")
                    sec("vector", _)
                    def _(sy):
                        wt(sy, "AC")
                        wt(sy, "VE")
                        wt(sy, "OUT")
                        inc(sy.dma_start(out=dbg_aps["dbg"][0:128, 0:512], in_=sgb[:, :]), "OUT")
                        wt(sy, "OUT")
                        inc(sy.dma_start(out=dbg_aps["dbg"][128:256, 0:128], in_=hb[:, 0:128]), "OUT")
                    sec("sync", _)
                    return
                if fstage in ("mm", "mm1", "mmc"):
                    if fstage == "mmc":
                        def _(v, t=t):
                            wt(v, "LD", w_ld)
                            inc(v.tensor_copy(hq[:, 0:512], WgT[:, 0:512]), "VE")
                        sec("vector", _)
                    def _(te, t=t):
                        wt(te, "TP", tp_xq)
                        wt(te, "LD", w_ld)
                        wt(te, "VE")
                        nk = 1 if fstage == "mm1" else KD
                        for k in range(nk):
                            rhs = hq[:, 0:512] if fstage == "mmc" else WgT[:, k * II: k * II + 512]
                            ins_ = te.matmul(psg[0][:, :], xqT[:, k * 128:(k + 1) * 128], rhs,
                                             start=(k == 0), stop=(k == nk - 1))
                        inc(ins_, "PE")
                    sec("tensor", _)
                    def _(a, t=t):
                        wt(a, "PE")
                        inc(a.activation(sgb[:, :], psg[0][:, :], AF.Identity), "AC")
                    sec("scalar", _)
                    def _(sy):
                        wt(sy, "AC")
                        wt(sy, "OUT")
                        inc(sy.dma_start(out=dbg_aps["dbg"][0:128, 0:512], in_=sgb[:, :]), "OUT")
                    sec("sync", _)
                    return

                for oi, (o0, ow) in enumerate(OC):
                    bk = oi % 2
                    nb = ow // 16
                    def _(te, t=t, oi=oi, o0=o0, ow=ow, bk=bk):
                        wt(te, "TP", tp_xq)
                        wt(te, "LD", w_ld)
                        wt(te, "SP", C["_WGU_SP"])
                        wt(te, "VE")
                        wt(te, "AC")
                        for k in range(KD):
                            ins_ = te.matmul(psg[bk][:, 0:ow], xqT[:, k * 128:(k + 1) * 128],
                                             WgT[:, k * II + o0:k * II + o0 + ow],
                                             start=(k == 0), stop=(k == KD - 1))
                        inc(ins_, "PE")
                        for k in range(KD):
                            ins_ = te.matmul(psu[bk][:, 0:ow], xqT[:, k * 128:(k + 1) * 128],
                                             WuT[:, k * II + o0:k * II + o0 + ow],
                                             start=(k == 0), stop=(k == KD - 1))
                        inc(ins_, "PE")
                    sec("tensor", _)

                    def _(a, t=t, oi=oi, o0=o0, ow=ow, bk=bk):
                        wt(a, "PE")
                        inc(a.activation(sgb[:, 0:ow], psg[bk][:, 0:ow], AF.Silu), "AC")
                    sec("scalar", _)

                    def _(v, t=t, oi=oi, o0=o0, ow=ow, bk=bk, nb=nb):
                        wt(v, "AC")
                        rsb = rs[:, 0:nb].unsqueeze(2).broadcast_to((128, nb, 16))
                        sbb = sb2[:, 0:nb].unsqueeze(2).broadcast_to((128, nb, 16))
                        v.tensor_tensor(hb[:, 0:ow], psu[bk][:, 0:ow], sgb[:, 0:ow], AluOpType.mult)
                        v.drain()
                        if fstage == "gu0" and oi == 0:
                            inc(v.tensor_copy(zh[:, 0:ow], hb[:, 0:ow]), "VE")
                            return
                        v.tensor_reduce(mb[:, 0:nb], hb[:, 0:ow].rearrange("p (a b) -> p a b", b=16),
                                        AX, AluOpType.max, apply_absolute_value=True)
                        v.drain()
                        v.tensor_scalar(mb[:, 0:64], mb[:, 0:64], 6e-12, None, AluOpType.max)
                        v.drain()
                        v.reciprocal(rs[:, 0:64], mb[:, 0:64])
                        v.tensor_scalar(sb2[:, 0:64], mb[:, 0:64], 1.0 / 12.0, None, AluOpType.mult)
                        v.drain()
                        v.tensor_scalar(rs[:, 0:64], rs[:, 0:64], 6.0, None, AluOpType.mult)
                        v.tensor_copy(sbf[:, 0:ow].rearrange("p (a b) -> p a b", b=16), sbb)
                        v.drain()
                        v.tensor_tensor(zh[:, 0:ow].rearrange("p (a b) -> p a b", b=16),
                                        hb[:, 0:ow].rearrange("p (a b) -> p a b", b=16),
                                        rsb, AluOpType.mult)
                        v._custom_dve(SIGNSCALE, out=hb[:, 0:ow], in0=hb[:, 0:ow],
                                      in1=sbf[:, 0:ow], s0=0.0, s1=1.0)
                        v.drain()
                        v._custom_dve(FP4A, out=zh[:, 0:ow], in0=zh[:, 0:ow],
                                      s0=0.5, s1=4.0, imm2=MAGIC)
                        v.drain()
                        inc(v._custom_dve(FP4B, out=hq[:, o0:o0 + ow], in0=zh[:, 0:ow],
                                          in1=hb[:, 0:ow], s0=2 * MAGIC + 4.0,
                                          s1=4 * MAGIC + 16.0, imm2=MAGIC), "VE")
                    sec("vector", _)
                    if fstage == "gu0" and oi == 0:
                        def _(sy):
                            wt(sy, "VE")
                            wt(sy, "OUT")
                            inc(sy.dma_start(out=dbg_aps["dbg"][0:128, 0:512], in_=zh[:, 0:512]), "OUT")
                            wt(sy, "OUT")
                            inc(sy.dma_start(out=dbg_aps["dbg"][128:256, 0:512], in_=sgb[:, 0:512]), "OUT")
                        sec("sync", _)
                        return
                    if fstage == "gu" and oi == 0:
                        def _(v):
                            inc(v.tensor_copy(zh[:, :], hq[:, 0:512]), "VE")
                        sec("vector", _)
                        def _(sy):
                            wt(sy, "VE")
                            wt(sy, "OUT")
                            inc(sy.dma_start(out=dbg_aps["dbg"][0:128, 0:512], in_=zh[:, :]), "OUT")
                            wt(sy, "OUT")
                            inc(sy.dma_start(out=dbg_aps["dbg"][128:256, 0:512], in_=sgb[:, :]), "OUT")
                        sec("sync", _)
                        return

                def _(sy, t=t):
                    wt(sy, "VE")
                    wt(sy, "LD")
                    wt(sy, "SP")
                    wt(sy, "OUT")
                    wt(sy, "GP")
                    wt(sy, "TP")
                    inc(sy.dma_start(out=hqT[:, :].rearrange("p (c j) -> p c j", c=KI),
                                     in_=hq[:, :], transpose=True), "TP")
                    wt(sy, "TP")
                sec("sync", _)
                tp_hq = C["TP"]

                def _(te, t=t):
                    wt(te, "TP", tp_hq)
                    wt(te, "VE")
                    for half in range(2):
                        for k in range(KI):
                            ins_ = te.matmul(psd[:, half * 512:(half + 1) * 512],
                                             hqT[:, k * 128:(k + 1) * 128],
                                             WdT[:, k * D + half * 512:k * D + half * 512 + 512],
                                             start=(k == 0), stop=(k == KI - 1))
                    inc(ins_, "PE")
                sec("tensor", _)

                if resid_scale_out is None:
                    def _(v, t=t):
                        wt(v, "PE")
                        inc(v.scalar_tensor_tensor(xt, psd[:, 0:D], 0.5, xt,
                                                   AluOpType.mult, AluOpType.add), "VE")
                    sec("vector", _)
                    if fstage == "down":
                        def _(sy):
                            wt(sy, "VE")
                            wt(sy, "OUT")
                            inc(sy.dma_start(out=dbg_aps["dbg"][0:128, :], in_=xt), "OUT")
                        sec("sync", _)
                        return
                else:
                    def _(v, t=t):
                        wt(v, "PE")
                        wt(v, "OUT")
                        inc(v.scalar_tensor_tensor(xout[:, :], psd[:, 0:D], 0.5, xt,
                                                   AluOpType.mult, AluOpType.add), "VE")
                    sec("vector", _)

                    def _(sy, t=t):
                        wt(sy, "VE")
                        rows = y_d[ch * CT + t * 128: ch * CT + (t + 1) * 128, :]
                        wt(sy, "LD")
                        wt(sy, "SP")
                        wt(sy, "OUT")
                        wt(sy, "GP")
                        wt(sy, "TP")
                        wt(sy, "OUT")
                        inc(sy.dma_start(out=rows, in_=xout[:, :]), "OUT")
                    sec("sync", _)

    ffn_phase(f"f1c{ch}", None)
    if _FSTAGE:
        return
    if _LIMIT == "ffn1":
        def _(sy):
            wt(sy, "VE")
            wt(sy, "OUT")
            inc(sy.dma_start(out=dbg_aps["dbg"].rearrange("(t p) j -> p t j", p=128),
                             in_=xsb[:, :].rearrange("p (t j) -> p t j", t=NT)), "OUT")
        sec("sync", _)
        return

    import contextlib
    with contextlib.ExitStack() as es:
        ec = es.enter_context
        gath = ec(nc.sbuf_tensor(f"gath{ch}", [128, 4 * 512], F32))
        u2T = ec(nc.sbuf_tensor(f"u2T{ch}", [128, KD * CT], BF16))
        u2b = ec(nc.sbuf_tensor(f"u2b{ch}", [128, D], BF16))
        qdm = ec(nc.sbuf_tensor(f"qdm{ch}", [128, 8 * CT], BF16))
        kdm = ec(nc.sbuf_tensor(f"kdm{ch}", [128, 8 * CT], BF16))
        ktok = ec(nc.sbuf_tensor(f"ktok{ch}", [128, NT * D], BF16))
        vtok = ec(nc.sbuf_tensor(f"vtok{ch}", [128, NT * D], BF16))
        wfc = ec(nc.sbuf_tensor(f"wfc{ch}", [128, 2 * KD * 128], BF16))
        wv = ec(nc.sbuf_tensor(f"wv{ch}", [128, KD * D], BF16))
        wow = ec(nc.sbuf_tensor(f"wow{ch}", [128, KD * D], BF16))
        scb = ec(nc.sbuf_tensor(f"scb{ch}", [128, 512], BF16))
        stot = ec(nc.sbuf_tensor(f"stot{ch}", [128, NT * 512], BF16))
        sprt = ec(nc.sbuf_tensor(f"sprt{ch}", [128, NT * 512], F32))
        attb = ec(nc.sbuf_tensor(f"attb{ch}", [128, NT * KD * 128], BF16))
        cbufa = ec(nc.sbuf_tensor(f"cbufa{ch}", [128, D], F32))
        def _(sy):
            wt(sy, "VE")
            wt(sy, "AC")
            wt(sy, "PE")
            wt(sy, "LD")
            inc(sy.dma_start(out=rpos[:, :], in_=rpos_d[:, ch * CT:(ch + 1) * CT]), "LD")
        sec("sync", _)

        for t in range(NT):
            xt = xsb[:, t * D:(t + 1) * D]

            def _(a, t=t):
                wt(a, "VE")
                wt(a, "TP")
                a.activation(cbufa[:, :], xt, AF.Square, accum_out=stats[:, 1:2])
                a.activation(u2b[:, :], xt, AF.Identity, accum_out=stats[:, 0:1])
                a.drain()
                a.activation(stats[:, 2:3], stats[:, 0:1], AF.Identity, scale=1.0 / D)
                a.activation(stats[:, 9:10], stats[:, 0:1], AF.Identity, scale=-1.0 / D)
                a.activation(stats[:, 3:4], stats[:, 1:2], AF.Identity,
                             scale=1.0 / D, bias=scrub[:, 1:2])
                a.drain()
                a.activation(stats[:, 8:9], stats[:, 2:3], AF.Square)
                a.drain()
                a.activation(stats[:, 4:5], stats[:, 8:9], AF.Identity,
                             scale=-1.0, bias=stats[:, 3:4])
                a.drain()
                a.activation(stats[:, 5:6], stats[:, 4:5], AF.Ln)
                a.drain()
                a.activation(stats[:, 6:7], stats[:, 5:6], AF.Exp, scale=-0.5)
                a.drain()
                a.activation(stats[:, 7:8], stats[:, 6:7], AF.Identity,
                             scale=stats[:, 9:10])
                a.drain()
                inc(a.activation(u2b[:, :], xt, AF.Identity,
                                 bias=stats[:, 7:8], scale=stats[:, 6:7]), "AC")
            sec("scalar", _)

            def _(sy, t=t):
                wt(sy, "AC")
                wt(sy, "LD")
                wt(sy, "SP")
                wt(sy, "OUT")
                wt(sy, "GP")
                dst = u2T[:, :].rearrange("p (c j) -> p c j", c=KD)[:, :, t * 128:(t + 1) * 128]
                wt(sy, "TP")
                inc(sy.dma_start(out=dst, in_=u2b[:, :], transpose=True), "TP")
                wt(sy, "TP")
            sec("sync", _)

        def _(sy):
            wt(sy, "AC")
            wt(sy, "LD")
            wt(sy, "SP")
            wt(sy, "OUT")
            wt(sy, "GP")
            wt(sy, "TP")
            wt(sy, "LD")
            inc(sy.dma_start(out=wv[:, :].rearrange("p (c j) -> p c j", c=KD),
                             in_=wqkvT_d.rearrange("p (c j) -> p c j", c=KD)[:, :, 2 * D:3 * D]), "LD")
            wt(sy, "LD")
            inc(sy.dma_start(out=wow[:, :], in_=woT_d), "LD")
        sec("sync", _)
        vw_ld = C["LD"]
        tp_u2 = C["TP"]

        with contextlib.ExitStack() as es2:
            ec2 = es2.enter_context
            psqk0 = ec2(nc.psum_tensor(f"psqk{ch}", [128, 512], F32))
            psqk1 = ec2(nc.psum_tensor(f"psqk2{ch}", [128, 512], F32))
            psv = ec2(nc.psum_tensor(f"psv{ch}", [128, 1024], F32))
            psst = ec2(nc.psum_tensor(f"psst{ch}", [128, 512], F32))
            psqk = [psqk0, psqk1]
            for fc in range(16):
                bk = fc % 2
                def _(sy, fc=fc, bk=bk):
                    if fc >= 2:
                        wt(sy, "PE")
                    src = wqkvT_d.rearrange("p (c j) -> p c j", c=KD)[:, :, fc * 128:(fc + 1) * 128]
                    wt(sy, "LD")
                    inc(sy.dma_start(out=wfc[:, bk * KD * 128:(bk + 1) * KD * 128]
                                     .rearrange("p (c j) -> p c j", c=KD), in_=src), "LD")
                sec("sync", _)

                def _(te, fc=fc, bk=bk):
                    wt(te, "LD")
                    wt(te, "TP", tp_u2)
                    wt(te, "AC")
                    for k in range(KD):
                        ins_ = te.matmul(psqk[bk][:, :],
                                         wfc[:, bk * KD * 128 + k * 128: bk * KD * 128 + (k + 1) * 128],
                                         u2T[:, k * CT:(k + 1) * CT],
                                         start=(k == 0), stop=(k == KD - 1))
                    inc(ins_, "PE")
                sec("tensor", _)

                def _(a, fc=fc, bk=bk):
                    wt(a, "PE")
                    if fc < 8:
                        inc(a.activation(qdm[:, fc * CT:(fc + 1) * CT], psqk[bk][:, :],
                                         AF.Identity, scale=0.125), "AC")
                    else:
                        dst = kdm[:, :].rearrange("p (t j) -> p t j", t=NT)[
                            :, :, (fc - 8) * 128:(fc - 7) * 128]
                        inc(a.activation(dst, psqk[bk][:, :], AF.Sign), "AC")
                sec("scalar", _)
            qk_ac = C["AC"]

            for t in range(NT):
                def _(te, t=t):
                    wt(te, "LD", vw_ld)
                    wt(te, "TP", tp_u2)
                    if t > 0:
                        wt(te, "AC")
                    for half in range(2):
                        for k in range(KD):
                            ins_ = te.matmul(psv[:, half * 512:(half + 1) * 512],
                                             u2T[:, k * CT + t * 128:k * CT + (t + 1) * 128],
                                             wv[:, k * D + half * 512:k * D + half * 512 + 512],
                                             start=(k == 0), stop=(k == KD - 1))
                    inc(ins_, "PE")
                sec("tensor", _)

                def _(a, t=t):
                    wt(a, "PE")
                    inc(a.activation(vtok[:, t * D:(t + 1) * D], psv[:, :], AF.Sign), "AC")
                sec("scalar", _)

            def _(sy):
                wt(sy, "AC", qk_ac)
                wt(sy, "LD")
                wt(sy, "SP")
                wt(sy, "OUT")
                wt(sy, "GP")
                for t in range(NT):
                    wt(sy, "TP")
                    inc(sy.dma_start(out=ktok[:, t * D:(t + 1) * D]
                                     .rearrange("p (c j) -> p c j", c=8),
                                     in_=kdm[:, t * D:(t + 1) * D], transpose=True), "TP")
                    wt(sy, "TP")
            sec("sync", _)
            tp_k = C["TP"]

            with contextlib.ExitStack() as esp:
                spre = esp.enter_context(nc.sbuf_tensor(f"spre{ch}", [128, 512], F32))
                def _(v):
                    inc(v.memset(spre[:, :], 0.0), "VE")
                sec("vector", _)

                for t in range(NT):
                    def _(te, t=t):
                        wt(te, "TP", tp_k)
                        wt(te, "AC")
                        wt(te, "VE")
                        for h in range(H):
                            pb = 64 * (h % 2)
                            fs = (h // 2) * 64
                            ins_ = te.matmul(psst[pb:pb + 64, fs:fs + 64],
                                             ktok[:, t * D + h * 64:t * D + (h + 1) * 64],
                                             vtok[:, t * D + h * 64:t * D + (h + 1) * 64],
                                             start=True, stop=True,
                                             skip_group_check=True)
                        inc(ins_, "PE")
                    sec("tensor", _)

                    def _(v, t=t):
                        wt(v, "PE")
                        v.tensor_copy(sprt[:, t * 512:(t + 1) * 512], spre[:, :])
                        v.drain()
                        inc(v.tensor_tensor(spre[:, :], psst[:, :], spre[:, :],
                                            AluOpType.add), "VE")
                    sec("vector", _)

                def _(v):
                    v.drain()
                    inc(v.tensor_copy(ccs[:, :], spre[:, :]), "VE")
                sec("vector", _)

                def _(sy):
                    wt(sy, "VE")
                    wt(sy, "CC")
                    wt(sy, "SP")
                    inc(sy.dma_start(out=cc_in, in_=ccs[:, :]), "SP")
                sec("sync", _)

                def _(g):
                    wt(g, "SP")
                    wt(g, "LD")
                    wt(g, "CC")
                    inc(g.collective_compute(
                        "AllGather", AluOpType.bypass,
                        replica_groups=[[0, 1, 2, 3], [4, 5, 6, 7]],
                        ins=[cc_in], outs=[cc_out]), "CC")
                sec("gpsimd", _)

                def _(sy):
                    wt(sy, "CC")
                    wt(sy, "LD")
                    inc(sy.dma_start(out=gath[:, :].rearrange("p (j f) -> p j f", j=4),
                                     in_=cc_out.rearrange("(j p) f -> p j f", p=128)), "LD")
                sec("sync", _)

                def _(v):
                    wt(v, "LD")
                    v.tensor_copy(sinc[:, :], sbase[:, :])
                    for j in range(4):
                        v.drain()
                        v.scalar_tensor_tensor(sinc[:, :], gath[:, j * 512:(j + 1) * 512],
                                               mska[:, j:j + 1], sinc[:, :],
                                               AluOpType.mult, AluOpType.add)
                    for j in range(4):
                        v.drain()
                        v.scalar_tensor_tensor(sbase[:, :], gath[:, j * 512:(j + 1) * 512],
                                               1.0, sbase[:, :], AluOpType.mult, AluOpType.add)
                    v.drain()
                    for t in range(NT):
                        ins_ = v.tensor_tensor(stot[:, t * 512:(t + 1) * 512],
                                               sprt[:, t * 512:(t + 1) * 512], sinc[:, :],
                                               AluOpType.add)
                    inc(ins_, "VE")
                sec("vector", _)
                s_ready = C["VE"]
                es2.close()
                if _LIMIT == "track":
                    def _(sy):
                        wt(sy, "VE")
                        wt(sy, "OUT")
                        inc(sy.dma_start(out=dbg_aps["dbg"].rearrange("(t p) j -> p t j", p=128),
                                         in_=xsb[:, :].rearrange("p (t j) -> p t j", t=NT)), "OUT")
                        wt(sy, "OUT")
                        inc(sy.dma_start(out=dbg_aps["dbg2"][:, 0:512], in_=sprt[:, 0:512]), "OUT")
                    sec("sync", _)
                    return

                with contextlib.ExitStack() as es3:
                    ec3 = es3.enter_context
                    pssc0 = ec3(nc.psum_tensor(f"pssc{ch}", [128, 512], F32))
                    pssc1 = ec3(nc.psum_tensor(f"pssc2{ch}", [128, 512], F32))
                    pso4 = [ec3(nc.psum_tensor(f"pso{ch}_{i}", [128, 512], F32))
                            for i in range(4)]
                    pssc = [pssc0, pssc1]
                    for t in range(NT):
                        for p in range(2):
                            for q in range(2):
                                def _(te, t=t, p=p, q=q):
                                    wt(te, "VE")
                                    if p == 0 and q == 0:
                                        wt(te, "AC")
                                    for jj in range(4):
                                        j = 4 * q + jj
                                        h = 2 * j + p
                                        pb = 64 * p
                                        fq = j * CT + t * 128
                                        fk = t * D + j * 128
                                        ins_ = te.matmul(pssc[p][:, jj * 128:(jj + 1) * 128],
                                                         kdm[pb:pb + 64, fk:fk + 128],
                                                         qdm[pb:pb + 64, fq:fq + 128],
                                                         start=True, stop=True,
                                                         skip_group_check=True)
                                    inc(ins_, "PE")
                                sec("tensor", _)

                                def _(v, t=t, p=p, q=q):
                                    wt(v, "PE")
                                    inc(v.tensor_tensor(scb[:, :], pssc[p][:, :], triu4[:, :],
                                                        AluOpType.mult), "VE")
                                sec("vector", _)
                                if _ALIMIT == "sc" and t == 0 and p == 0 and q == 0:
                                    def _(g):
                                        wt(g, "VE")
                                        inc(g.dma_start(out=dbg_aps["dbg"][0:128, 0:512], in_=scb[:, :]), "OUT")
                                    sec("gpsimd", _)
                                    return

                                def _(te, t=t, p=p, q=q):
                                    wt(te, "VE")
                                    for jj in range(4):
                                        j = 4 * q + jj
                                        h = 2 * j + p
                                        pb = 64 * p
                                        fq = j * CT + t * 128
                                        te.matmul(pso4[2 * p + q][pb:pb + 64, jj * 128:(jj + 1) * 128],
                                                  vtok[:, t * D + h * 64:t * D + (h + 1) * 64],
                                                  scb[:, jj * 128:(jj + 1) * 128],
                                                  start=True, stop=False,
                                                  skip_group_check=True)
                                        ins_ = te.matmul(pso4[2 * p + q][pb:pb + 64, jj * 128:(jj + 1) * 128],
                                                  stot[pb:pb + 64, t * 512 + j * 64:
                                                       t * 512 + j * 64 + 64],
                                                  qdm[pb:pb + 64, fq:fq + 128],
                                                  start=False, stop=True,
                                                  skip_group_check=True)
                                    inc(ins_, "PE")
                                sec("tensor", _)
                                if _ALIMIT == "ic" and t == 0 and p == 0 and q == 0:
                                    def _(v):
                                        wt(v, "PE")
                                        inc(v.tensor_copy(sprt[0:64, 0:512], pso4[0][0:64, :]), "VE")
                                    sec("vector", _)
                                    def _(g):
                                        wt(g, "VE")
                                        inc(g.dma_start(out=dbg_aps["dbg"][0:64, 0:512], in_=sprt[0:64, 0:512]), "OUT")
                                    sec("gpsimd", _)
                                    return

                        def _(a, t=t):
                            wt(a, "PE")
                            inc(a.nop(), "AC")
                        sec("scalar", _)

                        def _(v, t=t):
                            wt(v, "AC")
                            rp = rpos[:, t * 128:(t + 1) * 128]
                            for p in range(2):
                                for q in range(2):
                                    pb = 64 * p
                                    rpb = rp[pb:pb + 64, :].unsqueeze(1).broadcast_to((64, 4, 128))
                                    dst = attb[pb:pb + 64,
                                               t * KD * 128 + q * 4 * 128:
                                               t * KD * 128 + (q * 4 + 4) * 128]
                                    ins_ = v.tensor_tensor(
                                        dst.rearrange("p (s i) -> p s i", s=4),
                                        pso4[2 * p + q][pb:pb + 64, :].rearrange("p (s i) -> p s i", s=4),
                                        rpb, AluOpType.mult)
                            inc(ins_, "VE")
                        sec("vector", _)
                        def _(a, t=t):
                            wt(a, "VE")
                            inc(a.nop(), "AC")
                        sec("scalar", _)

            if _LIMIT == "main":
                def _(g):
                    wt(g, "VE")
                    wt(g, "OUT")
                    inc(g.dma_start(out=dbg_aps["dbg"].rearrange("(a p) j -> p a j", p=128),
                                    in_=attb[:, :].rearrange("p (a j) -> p a j", a=NT)), "OUT")
                sec("gpsimd", _)
                return

            with contextlib.ExitStack() as esy:
                psy = esy.enter_context(nc.psum_tensor(f"psy{ch}", [128, 1024], F32))
                for t in range(NT):
                    def _(te, t=t):
                        wt(te, "VE")
                        wt(te, "LD", vw_ld)
                        if t > 0:
                            wt(te, "AC")
                        for half in range(2):
                            for k in range(KD):
                                ins_ = te.matmul(psy[:, half * 512:(half + 1) * 512],
                                                 attb[:, (t * KD + k) * 128:(t * KD + k + 1) * 128],
                                                 wow[:, k * D + half * 512:k * D + half * 512 + 512],
                                                 start=(k == 0), stop=(k == KD - 1))
                        inc(ins_, "PE")
                    sec("tensor", _)

                    def _(v, t=t):
                        wt(v, "PE")
                        xt = xsb[:, t * D:(t + 1) * D]
                        inc(v.tensor_tensor(xt, psy[:, :], xt, AluOpType.add), "VE")
                    sec("vector", _)

                    def _(a, t=t):
                        wt(a, "VE")
                        inc(a.nop(), "AC")
                    sec("scalar", _)

    if _LIMIT == "attn":
        def _(sy):
            wt(sy, "VE")
            wt(sy, "OUT")
            inc(sy.dma_start(out=dbg_aps["dbg"].rearrange("(t p) j -> p t j", p=128),
                             in_=xsb[:, :].rearrange("p (t j) -> p t j", t=NT)), "OUT")
        sec("sync", _)
        return
    ffn_phase(f"f2c{ch}", "out")

    if ch + 1 < NCH:
        def _(sy):
            wt(sy, "VE")
            wt(sy, "LD")
            inc(sy.dma_start(
                out=xsb[:, :].rearrange("p (t j) -> p t j", t=NT),
                in_=x_d[(ch + 1) * CT:(ch + 2) * CT, :].rearrange("(t p) j -> p t j", p=128)), "LD")
        sec("sync", _)

        def _(v):
            wt(v, "LD")
            inc(v.nop(), "VE")
        sec("vector", _)



def _spc(v, scrub, src):
    v.tensor_copy(scrub[:, 2:3], src)

_NC_CACHE = None
_NC_KEY = 'x'


def kernel(L0_attn_x, L0_attn_W_qkv, L0_attn_W_o, L0_ffn_w_gate, L0_ffn_w_up,
           L0_ffn_w_down):
    global _NC_CACHE
    x = np.ascontiguousarray(np.asarray(L0_attn_x, dtype=np.float32)).reshape(B * T, D)
    wqkv = np.ascontiguousarray(np.asarray(L0_attn_W_qkv, dtype=np.float32))
    wo = np.ascontiguousarray(np.asarray(L0_attn_W_o, dtype=np.float32))
    wg = np.ascontiguousarray(np.asarray(L0_ffn_w_gate, dtype=np.float32))
    wu = np.ascontiguousarray(np.asarray(L0_ffn_w_up, dtype=np.float32))
    wd = np.ascontiguousarray(np.asarray(L0_ffn_w_down, dtype=np.float32))

    global _NC_KEY
    if _NC_CACHE is None or _NC_KEY != _LIMIT:
        _NC_CACHE = build_nc()
        _NC_KEY = _LIMIT
    nc = _NC_CACHE

    triu = np.tile(np.triu(np.ones((128, 128), dtype=np.float32)), (1, 4))
    wg_pad = np.zeros((3 * 128 * NCORE, D), dtype=np.float32)
    wg_pad[:II] = wg
    wu_pad = np.zeros((3 * 128 * NCORE, D), dtype=np.float32)
    wu_pad[:II] = wu
    in_maps = []
    spans = []
    for c in range(NCORE):
        seq, q = divmod(c, 4)
        starts = [seq * T + ch * (4 * CT) + q * CT for ch in range(NCH)]
        spans.append(starts)
        xo = np.concatenate([x[s:s + CT] for s in starts], axis=0)
        pos = np.concatenate(
            [np.arange(ch * 4 * CT + q * CT + 1, ch * 4 * CT + q * CT + CT + 1)
             for ch in range(NCH)]).astype(np.float32)
        rpos = np.broadcast_to((1.0 / pos)[None, :], (128, NCH * CT)).copy()
        mask = np.broadcast_to((np.arange(4) < q).astype(np.float32)[None, :], (128, 4)).copy()
        in_maps.append({
            "x": xo, "wqkv": wqkv, "wo": wo,
            "wg": wg_pad[c * 384:(c + 1) * 384],
            "wu": wu_pad[c * 384:(c + 1) * 384],
            "wd": wd[c * 128:(c + 1) * 128],
            "rposd": rpos, "triu4": triu, "maska": mask,
        })

    res = run_bass_kernel_spmd(nc, in_maps, core_ids=list(range(NCORE)))
    global _RES
    _RES = res
    out = np.empty((B * T, D), dtype=np.float32)
    for c in range(NCORE):
        yc = res.results[c]["y"]
        for ch in range(NCH):
            out[spans[c][ch]:spans[c][ch] + CT] = yc[ch * CT:(ch + 1) * CT]
    return out.reshape(B, T, D)
